# version 32
# baseline (speedup 1.0000x reference)
import numpy as np
import concourse.bass as bass
import concourse.mybir as mybir
from concourse.bass_utils import run_bass_kernel_spmd

F32 = mybir.dt.float32
BF16 = mybir.dt.bfloat16
AF = mybir.ActivationFunctionType
ALU = mybir.AluOpType
AX = mybir.AxisListType.X

S = 2048
E = 1024
H = 16
HD = 64
HALF = 1024
NT = 8
NCH = 8
EPS = 1e-5

CF_BQ = 0
CF_BK = 8
CF_MUR = 16
CF_MUK = 24
CF_EPS = 32
CF_ONE = 33
CF_W = 40

CB_I128 = 0
CB_MASK = 128
CB_I65 = 256
CB_ONES = 321
CB_BD = 452
CB_W = 580


def build_program(dbg=(), dbg_barrier=True):
    nc = bass.Bass("TRN2", target_bir_lowering=False)
    dbg = set(dbg)

    x_d = nc.dram_tensor("x", [S, E], F32, kind="ExternalInput").ap()
    out_d = nc.dram_tensor("out", [S, E], F32, kind="ExternalOutput").ap()
    w_d = {}
    for wn in ("wq", "wk", "wv", "wr", "wk2", "wv2"):
        w_d[wn] = nc.dram_tensor(wn, [E, E], F32, kind="ExternalInput").ap()
    cf_d = nc.dram_tensor("cf", [128, CF_W], F32, kind="ExternalInput").ap()
    cb_d = nc.dram_tensor("cb", [128, CB_W], F32, kind="ExternalInput").ap()
    bv_d = nc.dram_tensor("bvf", [1, E], F32, kind="ExternalInput").ap()

    dbg_d = {}

    def dbg_on(name, shape):
        if name in dbg:
            dbg_d[name] = nc.dram_tensor("dbg_" + name, shape, F32,
                                         kind="ExternalOutput").ap()
            return True
        return False

    import contextlib
    with contextlib.ExitStack() as _st:
        X = _st.enter_context(nc.sbuf_tensor("X", [128, NT, E], F32))
        KT = _st.enter_context(nc.sbuf_tensor("KT", [128, NCH, S], BF16))
        QT = _st.enter_context(nc.sbuf_tensor("QT", [128, NCH, HALF], BF16))
        VH = _st.enter_context(nc.sbuf_tensor("VH", [128, 16, H * 65], BF16))
        XN = _st.enter_context(nc.sbuf_tensor("XN", [128, NCH, 1040], BF16))
        AO = _st.enter_context(nc.sbuf_tensor("AO", [128, NT, E], BF16))
        W0 = _st.enter_context(nc.sbuf_tensor("W0", [128, NCH, E], BF16))
        W1 = _st.enter_context(nc.sbuf_tensor("W1", [128, NCH, E], BF16))
        W2 = _st.enter_context(nc.sbuf_tensor("W2", [128, NCH, E], BF16))
        PT = _st.enter_context(nc.sbuf_tensor("PT", [128, 3, 2, 512], BF16))
        OSB = _st.enter_context(nc.sbuf_tensor("OSB", [128, 2, 512], BF16))
        CF = _st.enter_context(nc.sbuf_tensor("CF", [128, CF_W], F32))
        CB = _st.enter_context(nc.sbuf_tensor("CB", [128, CB_W], BF16))
        STT = _st.enter_context(nc.sbuf_tensor("STT", [128, 128], F32))
        MH = _st.enter_context(nc.sbuf_tensor("MH", [128, 16], F32))
        MHP = _st.enter_context(nc.sbuf_tensor("MHP", [128, 8], F32))
        LNS = _st.enter_context(nc.sbuf_tensor("LNS", [128, 1024], BF16))
        BVB = _st.enter_context(nc.sbuf_tensor("BVB", [128, E], BF16))
        SMALL = _st.enter_context(nc.sbuf_tensor("SMALL", [128, 96], F32))
        NMB = _st.enter_context(nc.sbuf_tensor("NMB", [128, 16], BF16))
        CAR = _st.enter_context(nc.sbuf_tensor("CAR", [128, NCH, 1], BF16))
        RC = _st.enter_context(nc.sbuf_tensor("RC", [128, 8], F32))
        PSA = _st.enter_context(nc.psum_tensor("PSA", [128, 4, 512], F32))
        PSO = _st.enter_context(nc.psum_tensor("PSO", [128, 2, 512], F32))
        PST = _st.enter_context(nc.psum_tensor("PST", [128, 512], F32))
        PSS = _st.enter_context(nc.psum_tensor("PSS", [128, 512], F32))
        sLD = _st.enter_context(nc.semaphore("LD"))
        sWL = _st.enter_context(nc.semaphore("WL"))
        sPE = _st.enter_context(nc.semaphore("PE"))
        sACT = _st.enter_context(nc.semaphore("ACT"))
        sDVE = _st.enter_context(nc.semaphore("DVE"))
        sGP = _st.enter_context(nc.semaphore("GP"))
        sST = _st.enter_context(nc.semaphore("ST"))
        sXS = _st.enter_context(nc.semaphore("XS"))
        block = _st.enter_context(nc.Block())
        SEM = {"LD": sLD, "WL": sWL, "PE": sPE, "ACT": sACT, "DVE": sDVE,
               "GP": sGP, "ST": sST, "XS": sXS}
        cnt = {k: 0 for k in SEM}
        lastw = {}
        items = []
        cur_phase = ["init"]
        plan_labels = {}

        def em(eng, f):
            items.append((eng, f))

        def emlab(eng):
            plan_labels.setdefault(eng, []).append(cur_phase[0])

        def W(eng, sem, val):
            if val <= 0:
                return
            assert val <= cnt[sem], f"wait on future {sem} {val} > {cnt[sem]}"
            key = (eng, sem)
            if lastw.get(key, 0) >= val:
                return
            lastw[key] = val
            em(eng, lambda e, s=sem, v=val: e.wait_ge(SEM[s], v))

        def EMI(eng, f, inc=None, by=1):
            emlab(eng)
            if inc is not None:
                cnt[inc] += by
                em(eng, lambda e, ff=f, s=inc, b=by: ff(e).then_inc(SEM[s], b))
                return cnt[inc]
            em(eng, f)
            return None

        def dbg_bar():
            if not dbg_barrier:
                return
            for eng in ("vector", "scalar", "tensor", "sync"):
                W(eng, "ST", cnt["ST"])

        EMI("sync", lambda e: e.dma_start(out=CF[:], in_=cf_d), "LD", 16)
        ld_const = cnt["LD"]
        W("sync", "LD", cnt["LD"])
        EMI("sync", lambda e: e.dma_start(
            out=X[:, 0:4, :],
            in_=x_d[0:512, :].rearrange("(t p) f -> p t f", p=128)),
            "LD", 16)
        ld0a = cnt["LD"]
        W("sync", "LD", cnt["LD"])
        EMI("sync", lambda e: e.dma_start(
            out=X[:, 4:8, :],
            in_=x_d[512:HALF, :].rearrange("(t p) f -> p t f", p=128)),
            "LD", 16)
        ld_x = [[ld0a, cnt["LD"]], [None, None]]

        EMI("gpsimd", lambda e: e.dma_start(out=CB[:], in_=cb_d), "WL", 16)
        W("gpsimd", "WL", cnt["WL"])
        EMI("gpsimd", lambda e: e.dma_start(out=QT[0:1, 0, :], in_=bv_d),
            "WL", 16)
        wl_const = cnt["WL"]

        vh_ones = VH[:].rearrange("p g (h c) -> p g h c", c=65)
        EMI("gpsimd", lambda e: e.memset(vh_ones[:, :, :, 64:65], 1.0), "GP")
        EMI("gpsimd", lambda e: e.memset(CAR[:], 0.0), "GP")
        gp_init = cnt["GP"]

        wslot = {"wq": W0, "wk": W1, "wv": W2, "wr": W0, "wk2": W1,
                 "wv2": W2}
        wl_ready = {}

        def load_weight(wn, tag, wait_pe=0):
            if wait_pe:
                W("gpsimd", "PE", wait_pe)
            W("gpsimd", "WL", cnt["WL"])
            EMI("gpsimd", lambda e, w=wn: e.dma_start(
                out=wslot[w][:],
                in_=w_d[w].rearrange("(k p) m -> p k m", p=128)), "WL", 16)
            wl_ready[tag] = cnt["WL"]

        load_weight("wq", "wq0")
        load_weight("wk", "wk0")
        load_weight("wv", "wv0")

        W("tensor", "WL", wl_const)
        for n in range(2):
            pe = EMI("tensor", lambda e, n=n: e.matmul(
                PSS[:, 0:512], CB[0:1, CB_ONES:CB_ONES + 128],
                QT[0:1, 0, n * 512:(n + 1) * 512], start=True, stop=True),
                "PE")
            W("vector", "PE", pe)
            dv = EMI("vector", lambda e, n=n: e.tensor_copy(
                BVB[:, n * 512:(n + 1) * 512], PSS[:, 0:512]), "DVE")
            W("tensor", "DVE", dv)

        psa_evac = [(None, 0)] * 4
        psa_rr = [0]

        def psum_acc_group(n_k, lhsT_fn, rhs_fn, evac_fn, evac_eng,
                           multi=False, pre_k=None):
            b = psa_rr[0] % 4
            psa_rr[0] += 1
            gsem, gval = psa_evac[b]
            if gsem:
                W("tensor", gsem, gval)
            pe = None
            for k in range(n_k):
                if pre_k is not None:
                    pre_k(k)
                pe = EMI("tensor", lambda e, k=k, b=b: e.matmul(
                    PSA[:, b, :], lhsT_fn(k), rhs_fn(k),
                    start=(k == 0), stop=(k == n_k - 1)),
                    "PE" if k == n_k - 1 else None)
            W(evac_eng, "PE", pe)
            sem = "ACT" if evac_eng == "scalar" else "DVE"
            if multi:
                ev = evac_fn(PSA[:, b, :])
            else:
                ev = EMI(evac_eng, lambda e, b=b: evac_fn(e, PSA[:, b, :]),
                         sem)
            psa_evac[b] = (sem, ev)
            return ev

        tb_guard = {0: (None, 0), 1: (None, 0)}
        tb_banks = [PST, PSS]
        tb_rr = [0]

        def transpose_group(blocks, dest_fn, extra_pe_wait=()):
            bi = tb_rr[0] % 2
            tb_rr[0] += 1
            bank = tb_banks[bi]
            gsem, gval = tb_guard[bi]
            if gsem:
                W("tensor", gsem, gval)
            for sem, val in extra_pe_wait:
                W("tensor", sem, val)
            pe = None
            for i, blk in enumerate(blocks):
                pe = EMI("tensor", lambda e, i=i, blk=blk, bank=bank:
                         e.matmul(bank[:, i * 128:(i + 1) * 128], blk,
                                  CB[:, CB_I128:CB_I128 + 128],
                                  start=True, stop=True),
                         "PE" if i == len(blocks) - 1 else None)
            W("vector", "PE", pe)
            pv = bank[:].rearrange("p (i n) -> p i n", n=128)
            dv = EMI("vector", lambda e, pv=pv, m=len(blocks):
                     dest_fn(e, pv[:, 0:m, :]), "DVE")
            tb_guard[bi] = ("DVE", dv)
            return dv

        stat_ring_i = [0]

        def grab_stat_bank():
            k = stat_ring_i[0] % 6
            stat_ring_i[0] += 1
            if k < 4:
                gsem, gval = psa_evac[k]
                if gsem:
                    W("tensor", gsem, gval)
                bank = PSA[:, k, :]

                def done(sem, val, k=k):
                    psa_evac[k] = (sem, val)
            else:
                bi = k - 4
                gsem, gval = tb_guard[bi]
                if gsem:
                    W("tensor", gsem, gval)
                bank = tb_banks[bi]

                def done(sem, val, bi=bi):
                    tb_guard[bi] = (sem, val)
            return bank, done

        def grab_tb():
            bi = tb_rr[0] % 2
            tb_rr[0] += 1
            bank = tb_banks[bi]
            gsem, gval = tb_guard[bi]
            if gsem:
                W("tensor", gsem, gval)

            def done(sem, val, bi=bi):
                tb_guard[bi] = (sem, val)
            return bank, done

        osb_flat = OSB[:].rearrange("p a n -> p (a n)")
        sq_dummy = PT[:].rearrange("p a b n -> p (a b n)")[:, 0:E]
        ptflat = PT[:].rearrange("p a b n -> p (a b n)")

        def mcol(t):
            return STT[:, 96 + 2 * t:96 + 2 * t + 1]

        def rcol(t):
            return STT[:, 112 + t:112 + t + 1]

        def ln_stats(src_tiles, pre=(), base=0, pre_tile=None):
            nt = len(src_tiles)
            for sem, val in pre:
                W("vector", sem, val)
                W("scalar", sem, val)
            dst = None
            for i, ap in enumerate(src_tiles):
                if pre_tile is not None:
                    pre_tile(i)
                t = base + i
                for hlf in range(2):
                    dst = EMI("vector", lambda e, t=t, ap=ap, hlf=hlf:
                              e.bn_stats(
                                  STT[:, t * 12 + hlf * 6:t * 12 + hlf * 6
                                      + 6],
                                  ap[:, hlf * 512:(hlf + 1) * 512]),
                              "DVE" if (i == nt - 1 and hlf == 1) else None)
            W("vector", "DVE", dst)
            dag = None
            for i in range(nt):
                t = base + i
                dag = EMI("vector", lambda e, t=t: e.bn_aggr(
                    STT[:, 96 + 2 * t:96 + 2 * t + 2],
                    STT[:, t * 12:t * 12 + 12]),
                    "DVE" if i == nt - 1 else None)
            W("scalar", "DVE", dag)
            var_view = STT[:, 96 + 2 * base:96 + 2 * (base + nt)].rearrange(
                "p (t two) -> p t two", two=2)[:, :, 1:2]
            a1 = EMI("scalar", lambda e, vv=var_view, nt=nt, base=base:
                     e.activation(
                         STT[:, 120 + base:120 + base + nt], vv, AF.Ln,
                         bias=CF[:, CF_EPS:CF_EPS + 1]), "ACT")
            W("scalar", "ACT", a1)
            ac = EMI("scalar", lambda e, nt=nt, base=base: e.activation(
                STT[:, 112 + base:112 + base + nt],
                STT[:, 120 + base:120 + base + nt], AF.Exp,
                scale=-0.5), "ACT")
            return ac

        out_sts = []
        preA = {}
        preA_xn = {}
        for h in range(2):
            T0 = h * HALF
            W("vector", "LD", ld_x[h][0])
            W("scalar", "LD", ld_x[h][0])
            W("vector", "LD", ld_const)
            W("scalar", "LD", ld_const)

            if dbg_on("xload%d" % h, [128, NT, E]):
                W("gpsimd", "LD", ld_x[h][1])
                EMI("gpsimd", lambda e, h=h: e.dma_start(
                    out=dbg_d["xload%d" % h], in_=X[:]), "ST", 16)
                dbg_bar()
            cur_phase[0] = "A_ln1_h%d" % h
            rstd_b = {}
            xn_r = {}
            osb_free_pe = [0]

            def ln1_batch(batch):
                for t in range(batch * 4, batch * 4 + 4):
                    W("vector", "ACT", rstd_b[batch])
                    W("vector", "PE", osb_free_pe[0])
                    W("vector", "DVE", cnt["DVE"])
                    W("vector", "XS", cnt["XS"])
                    dvn = EMI("vector", lambda e, t=t: e.tensor_scalar(
                        osb_flat, X[:, t, :], mcol(t), rcol(t),
                        ALU.subtract, ALU.mult), "DVE")
                    for g in range(2):
                        c0 = g * 4
                        transpose_group(
                            [osb_flat[:, (c0 + c) * 128:(c0 + c + 1) * 128]
                             for c in range(4)],
                            lambda e, pv, t=t, c0=c0: e.tensor_copy(
                                XN[:, c0:c0 + 4, t * 128:(t + 1) * 128], pv),
                            extra_pe_wait=(("DVE", dvn), ("WL", wl_const)))
                    osb_free_pe[0] = cnt["PE"]
                xn_r[batch] = cnt["DVE"]

            if h in preA_xn:
                xn_r[0] = preA_xn[h]
            else:
                rstd_b[0] = ln_stats([X[:, t, :] for t in range(4)], base=0)
                ln1_batch(0)

            if dbg_on("xn%d" % h, [128, NCH, HALF]):
                W("gpsimd", "DVE", xn_ready)
                EMI("gpsimd", lambda e, h=h: e.dma_start(
                    out=dbg_d["xn%d" % h], in_=XN[:, :, 0:HALF]), "ST", 16)
                dbg_bar()

            sq_guard = [0]

            def plan_chunk_sq(src, evac_ac):
                W("gpsimd", "ACT", evac_ac)
                W("gpsimd", "PE", sq_guard[0])
                return EMI("gpsimd", lambda e, src=src: e.tensor_tensor(
                    ptflat[:, 0:HALF], src, src, ALU.mult), "GP")

            def plan_chunk_stats(sqv, out_col0, pi, kmax_combine):
                col = out_col0 + pi
                for n in range(2):
                    bank, done = grab_tb()
                    W("tensor", "GP", sqv)
                    pe = EMI("tensor", lambda e, n=n, bank=bank: e.matmul(
                        bank[:, 0:512],
                        CB[:, CB_BD:CB_BD + 128],
                        ptflat[:, n * 512:(n + 1) * 512],
                        start=True, stop=True), "PE")
                    sq_guard[0] = pe
                    W("vector", "PE", pe)
                    dvl = EMI("vector", lambda e, bank=bank, n=n:
                              e.reduce_max(
                                  SMALL[:, 48 + n:49 + n],
                                  bank[:, 0:512], axis=AX), "DVE")
                    done("DVE", dvl)
                W("vector", "DVE", dvl)
                if kmax_combine:
                    dvl = EMI("vector", lambda e: e.reduce_max(
                        SMALL[:, 52:53], SMALL[:, 48:50], axis=AX), "DVE")
                    W("vector", "DVE", dvl)
                    dvl = EMI("vector", lambda e, col=col: e.tensor_tensor(
                        SMALL[:, 8 + col:8 + col + 1],
                        SMALL[:, col:col + 1],
                        SMALL[:, 52:53], ALU.max), "DVE")
                    W("vector", "DVE", dvl)
                else:
                    dvl = EMI("vector", lambda e, col=col: e.reduce_max(
                        SMALL[:, col:col + 1],
                        SMALL[:, 48:50], axis=AX), "DVE")
                    W("vector", "DVE", dvl)

            q_evac = {}
            k_evac = {}
            stats_pending = []

            def qkv_n(n, wt, evac_fn_mk, tag_evacs):
                W("tensor", "DVE", xn_r[n])
                for m in range(NCH):
                    ev = psum_acc_group(
                        NCH,
                        lambda k, m=m, wt=wt: wt[:, k,
                                                 m * 128:(m + 1) * 128],
                        lambda k, n=n: XN[:, k, n * 512:(n + 1) * 512],
                        evac_fn_mk(m, n), "scalar")
                    if n == 1:
                        tag_evacs[m] = ev
                    if n == 1 and stats_pending:
                        stats_pending.pop(0)()

            def q_evac_mk(m, n):
                return lambda e, ps, m=m, n=n: e.activation(
                    QT[:, m, n * 512:(n + 1) * 512], ps, AF.Identity,
                    bias=CF[:, CF_BQ + m:CF_BQ + m + 1])

            def k_evac_mk(m, n):
                return lambda e, ps, m=m, n=n, T0=T0: e.activation(
                    KT[:, m, T0 + n * 512:T0 + (n + 1) * 512], ps,
                    AF.Identity, bias=CF[:, CF_BK + m:CF_BK + m + 1])

            def v_tile(t):
                g = h * NT + t
                dst = VH[:, g, :].rearrange("p (hh c) -> p hh c", c=65)
                W("tensor", "DVE", xn_r[t // 4])
                for n in range(2):
                    psum_acc_group(
                        NCH,
                        lambda k, t=t: XN[:, k, t * 128:(t + 1) * 128],
                        lambda k, n=n: W2[:, k, n * 512:(n + 1) * 512],
                        lambda e, ps, dst=dst, n=n: e.activation(
                            dst[:, n * 8:(n + 1) * 8, 0:64],
                            ps.rearrange("p (hh c) -> p hh c", c=64),
                            AF.Identity),
                        "scalar")
                if stats_pending:
                    stats_pending.pop(0)()

            cur_phase[0] = "A2_pass1_h%d" % h
            W("tensor", "WL", wl_ready["wq0" if h == 0 else "wq1"])
            qkv_n(0, W0, q_evac_mk, q_evac)
            W("tensor", "WL", wl_ready["wk0" if h == 0 else "wk1"])
            qkv_n(0, W1, k_evac_mk, k_evac)
            W("tensor", "WL", wl_ready["wv0" if h == 0 else "wv1"])
            W("tensor", "GP", gp_init)
            for t in range(4):
                v_tile(t)
            cur_phase[0] = "A_ln1b1_h%d" % h
            rstd_b[1] = ln_stats([X[:, t, :] for t in range(4, 8)],
                                 base=4, pre=(("LD", ld_x[h][1]),))
            ln1_batch(1)
            cur_phase[0] = "A2_pass2_h%d" % h
            qkv_n(1, W0, q_evac_mk, q_evac)
            q_done = cnt["PE"]
            load_weight("wr", "wr%d" % h, wait_pe=q_done)

            def two_stage(src_fn, col0, pi, comb):
                holder = {}

                def stage_sq(pi=pi):
                    holder["sqv"] = plan_chunk_sq(src_fn(pi),
                                                  holder["evac"])

                def stage_mm(pi=pi):
                    plan_chunk_stats(holder["sqv"], col0, pi, comb)
                return holder, stage_sq, stage_mm

            q_stage = []
            for pi in range(NCH):
                hld, s1f, s2f = two_stage(
                    lambda pi: QT[:, pi, :], 16, pi, False)
                hld["evac"] = q_evac[pi]
                q_stage.append((s1f, s2f))
            for s1f, s2f in q_stage:
                stats_pending.append(s1f)
                stats_pending.append(s2f)
            qkv_n(1, W1, k_evac_mk, k_evac)
            k_done = cnt["PE"]
            load_weight("wk2", "wk2%d" % h, wait_pe=k_done)
            k_stage = []
            for pi in range(NCH):
                hld, s1f, s2f = two_stage(
                    lambda pi, T0=T0: KT[:, pi, T0:T0 + HALF], 0, pi,
                    h == 1)
                hld["evac"] = k_evac[pi]
                k_stage.append((s1f, s2f))
            for s1f, s2f in k_stage:
                stats_pending.append(s1f)
                stats_pending.append(s2f)
            for t in range(4, NT):
                v_tile(t)
            while stats_pending:
                stats_pending.pop(0)()
            v_done = cnt["PE"]
            load_weight("wv2", "wv2%d" % h, wait_pe=v_done)

            cur_phase[0] = "NM_h%d" % h
            kb = 0 if h == 0 else 8
            dvq = EMI("vector", lambda e, kb=kb: e.tensor_tensor(
                SMALL[:, 32:40], SMALL[:, 16:24],
                SMALL[:, kb:kb + 8], ALU.mult), "DVE")
            W("scalar", "DVE", dvq)
            a1 = EMI("scalar", lambda e: e.activation(
                MH[:, 0:8], SMALL[:, 32:40], AF.Ln,
                bias=CF[:, CF_EPS:CF_EPS + 1]), "ACT")
            W("scalar", "ACT", a1)
            ac = EMI("scalar", lambda e: e.activation(
                MH[:, 8:16], MH[:, 0:8], AF.Exp, scale=0.5), "ACT")
            W("vector", "ACT", ac)
            dv = EMI("vector", lambda e: e.tensor_scalar(
                NMB[:, 0:8], MH[:, 8:16], -0.125, None, ALU.mult),
                "DVE")
            bank, done = grab_tb()
            W("tensor", "DVE", dv)
            EMI("tensor", lambda e, bank=bank: e.matmul(
                bank[:, 0:8], CB[0:1, CB_ONES:CB_ONES + 128],
                NMB[0:1, 0:8], start=True, stop=True), "PE")
            pe = EMI("tensor", lambda e, bank=bank: e.matmul(
                bank[:, 8:16], CB[64:65, CB_ONES:CB_ONES + 128],
                NMB[64:65, 0:8], start=True, stop=True,
                tile_position=(64, 0)), "PE")
            W("vector", "PE", pe)
            dcp = EMI("vector", lambda e, bank=bank: e.tensor_copy(
                MH[:, 0:8], bank[:, 0:8]), "DVE")
            W("vector", "DVE", dcp)
            mh_ready = EMI("vector", lambda e, bank=bank: e.tensor_tensor(
                MHP[:, 0:8], MH[:, 0:8], bank[:, 8:16], ALU.min), "DVE")
            done("DVE", mh_ready)

            if dbg_on("mh%d" % h, [128, 16]):
                W("gpsimd", "DVE", mh_ready)
                EMI("gpsimd", lambda e, h=h: e.dma_start(
                    out=dbg_d["mh%d" % h], in_=MH[:]), "ST", 16)
                dbg_bar()

            W("tensor", "ACT", cnt["ACT"])
            W("tensor", "DVE", cnt["DVE"])
            exp_of = {}
            pv_of = {}
            pt_guard = {}
            last_o_ac = [0]
            lns_flat = LNS[:]

            def jmax_of(sg):
                return (T0 + sg * 512 + 511) // 128

            def jparams(sg, j):
                seg_lo = T0 + sg * 512
                seg_hi = seg_lo + 512
                qstart = seg_lo if j * 128 < seg_lo else j * 128
                N = min(seg_hi - qstart, 512)
                diag = j * 128 >= seg_lo
                return qstart - T0, qstart - seg_lo, N, diag

            def emit_scores(sg, pi, j):
                cur_phase[0] = "attn_h%d_s%d_p%d" % (h, sg, pi)
                par = j % 2
                qoff, soff, N, diag = jparams(sg, j)
                if (par, 1) in exp_of:
                    W("tensor", "ACT", exp_of[(par, 1)])
                sc = None
                for head in range(2):
                    b = 2 * par + head
                    base = head * 64
                    sc = EMI("tensor", lambda e, b=b, base=base,
                             pi=pi, j=j, qoff=qoff, N=N: e.matmul(
                                 PSA[:, b, 0:N],
                                 KT[base:base + 64, pi,
                                    j * 128:(j + 1) * 128],
                                 QT[base:base + 64, pi,
                                    qoff:qoff + N],
                                 start=True, stop=True,
                                 tile_position=(base, 0)),
                             "PE" if head == 1 else None)
                return sc

            def emit_exp(sg, pi, j, sc_pe):
                par = j % 2
                slot = 2 if j == 0 else par
                qoff, soff, N, diag = jparams(sg, j)
                if slot in pt_guard:
                    W("scalar", "PE", pt_guard[slot])
                W("scalar", "PE", sc_pe)
                W("scalar", "DVE", mh_ready)
                EMI("scalar", lambda e, slot=slot, pi=pi, par=par, N=N:
                    e.activation(
                        PT[:, slot, :, 0:N],
                        PSA[:, 2 * par:2 * par + 2, 0:N],
                        AF.Exp, bias=MHP[:, pi:pi + 1],
                        scale=0.125), "ACT")
                exp_of[(par, 1)] = cnt["ACT"]
                ev = cnt["ACT"]
                pv_wait = [("ACT", ev)]
                if diag:
                    W("vector", "ACT", ev)
                    dvm = None
                    for head in range(2):
                        dvm = EMI("vector", lambda e, slot=slot,
                                  head=head: e.tensor_tensor(
                                      PT[:, slot, head, 0:128],
                                      PT[:, slot, head, 0:128],
                                      CB[:, CB_MASK:CB_MASK + 128],
                                      ALU.mult),
                                  "DVE" if head == 1 else None)
                    pv_wait.append(("DVE", dvm))
                return pv_wait

            def emit_pv(sg, pi, j, pv_wait):
                par = j % 2
                slot = 2 if j == 0 else par
                qoff, soff, N, diag = jparams(sg, j)
                jm = jmax_of(sg)
                if j == 0:
                    W("tensor", "DVE", last_o_ac[0])
                for sem, val in pv_wait:
                    W("tensor", sem, val)
                for head in range(2):
                    hh = pi * 2 + head
                    EMI("tensor", lambda e, head=head, hh=hh, j=j,
                        slot=slot, soff=soff, N=N, jm=jm:
                        e.matmul(
                            PSO[0:65, head, soff:soff + N],
                            VH[:, j, hh * 65:hh * 65 + 65],
                            PT[:, slot, head, 0:N],
                            start=(j == 0), stop=(j == jm)),
                        "PE" if head == 1 else None)
                pv_of[par] = cnt["PE"]
                pt_guard[slot] = cnt["PE"]

            def o_evac(sg, pi):
                cur_phase[0] = "ofin_h%d_s%d_p%d" % (h, sg, pi)
                W("vector", "PE", pv_of[0])
                W("vector", "PE", pv_of[1])
                for head in range(2):
                    EMI("vector", lambda e, head=head: e.tensor_copy(
                        OSB[0:65, head, :], PSO[0:65, head, :]),
                        "DVE" if head == 1 else None)
                o_dv = EMI("vector", lambda e: e.tensor_copy(
                    RC[:, 4:5], RC[:, 0:1]), "DVE")
                last_o_ac[0] = o_dv

            def o_transp(sg, pi):
                cur_phase[0] = "otr_h%d_s%d_p%d" % (h, sg, pi)
                o_ac = last_o_ac[0]
                for head in range(2):
                    hh = pi * 2 + head
                    bank, done = grab_tb()
                    W("tensor", "DVE", o_ac)
                    pe = None
                    for qt in range(4):
                        pe = EMI("tensor", lambda e, head=head, qt=qt,
                                 bank=bank: e.matmul(
                                     bank[:, qt * 128:qt * 128 + 65],
                                     OSB[0:65, head,
                                         qt * 128:(qt + 1) * 128],
                                     CB[0:65, CB_I65:CB_I65 + 65],
                                     start=True, stop=True),
                                 "PE" if qt == 3 else None)
                    W("vector", "PE", pe)
                    W("vector", "DVE", cnt["DVE"])
                    bv = bank[:].rearrange("p (i n) -> p i n", n=128)
                    dvr = EMI("vector", lambda e, bv=bv: e.reciprocal(
                        RC[:, 0:4], bv[:, 0:4, 64:65]), "DVE")
                    W("vector", "DVE", dvr)
                    dvo = None
                    for qt in range(4):
                        tt = sg * 4 + qt
                        dvo = EMI("vector", lambda e, bv=bv, qt=qt,
                                  tt=tt, hh=hh: e.scalar_tensor_tensor(
                                      AO[:, tt, hh * 64:hh * 64 + 64],
                                      bv[:, qt, 0:64],
                                      RC[:, qt:qt + 1],
                                      BVB[:, hh * 64:hh * 64 + 64],
                                      ALU.mult, ALU.add),
                                  "DVE" if qt == 3 else None)
                    done("DVE", dvo)

            def plan_C(qq, pre_dve):
                cur_phase[0] = "C_h%d_q%d" % (h, qq)
                tiles = [qq * 4 + i for i in range(4)]
                rstd_a = ln_stats([AO[:, t, :] for t in tiles],
                                  pre=(("DVE", pre_dve),))
                yield
                use_gp = (qq == 0)
                gp_adds = []
                dvx = None
                for i, t in enumerate(tiles):
                    W("vector", "ACT", rstd_a)
                    W("vector", "DVE", cnt["DVE"])
                    if gp_adds:
                        W("vector", "GP", gp_adds[-1])
                    dts = EMI("vector", lambda e, t=t, i=i: e.tensor_scalar(
                        lns_flat, AO[:, t, :], mcol(i), rcol(i),
                        ALU.subtract, ALU.mult), "DVE")
                    if use_gp:
                        W("gpsimd", "DVE", dts)
                        gp_adds.append(EMI("gpsimd", lambda e, t=t:
                                           e.tensor_tensor(
                                               X[:, t, :], X[:, t, :],
                                               lns_flat, ALU.add), "GP"))
                    else:
                        W("vector", "DVE", dts)
                        dvx = EMI("vector", lambda e, t=t: e.tensor_tensor(
                            X[:, t, :], X[:, t, :], lns_flat, ALU.add),
                            "DVE")
                    if i == 1:
                        yield
                yield
                if use_gp:
                    rstd_2 = ln_stats([X[:, t, :] for t in tiles],
                                      pre_tile=lambda i: W("vector", "GP",
                                                           gp_adds[i]))
                else:
                    rstd_2 = ln_stats([X[:, t, :] for t in tiles],
                                      pre=(("DVE", dvx),))
                if qq == 0:
                    W("vector", "GP", gp_init)
                    EMI("vector", lambda e: e.tensor_copy(
                        XN[:, :, 0:1], CAR[:]))
                else:
                    W("vector", "DVE", cnt["DVE"])
                    EMI("vector", lambda e: e.tensor_copy(
                        XN[:, :, 520:521], XN[:, :, 512:513]))
                yield
                osb_free = 0
                for i, t in enumerate(tiles):
                    W("vector", "ACT", rstd_2)
                    W("vector", "PE", osb_free)
                    dvn = EMI("vector", lambda e, t=t, i=i: e.tensor_scalar(
                        lns_flat, X[:, t, :], mcol(i), rcol(i),
                        ALU.subtract, ALU.mult), "DVE")
                    for g2 in range(2):
                        c0 = g2 * 4
                        transpose_group(
                            [lns_flat[:, (c0 + c) * 128:(c0 + c + 1) * 128]
                             for c in range(4)],
                            lambda e, pv, i=i, c0=c0, qq=qq: e.tensor_copy(
                                XN[:, c0:c0 + 4,
                                   qq * 520 + 1 + i * 128:
                                   qq * 520 + 1 + (i + 1) * 128], pv),
                            extra_pe_wait=(("DVE", dvn),))
                    osb_free = cnt["PE"]
                    yield
                if qq == 1:
                    W("vector", "DVE", cnt["DVE"])
                    EMI("vector", lambda e: e.tensor_copy(
                        CAR[:], XN[:, :, 1032:1033]))

            aof = AO[:].rearrange("p t f -> p (t f)")
            qtf = QT[:].rearrange("p c n -> p (c n)")
            Avs = [qtf[:, c * 512:(c + 1) * 512] for c in range(NCH)]
            Bvs = [qtf[:, 4096 + c * 512:4096 + (c + 1) * 512]
                   for c in range(NCH)]
            Cvs = [aof[:, c * 512:(c + 1) * 512] for c in range(NCH)]
            XKs = [aof[:, 4096 + c * 512:4096 + (c + 1) * 512]
                   for c in range(NCH)]
            OTs = [aof[:, 4096 + tl * 1024:4096 + (tl + 1) * 1024]
                   for tl in range(4)]

            def plan_mix_d(qq, pe_wait, dve_wait, on_gp):
                cur_phase[0] = "mixd_h%d_q%d" % (h, qq)
                x2 = [XN[:, c, qq * 520:qq * 520 + 513] for c in range(NCH)]
                eng, sem = ("gpsimd", "GP") if on_gp else ("vector", "DVE")
                if pe_wait:
                    W(eng, "PE", pe_wait)
                if on_gp and dve_wait:
                    W("gpsimd", "DVE", dve_wait)
                vals = []
                for c in range(NCH):
                    vals.append(EMI(eng, lambda e, c=c, xc=x2[c]:
                                    e.tensor_tensor(
                                        Cvs[c], xc[:, 1:513], xc[:, 0:512],
                                        ALU.subtract), sem))
                return vals

            def plan_mix_xr(qq, pe_wait, d_vals, d_on_gp):
                cur_phase[0] = "mixr_h%d_q%d" % (h, qq)
                x2 = [XN[:, c, qq * 520:qq * 520 + 513] for c in range(NCH)]
                if pe_wait:
                    W("vector", "PE", pe_wait)
                xr_vals = []
                for c in range(NCH):
                    if d_on_gp:
                        W("vector", "GP", d_vals[c])
                    xr_vals.append(EMI("vector", lambda e, c=c, xc=x2[c]:
                                       e.scalar_tensor_tensor(
                                           Avs[c], Cvs[c],
                                           CF[:, CF_MUR + c:CF_MUR + c + 1],
                                           xc[:, 0:512],
                                           ALU.mult, ALU.add), "DVE"))
                return xr_vals

            xr_gp = {}

            def mix0():
                cur_phase[0] = "mixd_h%d_q0" % h
                x2 = [XN[:, c, 0:513] for c in range(NCH)]
                W("vector", "PE", cnt["PE"])
                vals = []
                for c in range(NCH):
                    EMI("vector", lambda e, c=c, xc=x2[c]:
                        e.tensor_tensor(
                            Cvs[c], xc[:, 1:513], xc[:, 0:512],
                            ALU.subtract), None)
                    vals.append(EMI("vector", lambda e, c=c, xc=x2[c]:
                                    e.scalar_tensor_tensor(
                                        Avs[c], Cvs[c],
                                        CF[:, CF_MUR + c:CF_MUR + c + 1],
                                        xc[:, 0:512],
                                        ALU.mult, ALU.add), "DVE"))
                xr_gp[0] = vals

            mix_hook = [mix0]
            cgen = None
            units = [(sg, pi, j) for sg in range(2) for pi in range(NCH)
                     for j in range(jmax_of(sg) + 1)]
            nu = len(units)
            sc_of = {}
            pvw = {}

            def ensure_sc(ui):
                if ui < nu and ui not in sc_of:
                    sc_of[ui] = emit_scores(*units[ui])

            def ensure_exp(ui):
                if ui < nu and ui not in pvw:
                    ensure_sc(ui)
                    pvw[ui] = emit_exp(*units[ui], sc_of[ui])

            ensure_sc(0)
            ensure_sc(1)
            ensure_exp(0)
            pending_transp = [None]
            for ui in range(nu):
                sg, pi, j = units[ui]
                ensure_exp(ui + 1)
                ensure_sc(ui + 2)
                emit_pv(sg, pi, j, pvw[ui])
                if pending_transp[0] is not None:
                    pending_transp[0]()
                    pending_transp[0] = None
                if j == jmax_of(sg):
                    o_evac(sg, pi)
                    pending_transp[0] = (
                        lambda sg=sg, pi=pi: o_transp(sg, pi))
                    if sg == 1 and pi == 0:
                        cgen = plan_C(0, cnt["DVE"])
                    if sg == 1 and pi == NCH - 1:
                        if cgen is not None:
                            for _ in cgen:
                                pass
                            cgen = None
                        if mix_hook[0] is not None:
                            mix_hook[0]()
                            mix_hook[0] = None
                    ensure_exp(ui + 1)
                    ensure_exp(ui + 2)
                elif j == 1 and cgen is not None:
                    next(cgen, None)
            if pending_transp[0] is not None:
                pending_transp[0]()
                pending_transp[0] = None
            if cgen is not None:
                for _ in cgen:
                    pass
            attn_pe_done = cnt["PE"]
            seg1_done = cnt["DVE"]

            cgen1 = plan_C(1, seg1_done)

            if dbg_on("ao%d" % h, [128, NT, E]):
                W("gpsimd", "DVE", cnt["DVE"])
                EMI("gpsimd", lambda e, h=h: e.dma_start(
                    out=dbg_d["ao%d" % h], in_=AO[:]), "ST", 16)
                dbg_bar()

            ld_x1_tile = {}
            for qq in range(2):
                x2 = [XN[:, c, qq * 520:qq * 520 + 513] for c in range(NCH)]
                cur_phase[0] = "ffn_r_h%d_q%d" % (h, qq)
                W("tensor", "WL", wl_ready["wr%d" % h])
                for m in range(NCH):
                    def evac_r(ps, m=m):
                        s1 = PT[:, 0, m % 2, :]
                        s2 = PT[:, 1, m % 2, :]
                        a1 = EMI("scalar", lambda e, ps=ps, s1=s1:
                                 e.activation(s1, ps, AF.Exp, scale=-1.0),
                                 "ACT")
                        W("scalar", "ACT", a1)
                        a2 = EMI("scalar", lambda e, s1=s1, s2=s2:
                                 e.activation(s2, s1, AF.Ln,
                                              bias=CF[:, CF_ONE:CF_ONE + 1]),
                                 "ACT")
                        W("scalar", "ACT", a2)
                        return EMI("scalar", lambda e, s2=s2, m=m:
                                   e.activation(Bvs[m], s2, AF.Exp,
                                                scale=-1.0), "ACT")
                    r_ac = psum_acc_group(
                        NCH,
                        lambda k, m=m: W0[:, k, m * 128:(m + 1) * 128],
                        lambda k: Avs[k],
                        evac_r, "scalar", multi=True,
                        pre_k=lambda k, qq=qq: W("tensor", "DVE",
                                                 xr_gp[qq][k]))
                    if cgen1 is not None:
                        next(cgen1, None)
                r_pe = cnt["PE"]
                if cgen1 is not None:
                    for _ in cgen1:
                        pass
                    cgen1 = None
                    xn_q1_dve = cnt["DVE"]
                xk_dve = None
                for c in range(NCH):
                    xk_dve = EMI("vector", lambda e, c=c, xc=x2[c]:
                                 e.scalar_tensor_tensor(
                                     XKs[c], Cvs[c],
                                     CF[:, CF_MUK + c:CF_MUK + c + 1],
                                     xc[:, 0:512], ALU.mult, ALU.add),
                                 "DVE" if c == NCH - 1 else None)
                cur_phase[0] = "ffn_k_h%d_q%d" % (h, qq)
                W("tensor", "WL", wl_ready["wk2%d" % h])
                W("tensor", "DVE", xk_dve)
                preA_tile = [None]
                if h == 0 and qq == 1:
                    cur_phase[0] = "preA_h0"
                    preA[1] = ln_stats([X[:, t, :] for t in range(4)],
                                       base=0,
                                       pre_tile=lambda i: W(
                                           "vector", "LD", ld_x1_tile[i]))
                    ofp = [0]

                    def preA_tile_fn(t):
                        cur_phase[0] = "preA_h0"
                        W("vector", "ACT", preA[1])
                        W("vector", "PE", ofp[0])
                        dvn = EMI("vector", lambda e, t=t: e.tensor_scalar(
                            lns_flat, X[:, t, :], mcol(t), rcol(t),
                            ALU.subtract, ALU.mult), "DVE")
                        for g in range(2):
                            c0 = g * 4
                            transpose_group(
                                [lns_flat[:, (c0 + c) * 128:
                                          (c0 + c + 1) * 128]
                                 for c in range(4)],
                                lambda e, pv, t=t, c0=c0: e.tensor_copy(
                                    XN[:, c0:c0 + 4,
                                       t * 128:(t + 1) * 128], pv),
                                extra_pe_wait=(("DVE", dvn),))
                        ofp[0] = cnt["PE"]
                        if t == 3:
                            preA_xn[1] = cnt["DVE"]
                    preA_tile[0] = preA_tile_fn
                for m in range(NCH):
                    def evac_k(ps, m=m):
                        s1 = PT[:, 0, m % 2, :]
                        a1 = EMI("scalar", lambda e, ps=ps, s1=s1:
                                 e.activation(s1, ps, AF.Relu), "ACT")
                        W("scalar", "ACT", a1)
                        return EMI("scalar", lambda e, s1=s1, m=m:
                                   e.activation(Cvs[m], s1, AF.Square),
                                   "ACT")
                    psum_acc_group(
                        NCH,
                        lambda k, m=m: W1[:, k, m * 128:(m + 1) * 128],
                        lambda k: XKs[k],
                        evac_k, "scalar", multi=True)
                    if m % 2 == 1 and preA_tile[0] is not None:
                        preA_tile[0](m // 2)
                        cur_phase[0] = "ffn_k_h%d_q%d" % (h, qq)
                k2_pe = cnt["PE"]
                cur_phase[0] = "ffn_v_h%d_q%d" % (h, qq)
                W("tensor", "WL", wl_ready["wv2%d" % h])
                W("tensor", "ACT", cnt["ACT"])
                W("vector", "ACT", r_ac)
                for m in range(NCH):
                    psum_acc_group(
                        NCH,
                        lambda k, m=m: W2[:, k, m * 128:(m + 1) * 128],
                        lambda k: Cvs[k],
                        lambda e, ps, m=m: e.tensor_tensor(
                            Avs[m], ps, Bvs[m], ALU.mult),
                        "vector")
                v2_pe = cnt["PE"]
                outT_dve = cnt["DVE"]
                for tl in range(4):
                    for g2 in range(2):
                        c0 = g2 * 4
                        transpose_group(
                            [Avs[c0 + c][:, tl * 128:(tl + 1) * 128]
                             for c in range(4)],
                            lambda e, pv, tl=tl, c0=c0: e.tensor_copy(
                                OTs[tl].rearrange("p (c n) -> p c n",
                                                  n=128)[:, c0:c0 + 4, :],
                                pv),
                            extra_pe_wait=(("DVE", outT_dve),))
                outtok_dve = cnt["DVE"]
                if qq == 0:
                    d1_vals = plan_mix_d(1, cnt["PE"], xn_q1_dve, True)
                cur_phase[0] = "ln3_h%d_q%d" % (h, qq)
                tiles = [qq * 4 + i for i in range(4)]
                rstd_ac = ln_stats(OTs, pre=(("DVE", outtok_dve),), base=4)
                dvf = None
                for i, t in enumerate(tiles):
                    W("vector", "ACT", rstd_ac)
                    W("vector", "DVE", cnt["DVE"])
                    W("vector", "XS", cnt["XS"])
                    dts = EMI("vector", lambda e, i=i: e.tensor_scalar(
                        osb_flat, OTs[i], mcol(4 + i), rcol(4 + i),
                        ALU.subtract, ALU.mult), "DVE")
                    W("vector", "DVE", dts)
                    dvf = EMI("vector", lambda e, t=t: e.tensor_tensor(
                        X[:, t, :], X[:, t, :], osb_flat, ALU.add), "XS")
                    W("sync", "XS", dvf)
                    EMI("sync", lambda e, h=h, T0=T0, t=t: e.dma_start(
                        out=out_d[T0 + t * 128:T0 + (t + 1) * 128,
                                  :].rearrange("(a p) f -> p a f", p=128),
                        in_=X[:, t, :]), "ST", 16)
                    st_q = cnt["ST"]
                    if h == 0:
                        ldq = "gpsimd" if qq == 0 else "sync"
                        W(ldq, "ST", st_q)
                        W(ldq, "LD", cnt["LD"])
                        EMI(ldq, lambda e, t=t: e.dma_start(
                            out=X[:, t, :],
                            in_=x_d[HALF + t * 128:HALF + (t + 1) * 128,
                                    :].rearrange("(a p) f -> p a f", p=128)),
                            "LD", 16)
                        ld_x1_tile[t] = cnt["LD"]
                    if i == 0 and qq == 0:
                        xr_gp[1] = plan_mix_xr(1, cnt["PE"], d1_vals, True)
                        cur_phase[0] = "ln3_h%d_q%d" % (h, qq)
                if h == 0:
                    ld_x[1][qq] = cnt["LD"]

            half_done_dve = cnt["DVE"]
            if dbg_on("xf%d" % h, [128, NT, E]):
                W("gpsimd", "DVE", half_done_dve)
                EMI("gpsimd", lambda e, h=h: e.dma_start(
                    out=dbg_d["xf%d" % h], in_=X[:]), "ST", 16)
                dbg_bar()
            if h == 0:
                load_weight("wq", "wq1", wait_pe=r_pe)
                load_weight("wk", "wk1", wait_pe=k2_pe)
                load_weight("wv", "wv1", wait_pe=v2_pe)

        W("sync", "ST", cnt["ST"])

        def runner(name):
            def go(eng):
                for enm, f in items:
                    if enm == name:
                        f(eng)
            return go

        block.sync(runner("sync"))
        block.gpsimd(runner("gpsimd"))
        block.tensor(runner("tensor"))
        block.vector(runner("vector"))
        block.scalar(runner("scalar"))

    build_program.labels = plan_labels
    return nc, sorted(dbg_d.keys())


def _pack_inputs(ins):
    wqT = np.ascontiguousarray(
        np.asarray(ins["L0_attn_Wq"]).transpose(2, 0, 1).reshape(E, E))
    wkT = np.ascontiguousarray(
        np.asarray(ins["L0_attn_Wk"]).transpose(2, 0, 1).reshape(E, E))
    wvT = np.ascontiguousarray(
        np.asarray(ins["L0_attn_Wv"]).transpose(2, 0, 1).reshape(E, E))
    wrT = np.ascontiguousarray(np.asarray(ins["L0_ffn_W_r"]).T)
    wk2T = np.ascontiguousarray(np.asarray(ins["L0_ffn_W_k"]).T)
    wv2T = np.ascontiguousarray(np.asarray(ins["L0_ffn_W_v"]).T)

    cf = np.zeros((128, CF_W), dtype=np.float32)
    cf[:, CF_BQ:CF_BQ + 8] = np.asarray(
        ins["L0_attn_bq"], dtype=np.float32).reshape(8, 128).T
    cf[:, CF_BK:CF_BK + 8] = np.asarray(
        ins["L0_attn_bk"], dtype=np.float32).reshape(8, 128).T
    cf[:, CF_MUR:CF_MUR + 8] = np.asarray(
        ins["L0_ffn_mu_r"], dtype=np.float32).reshape(8, 128).T
    cf[:, CF_MUK:CF_MUK + 8] = np.asarray(
        ins["L0_ffn_mu_k"], dtype=np.float32).reshape(8, 128).T
    cf[:, CF_EPS] = EPS
    cf[:, CF_ONE] = 1.0

    cb = np.zeros((128, CB_W), dtype=np.float32)
    cb[:, CB_I128:CB_I128 + 128] = np.eye(128)
    p = np.arange(128)
    cb[:, CB_MASK:CB_MASK + 128] = (p[:, None] <= p[None, :])
    cb[0:65, CB_I65:CB_I65 + 65] = np.eye(65)
    cb[:, CB_ONES:CB_ONES + 128] = 1.0
    cb[0:64, CB_BD:CB_BD + 64] = 1.0
    cb[64:128, CB_BD + 64:CB_BD + 128] = 1.0

    bvf = np.ascontiguousarray(
        np.asarray(ins["L0_attn_bv"], dtype=np.float32).reshape(1, E))

    common = {"wq": wqT, "wk": wkT, "wv": wvT, "wr": wrT, "wk2": wk2T,
              "wv2": wv2T, "cf": cf, "cb": cb, "bvf": bvf}
    x = np.asarray(ins["L0_attn_x"])
    in_maps = []
    for b in range(8):
        m = dict(common)
        m["x"] = np.ascontiguousarray(x[b])
        in_maps.append(m)
    return in_maps


def kernel(**inputs):
    nc, _ = build_program()
    in_maps = _pack_inputs(inputs)
    res = run_bass_kernel_spmd(nc, in_maps, core_ids=list(range(8)))
    out = np.stack([np.asarray(res.results[b]["out"]) for b in range(8)],
                   axis=0)
    return out


# version 34
# speedup vs baseline: 1.2004x; 1.2004x over previous
import numpy as np
import concourse.bass as bass
import concourse.mybir as mybir
from concourse.bass_utils import run_bass_kernel_spmd

F32 = mybir.dt.float32
BF16 = mybir.dt.bfloat16
AF = mybir.ActivationFunctionType
ALU = mybir.AluOpType
AX = mybir.AxisListType.X

S = 2048
E = 1024
H = 16
HD = 64
HALF = 1024
NT = 8
NCH = 8
EPS = 1e-5

CF_BQ = 0
CF_BK = 8
CF_MUR = 16
CF_MUK = 24
CF_EPS = 32
CF_ONE = 33
CF_W = 40

CB_I128 = 0
CB_MASK = 128
CB_I65 = 256
CB_ONES = 321
CB_BD = 452
CB_W = 580


def build_program(dbg=(), dbg_barrier=True):
    nc = bass.Bass("TRN2", target_bir_lowering=False)
    dbg = set(dbg)

    x_d = nc.dram_tensor("x", [S, E], F32, kind="ExternalInput").ap()
    out_d = nc.dram_tensor("out", [S, E], F32, kind="ExternalOutput").ap()
    w_d = {}
    for wn in ("wq", "wk", "wv", "wr", "wk2", "wv2"):
        w_d[wn] = nc.dram_tensor(wn, [E, E], F32, kind="ExternalInput").ap()
    cf_d = nc.dram_tensor("cf", [128, CF_W], F32, kind="ExternalInput").ap()
    cb_d = nc.dram_tensor("cb", [128, CB_W], F32, kind="ExternalInput").ap()
    bv_d = nc.dram_tensor("bvf", [1, E], F32, kind="ExternalInput").ap()

    dbg_d = {}

    def dbg_on(name, shape):
        if name in dbg:
            dbg_d[name] = nc.dram_tensor("dbg_" + name, shape, F32,
                                         kind="ExternalOutput").ap()
            return True
        return False

    import contextlib
    with contextlib.ExitStack() as _st:
        X = _st.enter_context(nc.sbuf_tensor("X", [128, NT, E], F32))
        KT = _st.enter_context(nc.sbuf_tensor("KT", [128, NCH, S], BF16))
        QT = _st.enter_context(nc.sbuf_tensor("QT", [128, NCH, HALF], BF16))
        VH = _st.enter_context(nc.sbuf_tensor("VH", [128, 16, H * 65], BF16))
        XN = _st.enter_context(nc.sbuf_tensor("XN", [128, NCH, 1040], BF16))
        AO = _st.enter_context(nc.sbuf_tensor("AO", [128, NT, E], BF16))
        W0 = _st.enter_context(nc.sbuf_tensor("W0", [128, NCH, E], BF16))
        W1 = _st.enter_context(nc.sbuf_tensor("W1", [128, NCH, E], BF16))
        W2 = _st.enter_context(nc.sbuf_tensor("W2", [128, NCH, E], BF16))
        PT = _st.enter_context(nc.sbuf_tensor("PT", [128, 3, 2, 512], BF16))
        OSB = _st.enter_context(nc.sbuf_tensor("OSB", [128, 2, 512], BF16))
        CF = _st.enter_context(nc.sbuf_tensor("CF", [128, CF_W], F32))
        CB = _st.enter_context(nc.sbuf_tensor("CB", [128, CB_W], BF16))
        STT = _st.enter_context(nc.sbuf_tensor("STT", [128, 128], F32))
        MH = _st.enter_context(nc.sbuf_tensor("MH", [128, 16], F32))
        MHP = _st.enter_context(nc.sbuf_tensor("MHP", [128, 8], F32))
        LNS = _st.enter_context(nc.sbuf_tensor("LNS", [128, 1024], BF16))
        BVB = _st.enter_context(nc.sbuf_tensor("BVB", [128, E], BF16))
        SMALL = _st.enter_context(nc.sbuf_tensor("SMALL", [128, 96], F32))
        NMB = _st.enter_context(nc.sbuf_tensor("NMB", [128, 16], BF16))
        CAR = _st.enter_context(nc.sbuf_tensor("CAR", [128, NCH, 1], BF16))
        RC = _st.enter_context(nc.sbuf_tensor("RC", [128, 8], F32))
        PSA = _st.enter_context(nc.psum_tensor("PSA", [128, 4, 512], F32))
        PSO = _st.enter_context(nc.psum_tensor("PSO", [128, 2, 512], F32))
        PST = _st.enter_context(nc.psum_tensor("PST", [128, 512], F32))
        PSS = _st.enter_context(nc.psum_tensor("PSS", [128, 512], F32))
        sLD = _st.enter_context(nc.semaphore("LD"))
        sWL = _st.enter_context(nc.semaphore("WL"))
        sPE = _st.enter_context(nc.semaphore("PE"))
        sACT = _st.enter_context(nc.semaphore("ACT"))
        sDVE = _st.enter_context(nc.semaphore("DVE"))
        sGP = _st.enter_context(nc.semaphore("GP"))
        sST = _st.enter_context(nc.semaphore("ST"))
        sXS = _st.enter_context(nc.semaphore("XS"))
        block = _st.enter_context(nc.Block())
        SEM = {"LD": sLD, "WL": sWL, "PE": sPE, "ACT": sACT, "DVE": sDVE,
               "GP": sGP, "ST": sST, "XS": sXS}
        cnt = {k: 0 for k in SEM}
        lastw = {}
        items = []
        cur_phase = ["init"]
        plan_labels = {}

        def em(eng, f):
            items.append((eng, f))

        def emlab(eng):
            plan_labels.setdefault(eng, []).append(cur_phase[0])

        def W(eng, sem, val):
            if val <= 0:
                return
            assert val <= cnt[sem], f"wait on future {sem} {val} > {cnt[sem]}"
            key = (eng, sem)
            if lastw.get(key, 0) >= val:
                return
            lastw[key] = val
            em(eng, lambda e, s=sem, v=val: e.wait_ge(SEM[s], v))

        def EMI(eng, f, inc=None, by=1):
            emlab(eng)
            if inc is not None:
                cnt[inc] += by
                em(eng, lambda e, ff=f, s=inc, b=by: ff(e).then_inc(SEM[s], b))
                return cnt[inc]
            em(eng, f)
            return None

        def dbg_bar():
            if not dbg_barrier:
                return
            for eng in ("vector", "scalar", "tensor", "sync"):
                W(eng, "ST", cnt["ST"])

        EMI("sync", lambda e: e.dma_start(out=CF[:], in_=cf_d), "LD", 16)
        ld_const = cnt["LD"]
        W("sync", "LD", cnt["LD"])
        EMI("sync", lambda e: e.dma_start(
            out=X[:, 0:4, :],
            in_=x_d[0:512, :].rearrange("(t p) f -> p t f", p=128)),
            "LD", 16)
        ld0a = cnt["LD"]
        W("sync", "LD", cnt["LD"])
        EMI("sync", lambda e: e.dma_start(
            out=X[:, 4:8, :],
            in_=x_d[512:HALF, :].rearrange("(t p) f -> p t f", p=128)),
            "LD", 16)
        ld_x = [[ld0a, cnt["LD"]], [None, None]]

        EMI("gpsimd", lambda e: e.dma_start(out=CB[:], in_=cb_d), "WL", 16)
        W("gpsimd", "WL", cnt["WL"])
        EMI("gpsimd", lambda e: e.dma_start(out=QT[0:1, 0, :], in_=bv_d),
            "WL", 16)
        wl_const = cnt["WL"]

        vh_ones = VH[:].rearrange("p g (h c) -> p g h c", c=65)
        EMI("gpsimd", lambda e: e.memset(vh_ones[:, :, :, 64:65], 1.0), "GP")
        EMI("gpsimd", lambda e: e.memset(CAR[:], 0.0), "GP")
        gp_init = cnt["GP"]

        wslot = {"wq": W0, "wk": W1, "wv": W2, "wr": W0, "wk2": W1,
                 "wv2": W2}
        wl_ready = {}

        def load_weight(wn, tag, wait_pe=0):
            if wait_pe:
                W("gpsimd", "PE", wait_pe)
            W("gpsimd", "WL", cnt["WL"])
            EMI("gpsimd", lambda e, w=wn: e.dma_start(
                out=wslot[w][:],
                in_=w_d[w].rearrange("(k p) m -> p k m", p=128)), "WL", 16)
            wl_ready[tag] = cnt["WL"]

        load_weight("wq", "wq0")
        load_weight("wk", "wk0")
        load_weight("wv", "wv0")

        W("tensor", "WL", wl_const)
        for n in range(2):
            pe = EMI("tensor", lambda e, n=n: e.matmul(
                PSS[:, 0:512], CB[0:1, CB_ONES:CB_ONES + 128],
                QT[0:1, 0, n * 512:(n + 1) * 512], start=True, stop=True),
                "PE")
            W("vector", "PE", pe)
            dv = EMI("vector", lambda e, n=n: e.tensor_copy(
                BVB[:, n * 512:(n + 1) * 512], PSS[:, 0:512]), "DVE")
            W("tensor", "DVE", dv)

        psa_evac = [(None, 0)] * 4
        psa_rr = [0]

        def psum_acc_group(n_k, lhsT_fn, rhs_fn, evac_fn, evac_eng,
                           multi=False, pre_k=None):
            b = psa_rr[0] % 4
            psa_rr[0] += 1
            gsem, gval = psa_evac[b]
            if gsem:
                W("tensor", gsem, gval)
            pe = None
            for k in range(n_k):
                if pre_k is not None:
                    pre_k(k)
                pe = EMI("tensor", lambda e, k=k, b=b: e.matmul(
                    PSA[:, b, :], lhsT_fn(k), rhs_fn(k),
                    start=(k == 0), stop=(k == n_k - 1)),
                    "PE" if k == n_k - 1 else None)
            W(evac_eng, "PE", pe)
            sem = "ACT" if evac_eng == "scalar" else "DVE"
            if multi:
                ev = evac_fn(PSA[:, b, :])
            else:
                ev = EMI(evac_eng, lambda e, b=b: evac_fn(e, PSA[:, b, :]),
                         sem)
            psa_evac[b] = (sem, ev)
            return ev

        tb_guard = {0: (None, 0), 1: (None, 0)}
        tb_banks = [PST, PSS]
        tb_rr = [0]

        def transpose_group(blocks, dest_fn, extra_pe_wait=()):
            bi = tb_rr[0] % 2
            tb_rr[0] += 1
            bank = tb_banks[bi]
            gsem, gval = tb_guard[bi]
            if gsem:
                W("tensor", gsem, gval)
            for sem, val in extra_pe_wait:
                W("tensor", sem, val)
            pe = None
            for i, blk in enumerate(blocks):
                pe = EMI("tensor", lambda e, i=i, blk=blk, bank=bank:
                         e.matmul(bank[:, i * 128:(i + 1) * 128], blk,
                                  CB[:, CB_I128:CB_I128 + 128],
                                  start=True, stop=True),
                         "PE" if i == len(blocks) - 1 else None)
            W("vector", "PE", pe)
            pv = bank[:].rearrange("p (i n) -> p i n", n=128)
            dv = EMI("vector", lambda e, pv=pv, m=len(blocks):
                     dest_fn(e, pv[:, 0:m, :]), "DVE")
            tb_guard[bi] = ("DVE", dv)
            return dv

        stat_ring_i = [0]

        def grab_stat_bank():
            k = stat_ring_i[0] % 6
            stat_ring_i[0] += 1
            if k < 4:
                gsem, gval = psa_evac[k]
                if gsem:
                    W("tensor", gsem, gval)
                bank = PSA[:, k, :]

                def done(sem, val, k=k):
                    psa_evac[k] = (sem, val)
            else:
                bi = k - 4
                gsem, gval = tb_guard[bi]
                if gsem:
                    W("tensor", gsem, gval)
                bank = tb_banks[bi]

                def done(sem, val, bi=bi):
                    tb_guard[bi] = (sem, val)
            return bank, done

        def grab_tb():
            bi = tb_rr[0] % 2
            tb_rr[0] += 1
            bank = tb_banks[bi]
            gsem, gval = tb_guard[bi]
            if gsem:
                W("tensor", gsem, gval)

            def done(sem, val, bi=bi):
                tb_guard[bi] = (sem, val)
            return bank, done

        osb_flat = OSB[:].rearrange("p a n -> p (a n)")
        sq_dummy = PT[:].rearrange("p a b n -> p (a b n)")[:, 0:E]
        ptflat = PT[:].rearrange("p a b n -> p (a b n)")

        def mcol(t):
            return STT[:, 96 + 2 * t:96 + 2 * t + 1]

        def rcol(t):
            return STT[:, 112 + t:112 + t + 1]

        def ln_stats(src_tiles, pre=(), base=0, pre_tile=None):
            nt = len(src_tiles)
            for sem, val in pre:
                W("vector", sem, val)
                W("scalar", sem, val)
            dst = None
            for i, ap in enumerate(src_tiles):
                if pre_tile is not None:
                    pre_tile(i)
                t = base + i
                for hlf in range(2):
                    dst = EMI("vector", lambda e, t=t, ap=ap, hlf=hlf:
                              e.bn_stats(
                                  STT[:, t * 12 + hlf * 6:t * 12 + hlf * 6
                                      + 6],
                                  ap[:, hlf * 512:(hlf + 1) * 512]),
                              "DVE" if (i == nt - 1 and hlf == 1) else None)
            W("vector", "DVE", dst)
            dag = None
            for i in range(nt):
                t = base + i
                dag = EMI("vector", lambda e, t=t: e.bn_aggr(
                    STT[:, 96 + 2 * t:96 + 2 * t + 2],
                    STT[:, t * 12:t * 12 + 12]),
                    "DVE" if i == nt - 1 else None)
            W("scalar", "DVE", dag)
            var_view = STT[:, 96 + 2 * base:96 + 2 * (base + nt)].rearrange(
                "p (t two) -> p t two", two=2)[:, :, 1:2]
            a1 = EMI("scalar", lambda e, vv=var_view, nt=nt, base=base:
                     e.activation(
                         STT[:, 120 + base:120 + base + nt], vv, AF.Ln,
                         bias=CF[:, CF_EPS:CF_EPS + 1]), "ACT")
            W("scalar", "ACT", a1)
            ac = EMI("scalar", lambda e, nt=nt, base=base: e.activation(
                STT[:, 112 + base:112 + base + nt],
                STT[:, 120 + base:120 + base + nt], AF.Exp,
                scale=-0.5), "ACT")
            return ac

        out_sts = []
        preA = {}
        preA_xn = {}
        for h in range(2):
            T0 = h * HALF
            W("vector", "LD", ld_x[h][0])
            W("scalar", "LD", ld_x[h][0])
            W("vector", "LD", ld_const)
            W("scalar", "LD", ld_const)

            if dbg_on("xload%d" % h, [128, NT, E]):
                W("gpsimd", "LD", ld_x[h][1])
                EMI("gpsimd", lambda e, h=h: e.dma_start(
                    out=dbg_d["xload%d" % h], in_=X[:]), "ST", 16)
                dbg_bar()
            cur_phase[0] = "A_ln1_h%d" % h
            rstd_b = {}
            xn_r = {}
            osb_free_pe = [0]

            def ln1_batch(batch):
                for t in range(batch * 4, batch * 4 + 4):
                    W("vector", "ACT", rstd_b[batch])
                    W("vector", "PE", osb_free_pe[0])
                    W("vector", "DVE", cnt["DVE"])
                    W("vector", "XS", cnt["XS"])
                    dvn = EMI("vector", lambda e, t=t: e.tensor_scalar(
                        osb_flat, X[:, t, :], mcol(t), rcol(t),
                        ALU.subtract, ALU.mult), "DVE")
                    for g in range(2):
                        c0 = g * 4
                        transpose_group(
                            [osb_flat[:, (c0 + c) * 128:(c0 + c + 1) * 128]
                             for c in range(4)],
                            lambda e, pv, t=t, c0=c0: e.tensor_copy(
                                XN[:, c0:c0 + 4, t * 128:(t + 1) * 128], pv),
                            extra_pe_wait=(("DVE", dvn), ("WL", wl_const)))
                    osb_free_pe[0] = cnt["PE"]
                xn_r[batch] = cnt["DVE"]

            if h in preA_xn:
                xn_r[0] = preA_xn[h]
            else:
                rstd_b[0] = ln_stats([X[:, t, :] for t in range(4)], base=0)
                ln1_batch(0)

            if dbg_on("xn%d" % h, [128, NCH, HALF]):
                W("gpsimd", "DVE", xn_ready)
                EMI("gpsimd", lambda e, h=h: e.dma_start(
                    out=dbg_d["xn%d" % h], in_=XN[:, :, 0:HALF]), "ST", 16)
                dbg_bar()

            sq_guard = [0]

            def plan_chunk_sq(src, evac_ac):
                W("gpsimd", "ACT", evac_ac)
                W("gpsimd", "PE", sq_guard[0])
                return EMI("gpsimd", lambda e, src=src: e.tensor_tensor(
                    ptflat[:, 0:HALF], src, src, ALU.mult), "GP")

            def plan_chunk_stats(sqv, out_col0, pi, kmax_combine):
                col = out_col0 + pi
                for n in range(2):
                    bank, done = grab_tb()
                    W("tensor", "GP", sqv)
                    pe = EMI("tensor", lambda e, n=n, bank=bank: e.matmul(
                        bank[:, 0:512],
                        CB[:, CB_BD:CB_BD + 128],
                        ptflat[:, n * 512:(n + 1) * 512],
                        start=True, stop=True), "PE")
                    sq_guard[0] = pe
                    W("vector", "PE", pe)
                    dvl = EMI("vector", lambda e, bank=bank, n=n:
                              e.reduce_max(
                                  SMALL[:, 48 + n:49 + n],
                                  bank[:, 0:512], axis=AX), "DVE")
                    done("DVE", dvl)
                W("vector", "DVE", dvl)
                if kmax_combine:
                    dvl = EMI("vector", lambda e: e.reduce_max(
                        SMALL[:, 52:53], SMALL[:, 48:50], axis=AX), "DVE")
                    W("vector", "DVE", dvl)
                    dvl = EMI("vector", lambda e, col=col: e.tensor_tensor(
                        SMALL[:, 8 + col:8 + col + 1],
                        SMALL[:, col:col + 1],
                        SMALL[:, 52:53], ALU.max), "DVE")
                    W("vector", "DVE", dvl)
                else:
                    dvl = EMI("vector", lambda e, col=col: e.reduce_max(
                        SMALL[:, col:col + 1],
                        SMALL[:, 48:50], axis=AX), "DVE")
                    W("vector", "DVE", dvl)

            q_evac = {}
            k_evac = {}
            stats_pending = []

            def qkv_n(n, wt, evac_fn_mk, tag_evacs):
                W("tensor", "DVE", xn_r[n])
                for m in range(NCH):
                    ev = psum_acc_group(
                        NCH,
                        lambda k, m=m, wt=wt: wt[:, k,
                                                 m * 128:(m + 1) * 128],
                        lambda k, n=n: XN[:, k, n * 512:(n + 1) * 512],
                        evac_fn_mk(m, n), "scalar")
                    if n == 1:
                        tag_evacs[m] = ev
                    if n == 1 and stats_pending:
                        stats_pending.pop(0)()

            def q_evac_mk(m, n):
                return lambda e, ps, m=m, n=n: e.activation(
                    QT[:, m, n * 512:(n + 1) * 512], ps, AF.Identity,
                    bias=CF[:, CF_BQ + m:CF_BQ + m + 1])

            def k_evac_mk(m, n):
                return lambda e, ps, m=m, n=n, T0=T0: e.activation(
                    KT[:, m, T0 + n * 512:T0 + (n + 1) * 512], ps,
                    AF.Identity, bias=CF[:, CF_BK + m:CF_BK + m + 1])

            def v_tile(t):
                g = h * NT + t
                dst = VH[:, g, :].rearrange("p (hh c) -> p hh c", c=65)
                W("tensor", "DVE", xn_r[t // 4])
                for n in range(2):
                    psum_acc_group(
                        NCH,
                        lambda k, t=t: XN[:, k, t * 128:(t + 1) * 128],
                        lambda k, n=n: W2[:, k, n * 512:(n + 1) * 512],
                        lambda e, ps, dst=dst, n=n: e.activation(
                            dst[:, n * 8:(n + 1) * 8, 0:64],
                            ps.rearrange("p (hh c) -> p hh c", c=64),
                            AF.Identity),
                        "scalar")
                if stats_pending:
                    stats_pending.pop(0)()

            cur_phase[0] = "A2_pass1_h%d" % h
            W("tensor", "WL", wl_ready["wq0" if h == 0 else "wq1"])
            qkv_n(0, W0, q_evac_mk, q_evac)
            W("tensor", "WL", wl_ready["wk0" if h == 0 else "wk1"])
            qkv_n(0, W1, k_evac_mk, k_evac)
            W("tensor", "WL", wl_ready["wv0" if h == 0 else "wv1"])
            W("tensor", "GP", gp_init)
            for t in range(4):
                v_tile(t)
            cur_phase[0] = "A_ln1b1_h%d" % h
            rstd_b[1] = ln_stats([X[:, t, :] for t in range(4, 8)],
                                 base=4, pre=(("LD", ld_x[h][1]),))
            ln1_batch(1)
            cur_phase[0] = "A2_pass2_h%d" % h
            qkv_n(1, W0, q_evac_mk, q_evac)
            q_done = cnt["PE"]
            load_weight("wr", "wr%d" % h, wait_pe=q_done)

            def two_stage(src_fn, col0, pi, comb):
                holder = {}

                def stage_sq(pi=pi):
                    holder["sqv"] = plan_chunk_sq(src_fn(pi),
                                                  holder["evac"])

                def stage_mm(pi=pi):
                    plan_chunk_stats(holder["sqv"], col0, pi, comb)
                return holder, stage_sq, stage_mm

            q_stage = []
            for pi in range(NCH):
                hld, s1f, s2f = two_stage(
                    lambda pi: QT[:, pi, :], 16, pi, False)
                hld["evac"] = q_evac[pi]
                q_stage.append((s1f, s2f))
            for s1f, s2f in q_stage:
                stats_pending.append(s1f)
                stats_pending.append(s2f)
            qkv_n(1, W1, k_evac_mk, k_evac)
            k_done = cnt["PE"]
            load_weight("wk2", "wk2%d" % h, wait_pe=k_done)
            k_stage = []
            for pi in range(NCH):
                hld, s1f, s2f = two_stage(
                    lambda pi, T0=T0: KT[:, pi, T0:T0 + HALF], 0, pi,
                    h == 1)
                hld["evac"] = k_evac[pi]
                k_stage.append((s1f, s2f))
            for s1f, s2f in k_stage:
                stats_pending.append(s1f)
                stats_pending.append(s2f)
            for t in range(4, NT):
                v_tile(t)
            while stats_pending:
                stats_pending.pop(0)()
            v_done = cnt["PE"]
            load_weight("wv2", "wv2%d" % h, wait_pe=v_done)

            cur_phase[0] = "NM_h%d" % h
            kb = 0 if h == 0 else 8
            dvq = EMI("vector", lambda e, kb=kb: e.tensor_tensor(
                SMALL[:, 32:40], SMALL[:, 16:24],
                SMALL[:, kb:kb + 8], ALU.mult), "DVE")
            W("scalar", "DVE", dvq)
            a1 = EMI("scalar", lambda e: e.activation(
                MH[:, 0:8], SMALL[:, 32:40], AF.Ln,
                bias=CF[:, CF_EPS:CF_EPS + 1]), "ACT")
            W("scalar", "ACT", a1)
            ac = EMI("scalar", lambda e: e.activation(
                MH[:, 8:16], MH[:, 0:8], AF.Exp, scale=0.5), "ACT")
            W("vector", "ACT", ac)
            dv = EMI("vector", lambda e: e.tensor_scalar(
                NMB[:, 0:8], MH[:, 8:16], -0.125, None, ALU.mult),
                "DVE")
            bank, done = grab_tb()
            W("tensor", "DVE", dv)
            EMI("tensor", lambda e, bank=bank: e.matmul(
                bank[:, 0:8], CB[0:1, CB_ONES:CB_ONES + 128],
                NMB[0:1, 0:8], start=True, stop=True), "PE")
            pe = EMI("tensor", lambda e, bank=bank: e.matmul(
                bank[:, 8:16], CB[64:65, CB_ONES:CB_ONES + 128],
                NMB[64:65, 0:8], start=True, stop=True,
                tile_position=(64, 0)), "PE")
            W("vector", "PE", pe)
            dcp = EMI("vector", lambda e, bank=bank: e.tensor_copy(
                MH[:, 0:8], bank[:, 0:8]), "DVE")
            W("vector", "DVE", dcp)
            mh_ready = EMI("vector", lambda e, bank=bank: e.tensor_tensor(
                MHP[:, 0:8], MH[:, 0:8], bank[:, 8:16], ALU.min), "DVE")
            done("DVE", mh_ready)

            if dbg_on("mh%d" % h, [128, 16]):
                W("gpsimd", "DVE", mh_ready)
                EMI("gpsimd", lambda e, h=h: e.dma_start(
                    out=dbg_d["mh%d" % h], in_=MH[:]), "ST", 16)
                dbg_bar()

            W("tensor", "ACT", cnt["ACT"])
            W("tensor", "DVE", cnt["DVE"])
            exp_of = {}
            pv_of = {}
            pt_guard = {}
            last_o_ac = [0]
            lns_flat = LNS[:]

            def jmax_of(sg):
                return (T0 + sg * 512 + 511) // 128

            def jparams(sg, j):
                seg_lo = T0 + sg * 512
                seg_hi = seg_lo + 512
                qstart = seg_lo if j * 128 < seg_lo else j * 128
                N = min(seg_hi - qstart, 512)
                diag = j * 128 >= seg_lo
                return qstart - T0, qstart - seg_lo, N, diag

            def emit_scores(sg, pi, j):
                cur_phase[0] = "attn_h%d_s%d_p%d" % (h, sg, pi)
                par = j % 2
                qoff, soff, N, diag = jparams(sg, j)
                if (par, 1) in exp_of:
                    W("tensor", "ACT", exp_of[(par, 1)])
                sc = None
                for head in range(2):
                    b = 2 * par + head
                    base = head * 64
                    sc = EMI("tensor", lambda e, b=b, base=base,
                             pi=pi, j=j, qoff=qoff, N=N: e.matmul(
                                 PSA[:, b, 0:N],
                                 KT[base:base + 64, pi,
                                    j * 128:(j + 1) * 128],
                                 QT[base:base + 64, pi,
                                    qoff:qoff + N],
                                 start=True, stop=True,
                                 tile_position=(base, 0)),
                             "PE" if head == 1 else None)
                return sc

            def emit_exp(sg, pi, j, sc_pe):
                par = j % 2
                slot = 2 if j == 0 else par
                qoff, soff, N, diag = jparams(sg, j)
                if slot in pt_guard:
                    W("scalar", "PE", pt_guard[slot])
                W("scalar", "PE", sc_pe)
                W("scalar", "DVE", mh_ready)
                EMI("scalar", lambda e, slot=slot, pi=pi, par=par, N=N:
                    e.activation(
                        PT[:, slot, :, 0:N],
                        PSA[:, 2 * par:2 * par + 2, 0:N],
                        AF.Exp, bias=MHP[:, pi:pi + 1],
                        scale=0.125), "ACT")
                exp_of[(par, 1)] = cnt["ACT"]
                ev = cnt["ACT"]
                pv_wait = [("ACT", ev)]
                if diag:
                    W("vector", "ACT", ev)
                    dvm = None
                    for head in range(2):
                        dvm = EMI("vector", lambda e, slot=slot,
                                  head=head: e.tensor_tensor(
                                      PT[:, slot, head, 0:128],
                                      PT[:, slot, head, 0:128],
                                      CB[:, CB_MASK:CB_MASK + 128],
                                      ALU.mult),
                                  "DVE" if head == 1 else None)
                    pv_wait.append(("DVE", dvm))
                return pv_wait

            def emit_pv(sg, pi, j, pv_wait):
                par = j % 2
                slot = 2 if j == 0 else par
                qoff, soff, N, diag = jparams(sg, j)
                jm = jmax_of(sg)
                if j == 0:
                    W("tensor", "DVE", last_o_ac[0])
                for sem, val in pv_wait:
                    W("tensor", sem, val)
                for head in range(2):
                    hh = pi * 2 + head
                    EMI("tensor", lambda e, head=head, hh=hh, j=j,
                        slot=slot, soff=soff, N=N, jm=jm:
                        e.matmul(
                            PSO[0:65, head, soff:soff + N],
                            VH[:, j, hh * 65:hh * 65 + 65],
                            PT[:, slot, head, 0:N],
                            start=(j == 0), stop=(j == jm)),
                        "PE" if head == 1 else None)
                pv_of[par] = cnt["PE"]
                pt_guard[slot] = cnt["PE"]

            def o_evac(sg, pi):
                cur_phase[0] = "ofin_h%d_s%d_p%d" % (h, sg, pi)
                W("vector", "PE", pv_of[0])
                W("vector", "PE", pv_of[1])
                for head in range(2):
                    EMI("vector", lambda e, head=head: e.tensor_copy(
                        OSB[0:65, head, :], PSO[0:65, head, :]),
                        "DVE" if head == 1 else None)
                o_dv = EMI("vector", lambda e: e.tensor_copy(
                    RC[:, 4:5], RC[:, 0:1]), "DVE")
                last_o_ac[0] = o_dv

            def o_transp(sg, pi):
                cur_phase[0] = "otr_h%d_s%d_p%d" % (h, sg, pi)
                o_ac = last_o_ac[0]
                for head in range(2):
                    hh = pi * 2 + head
                    bank, done = grab_tb()
                    W("tensor", "DVE", o_ac)
                    pe = None
                    for qt in range(4):
                        pe = EMI("tensor", lambda e, head=head, qt=qt,
                                 bank=bank: e.matmul(
                                     bank[:, qt * 128:qt * 128 + 65],
                                     OSB[0:65, head,
                                         qt * 128:(qt + 1) * 128],
                                     CB[0:65, CB_I65:CB_I65 + 65],
                                     start=True, stop=True),
                                 "PE" if qt == 3 else None)
                    W("vector", "PE", pe)
                    W("vector", "DVE", cnt["DVE"])
                    bv = bank[:].rearrange("p (i n) -> p i n", n=128)
                    dvr = EMI("vector", lambda e, bv=bv: e.reciprocal(
                        RC[:, 0:4], bv[:, 0:4, 64:65]), "DVE")
                    W("vector", "DVE", dvr)
                    dvo = None
                    for qt in range(4):
                        tt = sg * 4 + qt
                        dvo = EMI("vector", lambda e, bv=bv, qt=qt,
                                  tt=tt, hh=hh: e.scalar_tensor_tensor(
                                      AO[:, tt, hh * 64:hh * 64 + 64],
                                      bv[:, qt, 0:64],
                                      RC[:, qt:qt + 1],
                                      BVB[:, hh * 64:hh * 64 + 64],
                                      ALU.mult, ALU.add),
                                  "DVE" if qt == 3 else None)
                    done("DVE", dvo)

            def plan_C(qq, pre_dve):
                cur_phase[0] = "C_h%d_q%d" % (h, qq)
                tiles = [qq * 4 + i for i in range(4)]
                rstd_a = ln_stats([AO[:, t, :] for t in tiles],
                                  pre=(("DVE", pre_dve),))
                yield
                use_gp = (qq == 0)
                gp_adds = []
                dvx = None
                for i, t in enumerate(tiles):
                    W("vector", "ACT", rstd_a)
                    W("vector", "DVE", cnt["DVE"])
                    if gp_adds:
                        W("vector", "GP", gp_adds[-1])
                    dts = EMI("vector", lambda e, t=t, i=i: e.tensor_scalar(
                        lns_flat, AO[:, t, :], mcol(i), rcol(i),
                        ALU.subtract, ALU.mult), "DVE")
                    if use_gp:
                        W("gpsimd", "DVE", dts)
                        gp_adds.append(EMI("gpsimd", lambda e, t=t:
                                           e.tensor_tensor(
                                               X[:, t, :], X[:, t, :],
                                               lns_flat, ALU.add), "GP"))
                    else:
                        W("vector", "DVE", dts)
                        dvx = EMI("vector", lambda e, t=t: e.tensor_tensor(
                            X[:, t, :], X[:, t, :], lns_flat, ALU.add),
                            "DVE")
                    if i == 1:
                        yield
                yield
                if use_gp:
                    rstd_2 = ln_stats([X[:, t, :] for t in tiles],
                                      pre_tile=lambda i: W("vector", "GP",
                                                           gp_adds[i]))
                else:
                    rstd_2 = ln_stats([X[:, t, :] for t in tiles],
                                      pre=(("DVE", dvx),))
                if qq == 0:
                    W("vector", "GP", gp_init)
                    EMI("vector", lambda e: e.tensor_copy(
                        XN[:, :, 0:1], CAR[:]))
                else:
                    W("vector", "DVE", cnt["DVE"])
                    EMI("vector", lambda e: e.tensor_copy(
                        XN[:, :, 520:521], XN[:, :, 512:513]))
                yield
                osb_free = 0
                for i, t in enumerate(tiles):
                    W("vector", "ACT", rstd_2)
                    W("vector", "PE", osb_free)
                    dvn = EMI("vector", lambda e, t=t, i=i: e.tensor_scalar(
                        lns_flat, X[:, t, :], mcol(i), rcol(i),
                        ALU.subtract, ALU.mult), "DVE")
                    for g2 in range(2):
                        c0 = g2 * 4
                        transpose_group(
                            [lns_flat[:, (c0 + c) * 128:(c0 + c + 1) * 128]
                             for c in range(4)],
                            lambda e, pv, i=i, c0=c0, qq=qq: e.tensor_copy(
                                XN[:, c0:c0 + 4,
                                   qq * 520 + 1 + i * 128:
                                   qq * 520 + 1 + (i + 1) * 128], pv),
                            extra_pe_wait=(("DVE", dvn),))
                    osb_free = cnt["PE"]
                    yield
                if qq == 1:
                    W("vector", "DVE", cnt["DVE"])
                    EMI("vector", lambda e: e.tensor_copy(
                        CAR[:], XN[:, :, 1032:1033]))

            aof = AO[:].rearrange("p t f -> p (t f)")
            qtf = QT[:].rearrange("p c n -> p (c n)")
            Avs = [qtf[:, c * 512:(c + 1) * 512] for c in range(NCH)]
            Bvs = [qtf[:, 4096 + c * 512:4096 + (c + 1) * 512]
                   for c in range(NCH)]
            Cvs = [aof[:, c * 512:(c + 1) * 512] for c in range(NCH)]
            XKs = [aof[:, 4096 + c * 512:4096 + (c + 1) * 512]
                   for c in range(NCH)]
            OTs = [aof[:, 4096 + tl * 1024:4096 + (tl + 1) * 1024]
                   for tl in range(4)]

            def plan_mix_d(qq, pe_wait, dve_wait, on_gp):
                cur_phase[0] = "mixd_h%d_q%d" % (h, qq)
                x2 = [XN[:, c, qq * 520:qq * 520 + 513] for c in range(NCH)]
                eng, sem = ("gpsimd", "GP") if on_gp else ("vector", "DVE")
                if pe_wait:
                    W(eng, "PE", pe_wait)
                if on_gp and dve_wait:
                    W("gpsimd", "DVE", dve_wait)
                vals = []
                for c in range(NCH):
                    vals.append(EMI(eng, lambda e, c=c, xc=x2[c]:
                                    e.tensor_tensor(
                                        Cvs[c], xc[:, 1:513], xc[:, 0:512],
                                        ALU.subtract), sem))
                return vals

            def plan_mix_xr(qq, pe_wait, d_vals, d_on_gp):
                cur_phase[0] = "mixr_h%d_q%d" % (h, qq)
                x2 = [XN[:, c, qq * 520:qq * 520 + 513] for c in range(NCH)]
                if pe_wait:
                    W("vector", "PE", pe_wait)
                xr_vals = []
                for c in range(NCH):
                    if d_on_gp:
                        W("vector", "GP", d_vals[c])
                    xr_vals.append(EMI("vector", lambda e, c=c, xc=x2[c]:
                                       e.scalar_tensor_tensor(
                                           Avs[c], Cvs[c],
                                           CF[:, CF_MUR + c:CF_MUR + c + 1],
                                           xc[:, 0:512],
                                           ALU.mult, ALU.add), "DVE"))
                return xr_vals

            xr_gp = {}

            def mix0():
                cur_phase[0] = "mixd_h%d_q0" % h
                x2 = [XN[:, c, 0:513] for c in range(NCH)]
                W("vector", "PE", cnt["PE"])
                vals = []
                for c in range(NCH):
                    EMI("vector", lambda e, c=c, xc=x2[c]:
                        e.tensor_tensor(
                            Cvs[c], xc[:, 1:513], xc[:, 0:512],
                            ALU.subtract), None)
                    vals.append(EMI("vector", lambda e, c=c, xc=x2[c]:
                                    e.scalar_tensor_tensor(
                                        Avs[c], Cvs[c],
                                        CF[:, CF_MUR + c:CF_MUR + c + 1],
                                        xc[:, 0:512],
                                        ALU.mult, ALU.add), "DVE"))
                xr_gp[0] = vals

            mix_hook = [mix0]
            cgen = None
            units = [(sg, pi, j) for sg in range(2) for pi in range(NCH)
                     for j in range(jmax_of(sg) + 1)]
            nu = len(units)
            sc_of = {}
            pvw = {}

            def ensure_sc(ui):
                if ui < nu and ui not in sc_of:
                    sc_of[ui] = emit_scores(*units[ui])

            def ensure_exp(ui):
                if ui < nu and ui not in pvw:
                    ensure_sc(ui)
                    pvw[ui] = emit_exp(*units[ui], sc_of[ui])

            ensure_sc(0)
            ensure_sc(1)
            ensure_exp(0)
            pending_transp = [None]
            for ui in range(nu):
                sg, pi, j = units[ui]
                ensure_exp(ui + 1)
                ensure_sc(ui + 2)
                emit_pv(sg, pi, j, pvw[ui])
                if pending_transp[0] is not None:
                    pending_transp[0]()
                    pending_transp[0] = None
                if j == jmax_of(sg):
                    o_evac(sg, pi)
                    pending_transp[0] = (
                        lambda sg=sg, pi=pi: o_transp(sg, pi))
                    if sg == 1 and pi == 0:
                        cgen = plan_C(0, cnt["DVE"])
                    if sg == 1 and pi == NCH - 1:
                        if cgen is not None:
                            for _ in cgen:
                                pass
                            cgen = None
                        if mix_hook[0] is not None:
                            mix_hook[0]()
                            mix_hook[0] = None
                    ensure_exp(ui + 1)
                    ensure_exp(ui + 2)
                elif j == 1 and cgen is not None:
                    next(cgen, None)
            if pending_transp[0] is not None:
                pending_transp[0]()
                pending_transp[0] = None
            if cgen is not None:
                for _ in cgen:
                    pass
            attn_pe_done = cnt["PE"]
            seg1_done = cnt["DVE"]

            cgen1 = plan_C(1, seg1_done)

            if dbg_on("ao%d" % h, [128, NT, E]):
                W("gpsimd", "DVE", cnt["DVE"])
                EMI("gpsimd", lambda e, h=h: e.dma_start(
                    out=dbg_d["ao%d" % h], in_=AO[:]), "ST", 16)
                dbg_bar()

            ld_x1_tile = {}
            for qq in range(2):
                x2 = [XN[:, c, qq * 520:qq * 520 + 513] for c in range(NCH)]
                cur_phase[0] = "ffn_r_h%d_q%d" % (h, qq)

                def emit_xk(x2=x2, qq=qq):
                    cur_phase[0] = "mixk_h%d_q%d" % (h, qq)
                    xk = None
                    for c in range(NCH):
                        xk = EMI("vector", lambda e, c=c, xc=x2[c]:
                                 e.scalar_tensor_tensor(
                                     XKs[c], Cvs[c],
                                     CF[:, CF_MUK + c:CF_MUK + c + 1],
                                     xc[:, 0:512], ALU.mult, ALU.add),
                                 "DVE" if c == NCH - 1 else None)
                    return xk

                xk_dve = None
                W("tensor", "WL", wl_ready["wr%d" % h])
                for m in range(NCH):
                    def evac_r(ps, m=m):
                        s1 = PT[:, 0, m % 2, :]
                        s2 = PT[:, 1, m % 2, :]
                        a1 = EMI("scalar", lambda e, ps=ps, s1=s1:
                                 e.activation(s1, ps, AF.Exp, scale=-1.0),
                                 "ACT")
                        W("scalar", "ACT", a1)
                        a2 = EMI("scalar", lambda e, s1=s1, s2=s2:
                                 e.activation(s2, s1, AF.Ln,
                                              bias=CF[:, CF_ONE:CF_ONE + 1]),
                                 "ACT")
                        W("scalar", "ACT", a2)
                        return EMI("scalar", lambda e, s2=s2, m=m:
                                   e.activation(Bvs[m], s2, AF.Exp,
                                                scale=-1.0), "ACT")
                    r_ac = psum_acc_group(
                        NCH,
                        lambda k, m=m: W0[:, k, m * 128:(m + 1) * 128],
                        lambda k: Avs[k],
                        evac_r, "scalar", multi=True,
                        pre_k=lambda k, qq=qq: W("tensor", "DVE",
                                                 xr_gp[qq][k]))
                    if cgen1 is not None:
                        next(cgen1, None)
                        if m == 2:
                            xk_dve = emit_xk()
                r_pe = cnt["PE"]
                if cgen1 is not None:
                    for _ in cgen1:
                        pass
                    cgen1 = None
                    xn_q1_dve = cnt["DVE"]
                if xk_dve is None:
                    xk_dve = emit_xk()
                cur_phase[0] = "ffn_k_h%d_q%d" % (h, qq)
                W("tensor", "WL", wl_ready["wk2%d" % h])
                W("tensor", "DVE", xk_dve)
                preA_tile = [None]
                if h == 0 and qq == 1:
                    cur_phase[0] = "preA_h0"
                    preA[1] = ln_stats([X[:, t, :] for t in range(4)],
                                       base=0,
                                       pre_tile=lambda i: W(
                                           "vector", "LD", ld_x1_tile[i]))
                    ofp = [0]

                    def preA_tile_fn(t):
                        cur_phase[0] = "preA_h0"
                        W("vector", "ACT", preA[1])
                        W("vector", "PE", ofp[0])
                        dvn = EMI("vector", lambda e, t=t: e.tensor_scalar(
                            lns_flat, X[:, t, :], mcol(t), rcol(t),
                            ALU.subtract, ALU.mult), "DVE")
                        for g in range(2):
                            c0 = g * 4
                            transpose_group(
                                [lns_flat[:, (c0 + c) * 128:
                                          (c0 + c + 1) * 128]
                                 for c in range(4)],
                                lambda e, pv, t=t, c0=c0: e.tensor_copy(
                                    XN[:, c0:c0 + 4,
                                       t * 128:(t + 1) * 128], pv),
                                extra_pe_wait=(("DVE", dvn),))
                        ofp[0] = cnt["PE"]
                        if t == 3:
                            preA_xn[1] = cnt["DVE"]
                    preA_tile[0] = preA_tile_fn
                for m in range(NCH):
                    def evac_k(ps, m=m):
                        s1 = PT[:, 0, m % 2, :]
                        a1 = EMI("scalar", lambda e, ps=ps, s1=s1:
                                 e.activation(s1, ps, AF.Relu), "ACT")
                        W("scalar", "ACT", a1)
                        return EMI("scalar", lambda e, s1=s1, m=m:
                                   e.activation(Cvs[m], s1, AF.Square),
                                   "ACT")
                    psum_acc_group(
                        NCH,
                        lambda k, m=m: W1[:, k, m * 128:(m + 1) * 128],
                        lambda k: XKs[k],
                        evac_k, "scalar", multi=True)
                    if m % 2 == 1 and preA_tile[0] is not None:
                        preA_tile[0](m // 2)
                        cur_phase[0] = "ffn_k_h%d_q%d" % (h, qq)
                k2_pe = cnt["PE"]
                cur_phase[0] = "ffn_v_h%d_q%d" % (h, qq)
                W("tensor", "WL", wl_ready["wv2%d" % h])
                W("tensor", "ACT", cnt["ACT"])
                W("vector", "ACT", r_ac)
                for m in range(NCH):
                    psum_acc_group(
                        NCH,
                        lambda k, m=m: W2[:, k, m * 128:(m + 1) * 128],
                        lambda k: Cvs[k],
                        lambda e, ps, m=m: e.tensor_tensor(
                            Avs[m], ps, Bvs[m], ALU.mult),
                        "vector")
                v2_pe = cnt["PE"]
                outT_dve = cnt["DVE"]
                for tl in range(4):
                    for g2 in range(2):
                        c0 = g2 * 4
                        transpose_group(
                            [Avs[c0 + c][:, tl * 128:(tl + 1) * 128]
                             for c in range(4)],
                            lambda e, pv, tl=tl, c0=c0: e.tensor_copy(
                                OTs[tl].rearrange("p (c n) -> p c n",
                                                  n=128)[:, c0:c0 + 4, :],
                                pv),
                            extra_pe_wait=(("DVE", outT_dve),))
                outtok_dve = cnt["DVE"]
                if qq == 0:
                    d1_vals = plan_mix_d(1, cnt["PE"], xn_q1_dve, True)
                cur_phase[0] = "ln3_h%d_q%d" % (h, qq)
                tiles = [qq * 4 + i for i in range(4)]
                rstd_ac = ln_stats(OTs, pre=(("DVE", outtok_dve),), base=4)
                dvf = None
                for i, t in enumerate(tiles):
                    W("vector", "ACT", rstd_ac)
                    W("vector", "DVE", cnt["DVE"])
                    W("vector", "XS", cnt["XS"])
                    dts = EMI("vector", lambda e, i=i: e.tensor_scalar(
                        osb_flat, OTs[i], mcol(4 + i), rcol(4 + i),
                        ALU.subtract, ALU.mult), "DVE")
                    W("vector", "DVE", dts)
                    dvf = EMI("vector", lambda e, t=t: e.tensor_tensor(
                        X[:, t, :], X[:, t, :], osb_flat, ALU.add), "XS")
                    W("sync", "XS", dvf)
                    EMI("sync", lambda e, h=h, T0=T0, t=t: e.dma_start(
                        out=out_d[T0 + t * 128:T0 + (t + 1) * 128,
                                  :].rearrange("(a p) f -> p a f", p=128),
                        in_=X[:, t, :]), "ST", 16)
                    st_q = cnt["ST"]
                    if h == 0:
                        ldq = "gpsimd" if qq == 0 else "sync"
                        W(ldq, "ST", st_q)
                        W(ldq, "LD", cnt["LD"])
                        EMI(ldq, lambda e, t=t: e.dma_start(
                            out=X[:, t, :],
                            in_=x_d[HALF + t * 128:HALF + (t + 1) * 128,
                                    :].rearrange("(a p) f -> p a f", p=128)),
                            "LD", 16)
                        ld_x1_tile[t] = cnt["LD"]
                    if i == 0 and qq == 0:
                        xr_gp[1] = plan_mix_xr(1, cnt["PE"], d1_vals, True)
                        cur_phase[0] = "ln3_h%d_q%d" % (h, qq)
                if h == 0:
                    ld_x[1][qq] = cnt["LD"]

            half_done_dve = cnt["DVE"]
            if dbg_on("xf%d" % h, [128, NT, E]):
                W("gpsimd", "DVE", half_done_dve)
                EMI("gpsimd", lambda e, h=h: e.dma_start(
                    out=dbg_d["xf%d" % h], in_=X[:]), "ST", 16)
                dbg_bar()
            if h == 0:
                load_weight("wq", "wq1", wait_pe=r_pe)
                load_weight("wk", "wk1", wait_pe=k2_pe)
                load_weight("wv", "wv1", wait_pe=v2_pe)

        W("sync", "ST", cnt["ST"])

        def runner(name):
            def go(eng):
                for enm, f in items:
                    if enm == name:
                        f(eng)
            return go

        block.sync(runner("sync"))
        block.gpsimd(runner("gpsimd"))
        block.tensor(runner("tensor"))
        block.vector(runner("vector"))
        block.scalar(runner("scalar"))

    build_program.labels = plan_labels
    return nc, sorted(dbg_d.keys())


def _pack_inputs(ins):
    wqT = np.ascontiguousarray(
        np.asarray(ins["L0_attn_Wq"]).transpose(2, 0, 1).reshape(E, E))
    wkT = np.ascontiguousarray(
        np.asarray(ins["L0_attn_Wk"]).transpose(2, 0, 1).reshape(E, E))
    wvT = np.ascontiguousarray(
        np.asarray(ins["L0_attn_Wv"]).transpose(2, 0, 1).reshape(E, E))
    wrT = np.ascontiguousarray(np.asarray(ins["L0_ffn_W_r"]).T)
    wk2T = np.ascontiguousarray(np.asarray(ins["L0_ffn_W_k"]).T)
    wv2T = np.ascontiguousarray(np.asarray(ins["L0_ffn_W_v"]).T)

    cf = np.zeros((128, CF_W), dtype=np.float32)
    cf[:, CF_BQ:CF_BQ + 8] = np.asarray(
        ins["L0_attn_bq"], dtype=np.float32).reshape(8, 128).T
    cf[:, CF_BK:CF_BK + 8] = np.asarray(
        ins["L0_attn_bk"], dtype=np.float32).reshape(8, 128).T
    cf[:, CF_MUR:CF_MUR + 8] = np.asarray(
        ins["L0_ffn_mu_r"], dtype=np.float32).reshape(8, 128).T
    cf[:, CF_MUK:CF_MUK + 8] = np.asarray(
        ins["L0_ffn_mu_k"], dtype=np.float32).reshape(8, 128).T
    cf[:, CF_EPS] = EPS
    cf[:, CF_ONE] = 1.0

    cb = np.zeros((128, CB_W), dtype=np.float32)
    cb[:, CB_I128:CB_I128 + 128] = np.eye(128)
    p = np.arange(128)
    cb[:, CB_MASK:CB_MASK + 128] = (p[:, None] <= p[None, :])
    cb[0:65, CB_I65:CB_I65 + 65] = np.eye(65)
    cb[:, CB_ONES:CB_ONES + 128] = 1.0
    cb[0:64, CB_BD:CB_BD + 64] = 1.0
    cb[64:128, CB_BD + 64:CB_BD + 128] = 1.0

    bvf = np.ascontiguousarray(
        np.asarray(ins["L0_attn_bv"], dtype=np.float32).reshape(1, E))

    common = {"wq": wqT, "wk": wkT, "wv": wvT, "wr": wrT, "wk2": wk2T,
              "wv2": wv2T, "cf": cf, "cb": cb, "bvf": bvf}
    x = np.asarray(ins["L0_attn_x"])
    in_maps = []
    for b in range(8):
        m = dict(common)
        m["x"] = np.ascontiguousarray(x[b])
        in_maps.append(m)
    return in_maps


def kernel(**inputs):
    nc, _ = build_program()
    in_maps = _pack_inputs(inputs)
    res = run_bass_kernel_spmd(nc, in_maps, core_ids=list(range(8)))
    out = np.stack([np.asarray(res.results[b]["out"]) for b in range(8)],
                   axis=0)
    return out


# version 35
# speedup vs baseline: 1.2044x; 1.0034x over previous
import numpy as np
import concourse.bass as bass
import concourse.mybir as mybir
from concourse.bass_utils import run_bass_kernel_spmd

F32 = mybir.dt.float32
BF16 = mybir.dt.bfloat16
AF = mybir.ActivationFunctionType
ALU = mybir.AluOpType
AX = mybir.AxisListType.X

S = 2048
E = 1024
H = 16
HD = 64
HALF = 1024
NT = 8
NCH = 8
EPS = 1e-5

CF_BQ = 0
CF_BK = 8
CF_MUR = 16
CF_MUK = 24
CF_EPS = 32
CF_ONE = 33
CF_W = 40

CB_I128 = 0
CB_MASK = 128
CB_I65 = 256
CB_ONES = 321
CB_BD = 452
CB_W = 580


def build_program(dbg=(), dbg_barrier=True):
    nc = bass.Bass("TRN2", target_bir_lowering=False)
    dbg = set(dbg)

    x_d = nc.dram_tensor("x", [S, E], F32, kind="ExternalInput").ap()
    out_d = nc.dram_tensor("out", [S, E], F32, kind="ExternalOutput").ap()
    w_d = {}
    for wn in ("wq", "wk", "wv", "wr", "wk2", "wv2"):
        w_d[wn] = nc.dram_tensor(wn, [E, E], F32, kind="ExternalInput").ap()
    cf_d = nc.dram_tensor("cf", [128, CF_W], F32, kind="ExternalInput").ap()
    cb_d = nc.dram_tensor("cb", [128, CB_W], F32, kind="ExternalInput").ap()
    bv_d = nc.dram_tensor("bvf", [1, E], F32, kind="ExternalInput").ap()

    dbg_d = {}

    def dbg_on(name, shape):
        if name in dbg:
            dbg_d[name] = nc.dram_tensor("dbg_" + name, shape, F32,
                                         kind="ExternalOutput").ap()
            return True
        return False

    import contextlib
    with contextlib.ExitStack() as _st:
        X = _st.enter_context(nc.sbuf_tensor("X", [128, NT, E], F32))
        KT = _st.enter_context(nc.sbuf_tensor("KT", [128, NCH, S], BF16))
        QT = _st.enter_context(nc.sbuf_tensor("QT", [128, NCH, HALF], BF16))
        VH = _st.enter_context(nc.sbuf_tensor("VH", [128, 16, H * 65], BF16))
        XN = _st.enter_context(nc.sbuf_tensor("XN", [128, NCH, 1040], BF16))
        AO = _st.enter_context(nc.sbuf_tensor("AO", [128, NT, E], BF16))
        W0 = _st.enter_context(nc.sbuf_tensor("W0", [128, NCH, E], BF16))
        W1 = _st.enter_context(nc.sbuf_tensor("W1", [128, NCH, E], BF16))
        W2 = _st.enter_context(nc.sbuf_tensor("W2", [128, NCH, E], BF16))
        PT = _st.enter_context(nc.sbuf_tensor("PT", [128, 3, 2, 512], BF16))
        OSB = _st.enter_context(nc.sbuf_tensor("OSB", [128, 2, 512], BF16))
        CF = _st.enter_context(nc.sbuf_tensor("CF", [128, CF_W], F32))
        CB = _st.enter_context(nc.sbuf_tensor("CB", [128, CB_W], BF16))
        STT = _st.enter_context(nc.sbuf_tensor("STT", [128, 128], F32))
        MH = _st.enter_context(nc.sbuf_tensor("MH", [128, 16], F32))
        MHP = _st.enter_context(nc.sbuf_tensor("MHP", [128, 8], F32))
        LNS = _st.enter_context(nc.sbuf_tensor("LNS", [128, 1024], BF16))
        BVB = _st.enter_context(nc.sbuf_tensor("BVB", [128, E], BF16))
        SMALL = _st.enter_context(nc.sbuf_tensor("SMALL", [128, 96], F32))
        NMB = _st.enter_context(nc.sbuf_tensor("NMB", [128, 16], BF16))
        CAR = _st.enter_context(nc.sbuf_tensor("CAR", [128, NCH, 1], BF16))
        RC = _st.enter_context(nc.sbuf_tensor("RC", [128, 8], F32))
        PSA = _st.enter_context(nc.psum_tensor("PSA", [128, 4, 512], F32))
        PSO = _st.enter_context(nc.psum_tensor("PSO", [128, 2, 512], F32))
        PST = _st.enter_context(nc.psum_tensor("PST", [128, 512], F32))
        PSS = _st.enter_context(nc.psum_tensor("PSS", [128, 512], F32))
        sLD = _st.enter_context(nc.semaphore("LD"))
        sWL = _st.enter_context(nc.semaphore("WL"))
        sPE = _st.enter_context(nc.semaphore("PE"))
        sACT = _st.enter_context(nc.semaphore("ACT"))
        sDVE = _st.enter_context(nc.semaphore("DVE"))
        sGP = _st.enter_context(nc.semaphore("GP"))
        sST = _st.enter_context(nc.semaphore("ST"))
        sXS = _st.enter_context(nc.semaphore("XS"))
        block = _st.enter_context(nc.Block())
        SEM = {"LD": sLD, "WL": sWL, "PE": sPE, "ACT": sACT, "DVE": sDVE,
               "GP": sGP, "ST": sST, "XS": sXS}
        cnt = {k: 0 for k in SEM}
        lastw = {}
        items = []
        cur_phase = ["init"]
        plan_labels = {}

        def em(eng, f):
            items.append((eng, f))

        def emlab(eng):
            plan_labels.setdefault(eng, []).append(cur_phase[0])

        def W(eng, sem, val):
            if val <= 0:
                return
            assert val <= cnt[sem], f"wait on future {sem} {val} > {cnt[sem]}"
            key = (eng, sem)
            if lastw.get(key, 0) >= val:
                return
            lastw[key] = val
            em(eng, lambda e, s=sem, v=val: e.wait_ge(SEM[s], v))

        def EMI(eng, f, inc=None, by=1):
            emlab(eng)
            if inc is not None:
                cnt[inc] += by
                em(eng, lambda e, ff=f, s=inc, b=by: ff(e).then_inc(SEM[s], b))
                return cnt[inc]
            em(eng, f)
            return None

        def dbg_bar():
            if not dbg_barrier:
                return
            for eng in ("vector", "scalar", "tensor", "sync"):
                W(eng, "ST", cnt["ST"])

        EMI("sync", lambda e: e.dma_start(out=CF[:], in_=cf_d), "LD", 16)
        ld_const = cnt["LD"]
        W("sync", "LD", cnt["LD"])
        EMI("sync", lambda e: e.dma_start(
            out=X[:, 0:4, :],
            in_=x_d[0:512, :].rearrange("(t p) f -> p t f", p=128)),
            "LD", 16)
        ld0a = cnt["LD"]
        W("sync", "LD", cnt["LD"])
        EMI("sync", lambda e: e.dma_start(
            out=X[:, 4:8, :],
            in_=x_d[512:HALF, :].rearrange("(t p) f -> p t f", p=128)),
            "LD", 16)
        ld_x = [[ld0a, cnt["LD"]], [None, None]]

        EMI("gpsimd", lambda e: e.dma_start(out=CB[:], in_=cb_d), "WL", 16)
        W("gpsimd", "WL", cnt["WL"])
        EMI("gpsimd", lambda e: e.dma_start(out=QT[0:1, 0, :], in_=bv_d),
            "WL", 16)
        wl_const = cnt["WL"]

        vh_ones = VH[:].rearrange("p g (h c) -> p g h c", c=65)
        EMI("gpsimd", lambda e: e.memset(vh_ones[:, :, :, 64:65], 1.0), "GP")
        EMI("gpsimd", lambda e: e.memset(CAR[:], 0.0), "GP")
        gp_init = cnt["GP"]

        wslot = {"wq": W0, "wk": W1, "wv": W2, "wr": W0, "wk2": W1,
                 "wv2": W2}
        wl_ready = {}

        def load_weight(wn, tag, wait_pe=0):
            if wait_pe:
                W("gpsimd", "PE", wait_pe)
            W("gpsimd", "WL", cnt["WL"])
            EMI("gpsimd", lambda e, w=wn: e.dma_start(
                out=wslot[w][:],
                in_=w_d[w].rearrange("(k p) m -> p k m", p=128)), "WL", 16)
            wl_ready[tag] = cnt["WL"]

        load_weight("wq", "wq0")
        load_weight("wk", "wk0")
        load_weight("wv", "wv0")

        W("tensor", "WL", wl_const)
        for n in range(2):
            pe = EMI("tensor", lambda e, n=n: e.matmul(
                PSS[:, 0:512], CB[0:1, CB_ONES:CB_ONES + 128],
                QT[0:1, 0, n * 512:(n + 1) * 512], start=True, stop=True),
                "PE")
            W("vector", "PE", pe)
            dv = EMI("vector", lambda e, n=n: e.tensor_copy(
                BVB[:, n * 512:(n + 1) * 512], PSS[:, 0:512]), "DVE")
            W("tensor", "DVE", dv)

        psa_evac = [(None, 0)] * 4
        psa_rr = [0]

        def psum_acc_group(n_k, lhsT_fn, rhs_fn, evac_fn, evac_eng,
                           multi=False, pre_k=None):
            b = psa_rr[0] % 4
            psa_rr[0] += 1
            gsem, gval = psa_evac[b]
            if gsem:
                W("tensor", gsem, gval)
            pe = None
            for k in range(n_k):
                if pre_k is not None:
                    pre_k(k)
                pe = EMI("tensor", lambda e, k=k, b=b: e.matmul(
                    PSA[:, b, :], lhsT_fn(k), rhs_fn(k),
                    start=(k == 0), stop=(k == n_k - 1)),
                    "PE" if k == n_k - 1 else None)
            W(evac_eng, "PE", pe)
            sem = "ACT" if evac_eng == "scalar" else "DVE"
            if multi:
                ev = evac_fn(PSA[:, b, :])
            else:
                ev = EMI(evac_eng, lambda e, b=b: evac_fn(e, PSA[:, b, :]),
                         sem)
            psa_evac[b] = (sem, ev)
            return ev

        tb_guard = {0: (None, 0), 1: (None, 0)}
        tb_banks = [PST, PSS]
        tb_rr = [0]

        def transpose_group(blocks, dest_fn, extra_pe_wait=()):
            bi = tb_rr[0] % 2
            tb_rr[0] += 1
            bank = tb_banks[bi]
            gsem, gval = tb_guard[bi]
            if gsem:
                W("tensor", gsem, gval)
            for sem, val in extra_pe_wait:
                W("tensor", sem, val)
            pe = None
            for i, blk in enumerate(blocks):
                pe = EMI("tensor", lambda e, i=i, blk=blk, bank=bank:
                         e.matmul(bank[:, i * 128:(i + 1) * 128], blk,
                                  CB[:, CB_I128:CB_I128 + 128],
                                  start=True, stop=True),
                         "PE" if i == len(blocks) - 1 else None)
            W("vector", "PE", pe)
            pv = bank[:].rearrange("p (i n) -> p i n", n=128)
            dv = EMI("vector", lambda e, pv=pv, m=len(blocks):
                     dest_fn(e, pv[:, 0:m, :]), "DVE")
            tb_guard[bi] = ("DVE", dv)
            return dv

        stat_ring_i = [0]

        def grab_stat_bank():
            k = stat_ring_i[0] % 6
            stat_ring_i[0] += 1
            if k < 4:
                gsem, gval = psa_evac[k]
                if gsem:
                    W("tensor", gsem, gval)
                bank = PSA[:, k, :]

                def done(sem, val, k=k):
                    psa_evac[k] = (sem, val)
            else:
                bi = k - 4
                gsem, gval = tb_guard[bi]
                if gsem:
                    W("tensor", gsem, gval)
                bank = tb_banks[bi]

                def done(sem, val, bi=bi):
                    tb_guard[bi] = (sem, val)
            return bank, done

        def grab_tb():
            bi = tb_rr[0] % 2
            tb_rr[0] += 1
            bank = tb_banks[bi]
            gsem, gval = tb_guard[bi]
            if gsem:
                W("tensor", gsem, gval)

            def done(sem, val, bi=bi):
                tb_guard[bi] = (sem, val)
            return bank, done

        osb_flat = OSB[:].rearrange("p a n -> p (a n)")
        sq_dummy = PT[:].rearrange("p a b n -> p (a b n)")[:, 0:E]
        ptflat = PT[:].rearrange("p a b n -> p (a b n)")

        def mcol(t):
            return STT[:, 96 + 2 * t:96 + 2 * t + 1]

        def rcol(t):
            return STT[:, 112 + t:112 + t + 1]

        def ln_stats(src_tiles, pre=(), base=0, pre_tile=None):
            nt = len(src_tiles)
            for sem, val in pre:
                W("vector", sem, val)
                W("scalar", sem, val)
            dst = None
            for i, ap in enumerate(src_tiles):
                if pre_tile is not None:
                    pre_tile(i)
                t = base + i
                for hlf in range(2):
                    dst = EMI("vector", lambda e, t=t, ap=ap, hlf=hlf:
                              e.bn_stats(
                                  STT[:, t * 12 + hlf * 6:t * 12 + hlf * 6
                                      + 6],
                                  ap[:, hlf * 512:(hlf + 1) * 512]),
                              "DVE" if (i == nt - 1 and hlf == 1) else None)
            W("vector", "DVE", dst)
            dag = None
            for i in range(nt):
                t = base + i
                dag = EMI("vector", lambda e, t=t: e.bn_aggr(
                    STT[:, 96 + 2 * t:96 + 2 * t + 2],
                    STT[:, t * 12:t * 12 + 12]),
                    "DVE" if i == nt - 1 else None)
            W("scalar", "DVE", dag)
            var_view = STT[:, 96 + 2 * base:96 + 2 * (base + nt)].rearrange(
                "p (t two) -> p t two", two=2)[:, :, 1:2]
            a1 = EMI("scalar", lambda e, vv=var_view, nt=nt, base=base:
                     e.activation(
                         STT[:, 120 + base:120 + base + nt], vv, AF.Ln,
                         bias=CF[:, CF_EPS:CF_EPS + 1]), "ACT")
            W("scalar", "ACT", a1)
            ac = EMI("scalar", lambda e, nt=nt, base=base: e.activation(
                STT[:, 112 + base:112 + base + nt],
                STT[:, 120 + base:120 + base + nt], AF.Exp,
                scale=-0.5), "ACT")
            return ac

        out_sts = []
        preA = {}
        preA_xn = {}
        for h in range(2):
            T0 = h * HALF
            W("vector", "LD", ld_x[h][0])
            W("scalar", "LD", ld_x[h][0])
            W("vector", "LD", ld_const)
            W("scalar", "LD", ld_const)

            if dbg_on("xload%d" % h, [128, NT, E]):
                W("gpsimd", "LD", ld_x[h][1])
                EMI("gpsimd", lambda e, h=h: e.dma_start(
                    out=dbg_d["xload%d" % h], in_=X[:]), "ST", 16)
                dbg_bar()
            cur_phase[0] = "A_ln1_h%d" % h
            rstd_b = {}
            xn_r = {}
            osb_free_pe = [0]

            def ln1_batch(batch):
                for t in range(batch * 4, batch * 4 + 4):
                    W("vector", "ACT", rstd_b[batch])
                    W("vector", "PE", osb_free_pe[0])
                    W("vector", "DVE", cnt["DVE"])
                    W("vector", "XS", cnt["XS"])
                    dvn = EMI("vector", lambda e, t=t: e.tensor_scalar(
                        osb_flat, X[:, t, :], mcol(t), rcol(t),
                        ALU.subtract, ALU.mult), "DVE")
                    for g in range(2):
                        c0 = g * 4
                        transpose_group(
                            [osb_flat[:, (c0 + c) * 128:(c0 + c + 1) * 128]
                             for c in range(4)],
                            lambda e, pv, t=t, c0=c0: e.tensor_copy(
                                XN[:, c0:c0 + 4, t * 128:(t + 1) * 128], pv),
                            extra_pe_wait=(("DVE", dvn), ("WL", wl_const)))
                    osb_free_pe[0] = cnt["PE"]
                xn_r[batch] = cnt["DVE"]

            if h in preA_xn:
                xn_r[0] = preA_xn[h]
            else:
                rstd_b[0] = ln_stats([X[:, t, :] for t in range(4)], base=0)
                ln1_batch(0)

            if dbg_on("xn%d" % h, [128, NCH, HALF]):
                W("gpsimd", "DVE", xn_ready)
                EMI("gpsimd", lambda e, h=h: e.dma_start(
                    out=dbg_d["xn%d" % h], in_=XN[:, :, 0:HALF]), "ST", 16)
                dbg_bar()

            sq_guard = [0]

            def plan_chunk_sq(src, evac_ac):
                W("gpsimd", "ACT", evac_ac)
                W("gpsimd", "PE", sq_guard[0])
                return EMI("gpsimd", lambda e, src=src: e.tensor_tensor(
                    ptflat[:, 0:HALF], src, src, ALU.mult), "GP")

            def plan_chunk_stats(sqv, out_col0, pi, kmax_combine):
                col = out_col0 + pi
                for n in range(2):
                    bank, done = grab_tb()
                    W("tensor", "GP", sqv)
                    pe = EMI("tensor", lambda e, n=n, bank=bank: e.matmul(
                        bank[:, 0:512],
                        CB[:, CB_BD:CB_BD + 128],
                        ptflat[:, n * 512:(n + 1) * 512],
                        start=True, stop=True), "PE")
                    sq_guard[0] = pe
                    W("vector", "PE", pe)
                    dvl = EMI("vector", lambda e, bank=bank, n=n:
                              e.reduce_max(
                                  SMALL[:, 48 + n:49 + n],
                                  bank[:, 0:512], axis=AX), "DVE")
                    done("DVE", dvl)
                W("vector", "DVE", dvl)
                if kmax_combine:
                    dvl = EMI("vector", lambda e: e.reduce_max(
                        SMALL[:, 52:53], SMALL[:, 48:50], axis=AX), "DVE")
                    W("vector", "DVE", dvl)
                    dvl = EMI("vector", lambda e, col=col: e.tensor_tensor(
                        SMALL[:, 8 + col:8 + col + 1],
                        SMALL[:, col:col + 1],
                        SMALL[:, 52:53], ALU.max), "DVE")
                    W("vector", "DVE", dvl)
                else:
                    dvl = EMI("vector", lambda e, col=col: e.reduce_max(
                        SMALL[:, col:col + 1],
                        SMALL[:, 48:50], axis=AX), "DVE")
                    W("vector", "DVE", dvl)

            q_evac = {}
            k_evac = {}
            stats_pending = []

            def qkv_n(n, wt, evac_fn_mk, tag_evacs):
                W("tensor", "DVE", xn_r[n])
                for m in range(NCH):
                    ev = psum_acc_group(
                        NCH,
                        lambda k, m=m, wt=wt: wt[:, k,
                                                 m * 128:(m + 1) * 128],
                        lambda k, n=n: XN[:, k, n * 512:(n + 1) * 512],
                        evac_fn_mk(m, n), "scalar")
                    if n == 1:
                        tag_evacs[m] = ev
                    if n == 1 and stats_pending:
                        stats_pending.pop(0)()

            def q_evac_mk(m, n):
                return lambda e, ps, m=m, n=n: e.activation(
                    QT[:, m, n * 512:(n + 1) * 512], ps, AF.Identity,
                    bias=CF[:, CF_BQ + m:CF_BQ + m + 1])

            def k_evac_mk(m, n):
                return lambda e, ps, m=m, n=n, T0=T0: e.activation(
                    KT[:, m, T0 + n * 512:T0 + (n + 1) * 512], ps,
                    AF.Identity, bias=CF[:, CF_BK + m:CF_BK + m + 1])

            def v_tile(t):
                g = h * NT + t
                dst = VH[:, g, :].rearrange("p (hh c) -> p hh c", c=65)
                W("tensor", "DVE", xn_r[t // 4])
                for n in range(2):
                    psum_acc_group(
                        NCH,
                        lambda k, t=t: XN[:, k, t * 128:(t + 1) * 128],
                        lambda k, n=n: W2[:, k, n * 512:(n + 1) * 512],
                        lambda e, ps, dst=dst, n=n: e.activation(
                            dst[:, n * 8:(n + 1) * 8, 0:64],
                            ps.rearrange("p (hh c) -> p hh c", c=64),
                            AF.Identity),
                        "scalar")
                if stats_pending:
                    stats_pending.pop(0)()

            cur_phase[0] = "A2_pass1_h%d" % h
            W("tensor", "WL", wl_ready["wq0" if h == 0 else "wq1"])
            qkv_n(0, W0, q_evac_mk, q_evac)
            W("tensor", "WL", wl_ready["wk0" if h == 0 else "wk1"])
            qkv_n(0, W1, k_evac_mk, k_evac)
            W("tensor", "WL", wl_ready["wv0" if h == 0 else "wv1"])
            W("tensor", "GP", gp_init)
            for t in range(4):
                v_tile(t)
            cur_phase[0] = "A_ln1b1_h%d" % h
            rstd_b[1] = ln_stats([X[:, t, :] for t in range(4, 8)],
                                 base=4, pre=(("LD", ld_x[h][1]),))
            ln1_batch(1)
            cur_phase[0] = "A2_pass2_h%d" % h
            qkv_n(1, W0, q_evac_mk, q_evac)
            q_done = cnt["PE"]
            load_weight("wr", "wr%d" % h, wait_pe=q_done)

            def two_stage(src_fn, col0, pi, comb):
                holder = {}

                def stage_sq(pi=pi):
                    holder["sqv"] = plan_chunk_sq(src_fn(pi),
                                                  holder["evac"])

                def stage_mm(pi=pi):
                    plan_chunk_stats(holder["sqv"], col0, pi, comb)
                return holder, stage_sq, stage_mm

            q_stage = []
            for pi in range(NCH):
                hld, s1f, s2f = two_stage(
                    lambda pi: QT[:, pi, :], 16, pi, False)
                hld["evac"] = q_evac[pi]
                q_stage.append((s1f, s2f))
            for s1f, s2f in q_stage:
                stats_pending.append(s1f)
                stats_pending.append(s2f)
            qkv_n(1, W1, k_evac_mk, k_evac)
            k_done = cnt["PE"]
            load_weight("wk2", "wk2%d" % h, wait_pe=k_done)
            k_stage = []
            for pi in range(NCH):
                hld, s1f, s2f = two_stage(
                    lambda pi, T0=T0: KT[:, pi, T0:T0 + HALF], 0, pi,
                    h == 1)
                hld["evac"] = k_evac[pi]
                k_stage.append((s1f, s2f))
            for s1f, s2f in k_stage:
                stats_pending.append(s1f)
                stats_pending.append(s2f)
            for t in range(4, NT):
                v_tile(t)
            while stats_pending:
                stats_pending.pop(0)()
            v_done = cnt["PE"]
            load_weight("wv2", "wv2%d" % h, wait_pe=v_done)

            cur_phase[0] = "NM_h%d" % h
            kb = 0 if h == 0 else 8
            dvq = EMI("vector", lambda e, kb=kb: e.tensor_tensor(
                SMALL[:, 32:40], SMALL[:, 16:24],
                SMALL[:, kb:kb + 8], ALU.mult), "DVE")
            W("scalar", "DVE", dvq)
            a1 = EMI("scalar", lambda e: e.activation(
                MH[:, 0:8], SMALL[:, 32:40], AF.Ln,
                bias=CF[:, CF_EPS:CF_EPS + 1]), "ACT")
            W("scalar", "ACT", a1)
            ac = EMI("scalar", lambda e: e.activation(
                MH[:, 8:16], MH[:, 0:8], AF.Exp, scale=0.5), "ACT")
            W("vector", "ACT", ac)
            dv = EMI("vector", lambda e: e.tensor_scalar(
                NMB[:, 0:8], MH[:, 8:16], -0.125, None, ALU.mult),
                "DVE")
            bank, done = grab_tb()
            W("tensor", "DVE", dv)
            EMI("tensor", lambda e, bank=bank: e.matmul(
                bank[:, 0:8], CB[0:1, CB_ONES:CB_ONES + 128],
                NMB[0:1, 0:8], start=True, stop=True), "PE")
            pe = EMI("tensor", lambda e, bank=bank: e.matmul(
                bank[:, 8:16], CB[64:65, CB_ONES:CB_ONES + 128],
                NMB[64:65, 0:8], start=True, stop=True,
                tile_position=(64, 0)), "PE")
            W("vector", "PE", pe)
            dcp = EMI("vector", lambda e, bank=bank: e.tensor_copy(
                MH[:, 0:8], bank[:, 0:8]), "DVE")
            W("vector", "DVE", dcp)
            mh_ready = EMI("vector", lambda e, bank=bank: e.tensor_tensor(
                MHP[:, 0:8], MH[:, 0:8], bank[:, 8:16], ALU.min), "DVE")
            done("DVE", mh_ready)

            if dbg_on("mh%d" % h, [128, 16]):
                W("gpsimd", "DVE", mh_ready)
                EMI("gpsimd", lambda e, h=h: e.dma_start(
                    out=dbg_d["mh%d" % h], in_=MH[:]), "ST", 16)
                dbg_bar()

            W("tensor", "ACT", cnt["ACT"])
            W("tensor", "DVE", cnt["DVE"])
            exp_of = {}
            pv_of = {}
            pt_guard = {}
            last_o_ac = [0]
            c_gp_last = [0]
            lns_flat = LNS[:]

            def jmax_of(sg):
                return (T0 + sg * 512 + 511) // 128

            def jparams(sg, j):
                seg_lo = T0 + sg * 512
                seg_hi = seg_lo + 512
                qstart = seg_lo if j * 128 < seg_lo else j * 128
                N = min(seg_hi - qstart, 512)
                diag = j * 128 >= seg_lo
                return qstart - T0, qstart - seg_lo, N, diag

            def emit_scores(sg, pi, j):
                cur_phase[0] = "attn_h%d_s%d_p%d" % (h, sg, pi)
                par = j % 2
                qoff, soff, N, diag = jparams(sg, j)
                if (par, 1) in exp_of:
                    W("tensor", "ACT", exp_of[(par, 1)])
                sc = None
                for head in range(2):
                    b = 2 * par + head
                    base = head * 64
                    sc = EMI("tensor", lambda e, b=b, base=base,
                             pi=pi, j=j, qoff=qoff, N=N: e.matmul(
                                 PSA[:, b, 0:N],
                                 KT[base:base + 64, pi,
                                    j * 128:(j + 1) * 128],
                                 QT[base:base + 64, pi,
                                    qoff:qoff + N],
                                 start=True, stop=True,
                                 tile_position=(base, 0)),
                             "PE" if head == 1 else None)
                return sc

            def emit_exp(sg, pi, j, sc_pe):
                par = j % 2
                slot = 2 if j == 0 else par
                qoff, soff, N, diag = jparams(sg, j)
                if slot in pt_guard:
                    W("scalar", "PE", pt_guard[slot])
                W("scalar", "PE", sc_pe)
                W("scalar", "DVE", mh_ready)
                EMI("scalar", lambda e, slot=slot, pi=pi, par=par, N=N:
                    e.activation(
                        PT[:, slot, :, 0:N],
                        PSA[:, 2 * par:2 * par + 2, 0:N],
                        AF.Exp, bias=MHP[:, pi:pi + 1],
                        scale=0.125), "ACT")
                exp_of[(par, 1)] = cnt["ACT"]
                ev = cnt["ACT"]
                pv_wait = [("ACT", ev)]
                if diag:
                    W("vector", "ACT", ev)
                    dvm = None
                    for head in range(2):
                        dvm = EMI("vector", lambda e, slot=slot,
                                  head=head: e.tensor_tensor(
                                      PT[:, slot, head, 0:128],
                                      PT[:, slot, head, 0:128],
                                      CB[:, CB_MASK:CB_MASK + 128],
                                      ALU.mult),
                                  "DVE" if head == 1 else None)
                    pv_wait.append(("DVE", dvm))
                return pv_wait

            def emit_pv(sg, pi, j, pv_wait):
                par = j % 2
                slot = 2 if j == 0 else par
                qoff, soff, N, diag = jparams(sg, j)
                jm = jmax_of(sg)
                if j == 0:
                    W("tensor", "DVE", last_o_ac[0])
                for sem, val in pv_wait:
                    W("tensor", sem, val)
                for head in range(2):
                    hh = pi * 2 + head
                    EMI("tensor", lambda e, head=head, hh=hh, j=j,
                        slot=slot, soff=soff, N=N, jm=jm:
                        e.matmul(
                            PSO[0:65, head, soff:soff + N],
                            VH[:, j, hh * 65:hh * 65 + 65],
                            PT[:, slot, head, 0:N],
                            start=(j == 0), stop=(j == jm)),
                        "PE" if head == 1 else None)
                pv_of[par] = cnt["PE"]
                pt_guard[slot] = cnt["PE"]

            def o_evac(sg, pi):
                cur_phase[0] = "ofin_h%d_s%d_p%d" % (h, sg, pi)
                W("vector", "PE", pv_of[0])
                W("vector", "PE", pv_of[1])
                for head in range(2):
                    EMI("vector", lambda e, head=head: e.tensor_copy(
                        OSB[0:65, head, :], PSO[0:65, head, :]),
                        "DVE" if head == 1 else None)
                o_dv = EMI("vector", lambda e: e.tensor_copy(
                    RC[:, 4:5], RC[:, 0:1]), "DVE")
                last_o_ac[0] = o_dv

            def o_transp(sg, pi):
                cur_phase[0] = "otr_h%d_s%d_p%d" % (h, sg, pi)
                o_ac = last_o_ac[0]
                for head in range(2):
                    hh = pi * 2 + head
                    bank, done = grab_tb()
                    W("tensor", "DVE", o_ac)
                    pe = None
                    for qt in range(4):
                        pe = EMI("tensor", lambda e, head=head, qt=qt,
                                 bank=bank: e.matmul(
                                     bank[:, qt * 128:qt * 128 + 65],
                                     OSB[0:65, head,
                                         qt * 128:(qt + 1) * 128],
                                     CB[0:65, CB_I65:CB_I65 + 65],
                                     start=True, stop=True),
                                 "PE" if qt == 3 else None)
                    W("vector", "PE", pe)
                    W("vector", "DVE", cnt["DVE"])
                    bv = bank[:].rearrange("p (i n) -> p i n", n=128)
                    dvr = EMI("vector", lambda e, bv=bv: e.reciprocal(
                        RC[:, 0:4], bv[:, 0:4, 64:65]), "DVE")
                    W("vector", "DVE", dvr)
                    dvo = None
                    for qt in range(4):
                        tt = sg * 4 + qt
                        dvo = EMI("vector", lambda e, bv=bv, qt=qt,
                                  tt=tt, hh=hh: e.scalar_tensor_tensor(
                                      AO[:, tt, hh * 64:hh * 64 + 64],
                                      bv[:, qt, 0:64],
                                      RC[:, qt:qt + 1],
                                      BVB[:, hh * 64:hh * 64 + 64],
                                      ALU.mult, ALU.add),
                                  "DVE" if qt == 3 else None)
                    done("DVE", dvo)

            def plan_C(qq, pre_dve):
                cur_phase[0] = "C_h%d_q%d" % (h, qq)
                tiles = [qq * 4 + i for i in range(4)]
                rstd_a = ln_stats([AO[:, t, :] for t in tiles],
                                  pre=(("DVE", pre_dve),))
                yield
                two_buf = (qq == 1)
                gp_adds = []
                for i, t in enumerate(tiles):
                    scr = osb_flat if (two_buf and i % 2 == 1) else lns_flat
                    W("vector", "ACT", rstd_a)
                    W("vector", "DVE", cnt["DVE"])
                    gi = i - 2 if two_buf else i - 1
                    if gi >= 0:
                        W("vector", "GP", gp_adds[gi])
                    dts = EMI("vector", lambda e, t=t, i=i, scr=scr:
                              e.tensor_scalar(
                                  scr, AO[:, t, :], mcol(i), rcol(i),
                                  ALU.subtract, ALU.mult), "DVE")
                    W("gpsimd", "DVE", dts)
                    gp_adds.append(EMI("gpsimd", lambda e, t=t, scr=scr:
                                       e.tensor_tensor(
                                           X[:, t, :], X[:, t, :],
                                           scr, ALU.add), "GP"))
                    if i == 1:
                        yield
                yield
                c_gp_last[0] = gp_adds[-1]
                rstd_2 = ln_stats([X[:, t, :] for t in tiles],
                                  pre_tile=lambda i: W("vector", "GP",
                                                       gp_adds[i]))
                if qq == 0:
                    W("vector", "GP", gp_init)
                    EMI("vector", lambda e: e.tensor_copy(
                        XN[:, :, 0:1], CAR[:]))
                else:
                    W("vector", "DVE", cnt["DVE"])
                    EMI("vector", lambda e: e.tensor_copy(
                        XN[:, :, 520:521], XN[:, :, 512:513]))
                yield
                osb_free = 0
                for i, t in enumerate(tiles):
                    W("vector", "ACT", rstd_2)
                    W("vector", "PE", osb_free)
                    dvn = EMI("vector", lambda e, t=t, i=i: e.tensor_scalar(
                        lns_flat, X[:, t, :], mcol(i), rcol(i),
                        ALU.subtract, ALU.mult), "DVE")
                    for g2 in range(2):
                        c0 = g2 * 4
                        transpose_group(
                            [lns_flat[:, (c0 + c) * 128:(c0 + c + 1) * 128]
                             for c in range(4)],
                            lambda e, pv, i=i, c0=c0, qq=qq: e.tensor_copy(
                                XN[:, c0:c0 + 4,
                                   qq * 520 + 1 + i * 128:
                                   qq * 520 + 1 + (i + 1) * 128], pv),
                            extra_pe_wait=(("DVE", dvn),))
                    osb_free = cnt["PE"]
                    yield
                if qq == 1:
                    W("vector", "DVE", cnt["DVE"])
                    EMI("vector", lambda e: e.tensor_copy(
                        CAR[:], XN[:, :, 1032:1033]))

            aof = AO[:].rearrange("p t f -> p (t f)")
            qtf = QT[:].rearrange("p c n -> p (c n)")
            Avs = [qtf[:, c * 512:(c + 1) * 512] for c in range(NCH)]
            Bvs = [qtf[:, 4096 + c * 512:4096 + (c + 1) * 512]
                   for c in range(NCH)]
            Cvs = [aof[:, c * 512:(c + 1) * 512] for c in range(NCH)]
            XKs = [aof[:, 4096 + c * 512:4096 + (c + 1) * 512]
                   for c in range(NCH)]
            OTs = [aof[:, 4096 + tl * 1024:4096 + (tl + 1) * 1024]
                   for tl in range(4)]

            def plan_mix_d(qq, pe_wait, dve_wait, on_gp):
                cur_phase[0] = "mixd_h%d_q%d" % (h, qq)
                x2 = [XN[:, c, qq * 520:qq * 520 + 513] for c in range(NCH)]
                eng, sem = ("gpsimd", "GP") if on_gp else ("vector", "DVE")
                if pe_wait:
                    W(eng, "PE", pe_wait)
                if on_gp and dve_wait:
                    W("gpsimd", "DVE", dve_wait)
                vals = []
                for c in range(NCH):
                    vals.append(EMI(eng, lambda e, c=c, xc=x2[c]:
                                    e.tensor_tensor(
                                        Cvs[c], xc[:, 1:513], xc[:, 0:512],
                                        ALU.subtract), sem))
                return vals

            def plan_mix_xr(qq, pe_wait, d_vals, d_on_gp):
                cur_phase[0] = "mixr_h%d_q%d" % (h, qq)
                x2 = [XN[:, c, qq * 520:qq * 520 + 513] for c in range(NCH)]
                if pe_wait:
                    W("vector", "PE", pe_wait)
                xr_vals = []
                for c in range(NCH):
                    if d_on_gp:
                        W("vector", "GP", d_vals[c])
                    xr_vals.append(EMI("vector", lambda e, c=c, xc=x2[c]:
                                       e.scalar_tensor_tensor(
                                           Avs[c], Cvs[c],
                                           CF[:, CF_MUR + c:CF_MUR + c + 1],
                                           xc[:, 0:512],
                                           ALU.mult, ALU.add), "DVE"))
                return xr_vals

            xr_gp = {}

            def mix0():
                cur_phase[0] = "mixd_h%d_q0" % h
                x2 = [XN[:, c, 0:513] for c in range(NCH)]
                W("vector", "PE", cnt["PE"])
                vals = []
                for c in range(NCH):
                    EMI("vector", lambda e, c=c, xc=x2[c]:
                        e.tensor_tensor(
                            Cvs[c], xc[:, 1:513], xc[:, 0:512],
                            ALU.subtract), None)
                    vals.append(EMI("vector", lambda e, c=c, xc=x2[c]:
                                    e.scalar_tensor_tensor(
                                        Avs[c], Cvs[c],
                                        CF[:, CF_MUR + c:CF_MUR + c + 1],
                                        xc[:, 0:512],
                                        ALU.mult, ALU.add), "DVE"))
                xr_gp[0] = vals

            mix_hook = [mix0]
            cgen = None
            units = [(sg, pi, j) for sg in range(2) for pi in range(NCH)
                     for j in range(jmax_of(sg) + 1)]
            nu = len(units)
            sc_of = {}
            pvw = {}

            def ensure_sc(ui):
                if ui < nu and ui not in sc_of:
                    sc_of[ui] = emit_scores(*units[ui])

            def ensure_exp(ui):
                if ui < nu and ui not in pvw:
                    ensure_sc(ui)
                    pvw[ui] = emit_exp(*units[ui], sc_of[ui])

            ensure_sc(0)
            ensure_sc(1)
            ensure_exp(0)
            pending_transp = [None]
            for ui in range(nu):
                sg, pi, j = units[ui]
                ensure_exp(ui + 1)
                ensure_sc(ui + 2)
                emit_pv(sg, pi, j, pvw[ui])
                if pending_transp[0] is not None:
                    pending_transp[0]()
                    pending_transp[0] = None
                if j == jmax_of(sg):
                    o_evac(sg, pi)
                    pending_transp[0] = (
                        lambda sg=sg, pi=pi: o_transp(sg, pi))
                    if sg == 1 and pi == 0:
                        cgen = plan_C(0, cnt["DVE"])
                    if sg == 1 and pi == NCH - 1:
                        if cgen is not None:
                            for _ in cgen:
                                pass
                            cgen = None
                        if mix_hook[0] is not None:
                            mix_hook[0]()
                            mix_hook[0] = None
                    ensure_exp(ui + 1)
                    ensure_exp(ui + 2)
                elif j == 1 and cgen is not None:
                    next(cgen, None)
            if pending_transp[0] is not None:
                pending_transp[0]()
                pending_transp[0] = None
            if cgen is not None:
                for _ in cgen:
                    pass
            attn_pe_done = cnt["PE"]
            seg1_done = cnt["DVE"]

            cgen1 = plan_C(1, seg1_done)

            if dbg_on("ao%d" % h, [128, NT, E]):
                W("gpsimd", "DVE", cnt["DVE"])
                EMI("gpsimd", lambda e, h=h: e.dma_start(
                    out=dbg_d["ao%d" % h], in_=AO[:]), "ST", 16)
                dbg_bar()

            ld_x1_tile = {}
            for qq in range(2):
                x2 = [XN[:, c, qq * 520:qq * 520 + 513] for c in range(NCH)]
                cur_phase[0] = "ffn_r_h%d_q%d" % (h, qq)

                def emit_xk(x2=x2, qq=qq):
                    cur_phase[0] = "mixk_h%d_q%d" % (h, qq)
                    xk = None
                    for c in range(NCH):
                        xk = EMI("vector", lambda e, c=c, xc=x2[c]:
                                 e.scalar_tensor_tensor(
                                     XKs[c], Cvs[c],
                                     CF[:, CF_MUK + c:CF_MUK + c + 1],
                                     xc[:, 0:512], ALU.mult, ALU.add),
                                 "DVE" if c == NCH - 1 else None)
                    return xk

                xk_dve = None
                W("tensor", "WL", wl_ready["wr%d" % h])
                for m in range(NCH):
                    def evac_r(ps, m=m):
                        s1 = PT[:, 0, m % 2, :]
                        s2 = PT[:, 1, m % 2, :]
                        a1 = EMI("scalar", lambda e, ps=ps, s1=s1:
                                 e.activation(s1, ps, AF.Exp, scale=-1.0),
                                 "ACT")
                        W("scalar", "ACT", a1)
                        a2 = EMI("scalar", lambda e, s1=s1, s2=s2:
                                 e.activation(s2, s1, AF.Ln,
                                              bias=CF[:, CF_ONE:CF_ONE + 1]),
                                 "ACT")
                        W("scalar", "ACT", a2)
                        return EMI("scalar", lambda e, s2=s2, m=m:
                                   e.activation(Bvs[m], s2, AF.Exp,
                                                scale=-1.0), "ACT")
                    r_ac = psum_acc_group(
                        NCH,
                        lambda k, m=m: W0[:, k, m * 128:(m + 1) * 128],
                        lambda k: Avs[k],
                        evac_r, "scalar", multi=True,
                        pre_k=lambda k, qq=qq: W("tensor", "DVE",
                                                 xr_gp[qq][k]))
                    if cgen1 is not None:
                        next(cgen1, None)
                        if m == 2:
                            xk_dve = emit_xk()
                r_pe = cnt["PE"]
                if cgen1 is not None:
                    for _ in cgen1:
                        pass
                    cgen1 = None
                    xn_q1_dve = cnt["DVE"]
                if xk_dve is None:
                    xk_dve = emit_xk()
                cur_phase[0] = "ffn_k_h%d_q%d" % (h, qq)
                W("tensor", "WL", wl_ready["wk2%d" % h])
                W("tensor", "DVE", xk_dve)
                preA_tile = [None]
                if h == 0 and qq == 1:
                    cur_phase[0] = "preA_h0"
                    preA[1] = ln_stats([X[:, t, :] for t in range(4)],
                                       base=0,
                                       pre_tile=lambda i: W(
                                           "vector", "LD", ld_x1_tile[i]))
                    ofp = [0]

                    def preA_tile_fn(t):
                        cur_phase[0] = "preA_h0"
                        W("vector", "GP", c_gp_last[0])
                        W("vector", "ACT", preA[1])
                        W("vector", "PE", ofp[0])
                        dvn = EMI("vector", lambda e, t=t: e.tensor_scalar(
                            lns_flat, X[:, t, :], mcol(t), rcol(t),
                            ALU.subtract, ALU.mult), "DVE")
                        for g in range(2):
                            c0 = g * 4
                            transpose_group(
                                [lns_flat[:, (c0 + c) * 128:
                                          (c0 + c + 1) * 128]
                                 for c in range(4)],
                                lambda e, pv, t=t, c0=c0: e.tensor_copy(
                                    XN[:, c0:c0 + 4,
                                       t * 128:(t + 1) * 128], pv),
                                extra_pe_wait=(("DVE", dvn),))
                        ofp[0] = cnt["PE"]
                        if t == 3:
                            preA_xn[1] = cnt["DVE"]
                    preA_tile[0] = preA_tile_fn
                for m in range(NCH):
                    def evac_k(ps, m=m):
                        s1 = PT[:, 0, m % 2, :]
                        a1 = EMI("scalar", lambda e, ps=ps, s1=s1:
                                 e.activation(s1, ps, AF.Relu), "ACT")
                        W("scalar", "ACT", a1)
                        return EMI("scalar", lambda e, s1=s1, m=m:
                                   e.activation(Cvs[m], s1, AF.Square),
                                   "ACT")
                    psum_acc_group(
                        NCH,
                        lambda k, m=m: W1[:, k, m * 128:(m + 1) * 128],
                        lambda k: XKs[k],
                        evac_k, "scalar", multi=True)
                    if m % 2 == 1 and preA_tile[0] is not None:
                        preA_tile[0](m // 2)
                        cur_phase[0] = "ffn_k_h%d_q%d" % (h, qq)
                k2_pe = cnt["PE"]
                cur_phase[0] = "ffn_v_h%d_q%d" % (h, qq)
                W("tensor", "WL", wl_ready["wv2%d" % h])
                W("tensor", "ACT", cnt["ACT"])
                W("vector", "ACT", r_ac)
                for m in range(NCH):
                    psum_acc_group(
                        NCH,
                        lambda k, m=m: W2[:, k, m * 128:(m + 1) * 128],
                        lambda k: Cvs[k],
                        lambda e, ps, m=m: e.tensor_tensor(
                            Avs[m], ps, Bvs[m], ALU.mult),
                        "vector")
                v2_pe = cnt["PE"]
                outT_dve = cnt["DVE"]
                for tl in range(4):
                    for g2 in range(2):
                        c0 = g2 * 4
                        transpose_group(
                            [Avs[c0 + c][:, tl * 128:(tl + 1) * 128]
                             for c in range(4)],
                            lambda e, pv, tl=tl, c0=c0: e.tensor_copy(
                                OTs[tl].rearrange("p (c n) -> p c n",
                                                  n=128)[:, c0:c0 + 4, :],
                                pv),
                            extra_pe_wait=(("DVE", outT_dve),))
                outtok_dve = cnt["DVE"]
                if qq == 0:
                    d1_vals = plan_mix_d(1, cnt["PE"], xn_q1_dve, True)
                cur_phase[0] = "ln3_h%d_q%d" % (h, qq)
                tiles = [qq * 4 + i for i in range(4)]
                rstd_ac = ln_stats(OTs, pre=(("DVE", outtok_dve),), base=4)
                W("vector", "GP", c_gp_last[0])
                dvf = None
                for i, t in enumerate(tiles):
                    W("vector", "ACT", rstd_ac)
                    W("vector", "DVE", cnt["DVE"])
                    W("vector", "XS", cnt["XS"])
                    dts = EMI("vector", lambda e, i=i: e.tensor_scalar(
                        osb_flat, OTs[i], mcol(4 + i), rcol(4 + i),
                        ALU.subtract, ALU.mult), "DVE")
                    W("vector", "DVE", dts)
                    dvf = EMI("vector", lambda e, t=t: e.tensor_tensor(
                        X[:, t, :], X[:, t, :], osb_flat, ALU.add), "XS")
                    W("sync", "XS", dvf)
                    EMI("sync", lambda e, h=h, T0=T0, t=t: e.dma_start(
                        out=out_d[T0 + t * 128:T0 + (t + 1) * 128,
                                  :].rearrange("(a p) f -> p a f", p=128),
                        in_=X[:, t, :]), "ST", 16)
                    st_q = cnt["ST"]
                    if h == 0:
                        ldq = "gpsimd" if qq == 0 else "sync"
                        W(ldq, "ST", st_q)
                        W(ldq, "LD", cnt["LD"])
                        EMI(ldq, lambda e, t=t: e.dma_start(
                            out=X[:, t, :],
                            in_=x_d[HALF + t * 128:HALF + (t + 1) * 128,
                                    :].rearrange("(a p) f -> p a f", p=128)),
                            "LD", 16)
                        ld_x1_tile[t] = cnt["LD"]
                    if i == 0 and qq == 0:
                        xr_gp[1] = plan_mix_xr(1, cnt["PE"], d1_vals, True)
                        cur_phase[0] = "ln3_h%d_q%d" % (h, qq)
                if h == 0:
                    ld_x[1][qq] = cnt["LD"]

            half_done_dve = cnt["DVE"]
            if dbg_on("xf%d" % h, [128, NT, E]):
                W("gpsimd", "DVE", half_done_dve)
                EMI("gpsimd", lambda e, h=h: e.dma_start(
                    out=dbg_d["xf%d" % h], in_=X[:]), "ST", 16)
                dbg_bar()
            if h == 0:
                load_weight("wq", "wq1", wait_pe=r_pe)
                load_weight("wk", "wk1", wait_pe=k2_pe)
                load_weight("wv", "wv1", wait_pe=v2_pe)

        W("sync", "ST", cnt["ST"])

        def runner(name):
            def go(eng):
                for enm, f in items:
                    if enm == name:
                        f(eng)
            return go

        block.sync(runner("sync"))
        block.gpsimd(runner("gpsimd"))
        block.tensor(runner("tensor"))
        block.vector(runner("vector"))
        block.scalar(runner("scalar"))

    build_program.labels = plan_labels
    return nc, sorted(dbg_d.keys())


def _pack_inputs(ins):
    wqT = np.ascontiguousarray(
        np.asarray(ins["L0_attn_Wq"]).transpose(2, 0, 1).reshape(E, E))
    wkT = np.ascontiguousarray(
        np.asarray(ins["L0_attn_Wk"]).transpose(2, 0, 1).reshape(E, E))
    wvT = np.ascontiguousarray(
        np.asarray(ins["L0_attn_Wv"]).transpose(2, 0, 1).reshape(E, E))
    wrT = np.ascontiguousarray(np.asarray(ins["L0_ffn_W_r"]).T)
    wk2T = np.ascontiguousarray(np.asarray(ins["L0_ffn_W_k"]).T)
    wv2T = np.ascontiguousarray(np.asarray(ins["L0_ffn_W_v"]).T)

    cf = np.zeros((128, CF_W), dtype=np.float32)
    cf[:, CF_BQ:CF_BQ + 8] = np.asarray(
        ins["L0_attn_bq"], dtype=np.float32).reshape(8, 128).T
    cf[:, CF_BK:CF_BK + 8] = np.asarray(
        ins["L0_attn_bk"], dtype=np.float32).reshape(8, 128).T
    cf[:, CF_MUR:CF_MUR + 8] = np.asarray(
        ins["L0_ffn_mu_r"], dtype=np.float32).reshape(8, 128).T
    cf[:, CF_MUK:CF_MUK + 8] = np.asarray(
        ins["L0_ffn_mu_k"], dtype=np.float32).reshape(8, 128).T
    cf[:, CF_EPS] = EPS
    cf[:, CF_ONE] = 1.0

    cb = np.zeros((128, CB_W), dtype=np.float32)
    cb[:, CB_I128:CB_I128 + 128] = np.eye(128)
    p = np.arange(128)
    cb[:, CB_MASK:CB_MASK + 128] = (p[:, None] <= p[None, :])
    cb[0:65, CB_I65:CB_I65 + 65] = np.eye(65)
    cb[:, CB_ONES:CB_ONES + 128] = 1.0
    cb[0:64, CB_BD:CB_BD + 64] = 1.0
    cb[64:128, CB_BD + 64:CB_BD + 128] = 1.0

    bvf = np.ascontiguousarray(
        np.asarray(ins["L0_attn_bv"], dtype=np.float32).reshape(1, E))

    common = {"wq": wqT, "wk": wkT, "wv": wvT, "wr": wrT, "wk2": wk2T,
              "wv2": wv2T, "cf": cf, "cb": cb, "bvf": bvf}
    x = np.asarray(ins["L0_attn_x"])
    in_maps = []
    for b in range(8):
        m = dict(common)
        m["x"] = np.ascontiguousarray(x[b])
        in_maps.append(m)
    return in_maps


def kernel(**inputs):
    nc, _ = build_program()
    in_maps = _pack_inputs(inputs)
    res = run_bass_kernel_spmd(nc, in_maps, core_ids=list(range(8)))
    out = np.stack([np.asarray(res.results[b]["out"]) for b in range(8)],
                   axis=0)
    return out


# version 36
# speedup vs baseline: 1.2114x; 1.0058x over previous
import numpy as np
import concourse.bass as bass
import concourse.mybir as mybir
from concourse.bass_utils import run_bass_kernel_spmd

F32 = mybir.dt.float32
BF16 = mybir.dt.bfloat16
AF = mybir.ActivationFunctionType
ALU = mybir.AluOpType
AX = mybir.AxisListType.X

S = 2048
E = 1024
H = 16
HD = 64
HALF = 1024
NT = 8
NCH = 8
EPS = 1e-5

CF_BQ = 0
CF_BK = 8
CF_MUR = 16
CF_MUK = 24
CF_EPS = 32
CF_ONE = 33
CF_W = 40

CB_I128 = 0
CB_MASK = 128
CB_I65 = 256
CB_ONES = 321
CB_BD = 452
CB_W = 580


def build_program(dbg=(), dbg_barrier=True):
    nc = bass.Bass("TRN2", target_bir_lowering=False)
    dbg = set(dbg)

    x_d = nc.dram_tensor("x", [S, E], F32, kind="ExternalInput").ap()
    out_d = nc.dram_tensor("out", [S, E], F32, kind="ExternalOutput").ap()
    w_d = {}
    for wn in ("wq", "wk", "wv", "wr", "wk2", "wv2"):
        w_d[wn] = nc.dram_tensor(wn, [E, E], F32, kind="ExternalInput").ap()
    cf_d = nc.dram_tensor("cf", [128, CF_W], F32, kind="ExternalInput").ap()
    cb_d = nc.dram_tensor("cb", [128, CB_W], F32, kind="ExternalInput").ap()
    bv_d = nc.dram_tensor("bvf", [1, E], F32, kind="ExternalInput").ap()

    dbg_d = {}

    def dbg_on(name, shape):
        if name in dbg:
            dbg_d[name] = nc.dram_tensor("dbg_" + name, shape, F32,
                                         kind="ExternalOutput").ap()
            return True
        return False

    import contextlib
    with contextlib.ExitStack() as _st:
        X = _st.enter_context(nc.sbuf_tensor("X", [128, NT, E], F32))
        KT = _st.enter_context(nc.sbuf_tensor("KT", [128, NCH, S], BF16))
        QT = _st.enter_context(nc.sbuf_tensor("QT", [128, NCH, HALF], BF16))
        VH = _st.enter_context(nc.sbuf_tensor("VH", [128, 16, H * 65], BF16))
        XN = _st.enter_context(nc.sbuf_tensor("XN", [128, NCH, 1040], BF16))
        AO = _st.enter_context(nc.sbuf_tensor("AO", [128, NT, E], BF16))
        W0 = _st.enter_context(nc.sbuf_tensor("W0", [128, NCH, E], BF16))
        W1 = _st.enter_context(nc.sbuf_tensor("W1", [128, NCH, E], BF16))
        W2 = _st.enter_context(nc.sbuf_tensor("W2", [128, NCH, E], BF16))
        PT = _st.enter_context(nc.sbuf_tensor("PT", [128, 3, 2, 512], BF16))
        OSB = _st.enter_context(nc.sbuf_tensor("OSB", [128, 2, 512], BF16))
        CF = _st.enter_context(nc.sbuf_tensor("CF", [128, CF_W], F32))
        CB = _st.enter_context(nc.sbuf_tensor("CB", [128, CB_W], BF16))
        STT = _st.enter_context(nc.sbuf_tensor("STT", [128, 128], F32))
        MH = _st.enter_context(nc.sbuf_tensor("MH", [128, 16], F32))
        MHP = _st.enter_context(nc.sbuf_tensor("MHP", [128, 8], F32))
        LNS = _st.enter_context(nc.sbuf_tensor("LNS", [128, 1024], BF16))
        BVB = _st.enter_context(nc.sbuf_tensor("BVB", [128, E], BF16))
        SMALL = _st.enter_context(nc.sbuf_tensor("SMALL", [128, 96], F32))
        NMB = _st.enter_context(nc.sbuf_tensor("NMB", [128, 16], BF16))
        CAR = _st.enter_context(nc.sbuf_tensor("CAR", [128, NCH, 1], BF16))
        RC = _st.enter_context(nc.sbuf_tensor("RC", [128, 8], F32))
        PSA = _st.enter_context(nc.psum_tensor("PSA", [128, 4, 512], F32))
        PSO = _st.enter_context(nc.psum_tensor("PSO", [128, 2, 512], F32))
        PST = _st.enter_context(nc.psum_tensor("PST", [128, 512], F32))
        PSS = _st.enter_context(nc.psum_tensor("PSS", [128, 512], F32))
        sLD = _st.enter_context(nc.semaphore("LD"))
        sWL = _st.enter_context(nc.semaphore("WL"))
        sPE = _st.enter_context(nc.semaphore("PE"))
        sACT = _st.enter_context(nc.semaphore("ACT"))
        sDVE = _st.enter_context(nc.semaphore("DVE"))
        sGP = _st.enter_context(nc.semaphore("GP"))
        sST = _st.enter_context(nc.semaphore("ST"))
        sXS = _st.enter_context(nc.semaphore("XS"))
        block = _st.enter_context(nc.Block())
        SEM = {"LD": sLD, "WL": sWL, "PE": sPE, "ACT": sACT, "DVE": sDVE,
               "GP": sGP, "ST": sST, "XS": sXS}
        cnt = {k: 0 for k in SEM}
        lastw = {}
        items = []
        cur_phase = ["init"]
        plan_labels = {}

        def em(eng, f):
            items.append((eng, f))

        def emlab(eng):
            plan_labels.setdefault(eng, []).append(cur_phase[0])

        def W(eng, sem, val):
            if val <= 0:
                return
            assert val <= cnt[sem], f"wait on future {sem} {val} > {cnt[sem]}"
            key = (eng, sem)
            if lastw.get(key, 0) >= val:
                return
            lastw[key] = val
            em(eng, lambda e, s=sem, v=val: e.wait_ge(SEM[s], v))

        def EMI(eng, f, inc=None, by=1):
            emlab(eng)
            if inc is not None:
                cnt[inc] += by
                em(eng, lambda e, ff=f, s=inc, b=by: ff(e).then_inc(SEM[s], b))
                return cnt[inc]
            em(eng, f)
            return None

        def dbg_bar():
            if not dbg_barrier:
                return
            for eng in ("vector", "scalar", "tensor", "sync"):
                W(eng, "ST", cnt["ST"])

        EMI("sync", lambda e: e.dma_start(out=CF[:], in_=cf_d), "LD", 16)
        ld_const = cnt["LD"]
        W("sync", "LD", cnt["LD"])
        EMI("sync", lambda e: e.dma_start(
            out=X[:, 0:4, :],
            in_=x_d[0:512, :].rearrange("(t p) f -> p t f", p=128)),
            "LD", 16)
        ld0a = cnt["LD"]
        W("sync", "LD", cnt["LD"])
        EMI("sync", lambda e: e.dma_start(
            out=X[:, 4:8, :],
            in_=x_d[512:HALF, :].rearrange("(t p) f -> p t f", p=128)),
            "LD", 16)
        ld_x = [[ld0a, cnt["LD"]], [None, None]]

        EMI("gpsimd", lambda e: e.dma_start(out=CB[:], in_=cb_d), "WL", 16)
        W("gpsimd", "WL", cnt["WL"])
        EMI("gpsimd", lambda e: e.dma_start(out=QT[0:1, 0, :], in_=bv_d),
            "WL", 16)
        wl_const = cnt["WL"]

        vh_ones = VH[:].rearrange("p g (h c) -> p g h c", c=65)
        EMI("gpsimd", lambda e: e.memset(vh_ones[:, :, :, 64:65], 1.0), "GP")
        EMI("gpsimd", lambda e: e.memset(CAR[:], 0.0), "GP")
        gp_init = cnt["GP"]

        wslot = {"wq": W0, "wk": W1, "wv": W2, "wr": W0, "wk2": W1,
                 "wv2": W2}
        wl_ready = {}

        def load_weight(wn, tag, wait_pe=0):
            if wait_pe:
                W("gpsimd", "PE", wait_pe)
            W("gpsimd", "WL", cnt["WL"])
            EMI("gpsimd", lambda e, w=wn: e.dma_start(
                out=wslot[w][:],
                in_=w_d[w].rearrange("(k p) m -> p k m", p=128)), "WL", 16)
            wl_ready[tag] = cnt["WL"]

        load_weight("wq", "wq0")
        load_weight("wk", "wk0")
        load_weight("wv", "wv0")

        W("tensor", "WL", wl_const)
        for n in range(2):
            pe = EMI("tensor", lambda e, n=n: e.matmul(
                PSS[:, 0:512], CB[0:1, CB_ONES:CB_ONES + 128],
                QT[0:1, 0, n * 512:(n + 1) * 512], start=True, stop=True),
                "PE")
            W("vector", "PE", pe)
            dv = EMI("vector", lambda e, n=n: e.tensor_copy(
                BVB[:, n * 512:(n + 1) * 512], PSS[:, 0:512]), "DVE")
            W("tensor", "DVE", dv)

        psa_evac = [(None, 0)] * 4
        psa_rr = [0]

        def psum_acc_group(n_k, lhsT_fn, rhs_fn, evac_fn, evac_eng,
                           multi=False, pre_k=None):
            b = psa_rr[0] % 4
            psa_rr[0] += 1
            gsem, gval = psa_evac[b]
            if gsem:
                W("tensor", gsem, gval)
            pe = None
            for k in range(n_k):
                if pre_k is not None:
                    pre_k(k)
                pe = EMI("tensor", lambda e, k=k, b=b: e.matmul(
                    PSA[:, b, :], lhsT_fn(k), rhs_fn(k),
                    start=(k == 0), stop=(k == n_k - 1)),
                    "PE" if k == n_k - 1 else None)
            W(evac_eng, "PE", pe)
            sem = "ACT" if evac_eng == "scalar" else "DVE"
            if multi:
                ev = evac_fn(PSA[:, b, :])
            else:
                ev = EMI(evac_eng, lambda e, b=b: evac_fn(e, PSA[:, b, :]),
                         sem)
            psa_evac[b] = (sem, ev)
            return ev

        tb_guard = {0: (None, 0), 1: (None, 0)}
        tb_banks = [PST, PSS]
        tb_rr = [0]

        def transpose_group(blocks, dest_fn, extra_pe_wait=()):
            bi = tb_rr[0] % 2
            tb_rr[0] += 1
            bank = tb_banks[bi]
            gsem, gval = tb_guard[bi]
            if gsem:
                W("tensor", gsem, gval)
            for sem, val in extra_pe_wait:
                W("tensor", sem, val)
            pe = None
            for i, blk in enumerate(blocks):
                pe = EMI("tensor", lambda e, i=i, blk=blk, bank=bank:
                         e.matmul(bank[:, i * 128:(i + 1) * 128], blk,
                                  CB[:, CB_I128:CB_I128 + 128],
                                  start=True, stop=True),
                         "PE" if i == len(blocks) - 1 else None)
            W("vector", "PE", pe)
            pv = bank[:].rearrange("p (i n) -> p i n", n=128)
            dv = EMI("vector", lambda e, pv=pv, m=len(blocks):
                     dest_fn(e, pv[:, 0:m, :]), "DVE")
            tb_guard[bi] = ("DVE", dv)
            return dv

        stat_ring_i = [0]

        def grab_stat_bank():
            k = stat_ring_i[0] % 6
            stat_ring_i[0] += 1
            if k < 4:
                gsem, gval = psa_evac[k]
                if gsem:
                    W("tensor", gsem, gval)
                bank = PSA[:, k, :]

                def done(sem, val, k=k):
                    psa_evac[k] = (sem, val)
            else:
                bi = k - 4
                gsem, gval = tb_guard[bi]
                if gsem:
                    W("tensor", gsem, gval)
                bank = tb_banks[bi]

                def done(sem, val, bi=bi):
                    tb_guard[bi] = (sem, val)
            return bank, done

        def grab_tb():
            bi = tb_rr[0] % 2
            tb_rr[0] += 1
            bank = tb_banks[bi]
            gsem, gval = tb_guard[bi]
            if gsem:
                W("tensor", gsem, gval)

            def done(sem, val, bi=bi):
                tb_guard[bi] = (sem, val)
            return bank, done

        osb_flat = OSB[:].rearrange("p a n -> p (a n)")
        sq_dummy = PT[:].rearrange("p a b n -> p (a b n)")[:, 0:E]
        ptflat = PT[:].rearrange("p a b n -> p (a b n)")

        def mcol(t):
            return STT[:, 96 + 2 * t:96 + 2 * t + 1]

        def rcol(t):
            return STT[:, 112 + t:112 + t + 1]

        def ln_stats(src_tiles, pre=(), base=0, pre_tile=None):
            nt = len(src_tiles)
            for sem, val in pre:
                W("vector", sem, val)
                W("scalar", sem, val)
            dst = None
            for i, ap in enumerate(src_tiles):
                if pre_tile is not None:
                    pre_tile(i)
                t = base + i
                for hlf in range(2):
                    dst = EMI("vector", lambda e, t=t, ap=ap, hlf=hlf:
                              e.bn_stats(
                                  STT[:, t * 12 + hlf * 6:t * 12 + hlf * 6
                                      + 6],
                                  ap[:, hlf * 512:(hlf + 1) * 512]),
                              "DVE" if (i == nt - 1 and hlf == 1) else None)
            W("vector", "DVE", dst)
            dag = None
            for i in range(nt):
                t = base + i
                dag = EMI("vector", lambda e, t=t: e.bn_aggr(
                    STT[:, 96 + 2 * t:96 + 2 * t + 2],
                    STT[:, t * 12:t * 12 + 12]),
                    "DVE" if i == nt - 1 else None)
            W("scalar", "DVE", dag)
            var_view = STT[:, 96 + 2 * base:96 + 2 * (base + nt)].rearrange(
                "p (t two) -> p t two", two=2)[:, :, 1:2]
            a1 = EMI("scalar", lambda e, vv=var_view, nt=nt, base=base:
                     e.activation(
                         STT[:, 120 + base:120 + base + nt], vv, AF.Ln,
                         bias=CF[:, CF_EPS:CF_EPS + 1]), "ACT")
            W("scalar", "ACT", a1)
            ac = EMI("scalar", lambda e, nt=nt, base=base: e.activation(
                STT[:, 112 + base:112 + base + nt],
                STT[:, 120 + base:120 + base + nt], AF.Exp,
                scale=-0.5), "ACT")
            return ac

        out_sts = []
        preA = {}
        preA_xn = {}
        for h in range(2):
            T0 = h * HALF
            W("vector", "LD", ld_x[h][0])
            W("scalar", "LD", ld_x[h][0])
            W("vector", "LD", ld_const)
            W("scalar", "LD", ld_const)

            if dbg_on("xload%d" % h, [128, NT, E]):
                W("gpsimd", "LD", ld_x[h][1])
                EMI("gpsimd", lambda e, h=h: e.dma_start(
                    out=dbg_d["xload%d" % h], in_=X[:]), "ST", 16)
                dbg_bar()
            cur_phase[0] = "A_ln1_h%d" % h
            rstd_b = {}
            xn_r = {}
            osb_free_pe = [0]

            def ln1_batch(batch):
                for t in range(batch * 4, batch * 4 + 4):
                    W("vector", "ACT", rstd_b[batch])
                    W("vector", "PE", osb_free_pe[0])
                    W("vector", "DVE", cnt["DVE"])
                    W("vector", "XS", cnt["XS"])
                    dvn = EMI("vector", lambda e, t=t: e.tensor_scalar(
                        osb_flat, X[:, t, :], mcol(t), rcol(t),
                        ALU.subtract, ALU.mult), "DVE")
                    for g in range(2):
                        c0 = g * 4
                        transpose_group(
                            [osb_flat[:, (c0 + c) * 128:(c0 + c + 1) * 128]
                             for c in range(4)],
                            lambda e, pv, t=t, c0=c0: e.tensor_copy(
                                XN[:, c0:c0 + 4, t * 128:(t + 1) * 128], pv),
                            extra_pe_wait=(("DVE", dvn), ("WL", wl_const)))
                    osb_free_pe[0] = cnt["PE"]
                xn_r[batch] = cnt["DVE"]

            if h in preA_xn:
                xn_r[0] = preA_xn[h]
            else:
                rstd_b[0] = ln_stats([X[:, t, :] for t in range(4)], base=0)
                ln1_batch(0)

            if dbg_on("xn%d" % h, [128, NCH, HALF]):
                W("gpsimd", "DVE", xn_ready)
                EMI("gpsimd", lambda e, h=h: e.dma_start(
                    out=dbg_d["xn%d" % h], in_=XN[:, :, 0:HALF]), "ST", 16)
                dbg_bar()

            sq_guard = [0]

            def plan_chunk_sq(src, evac_ac):
                W("gpsimd", "ACT", evac_ac)
                W("gpsimd", "PE", sq_guard[0])
                return EMI("gpsimd", lambda e, src=src: e.tensor_tensor(
                    ptflat[:, 0:HALF], src, src, ALU.mult), "GP")

            def plan_chunk_stats(sqv, out_col0, pi, kmax_combine):
                col = out_col0 + pi
                for n in range(2):
                    bank, done = grab_tb()
                    W("tensor", "GP", sqv)
                    pe = EMI("tensor", lambda e, n=n, bank=bank: e.matmul(
                        bank[:, 0:512],
                        CB[:, CB_BD:CB_BD + 128],
                        ptflat[:, n * 512:(n + 1) * 512],
                        start=True, stop=True), "PE")
                    sq_guard[0] = pe
                    W("vector", "PE", pe)
                    dvl = EMI("vector", lambda e, bank=bank, n=n:
                              e.reduce_max(
                                  SMALL[:, 48 + n:49 + n],
                                  bank[:, 0:512], axis=AX), "DVE")
                    done("DVE", dvl)
                W("vector", "DVE", dvl)
                if kmax_combine:
                    dvl = EMI("vector", lambda e: e.reduce_max(
                        SMALL[:, 52:53], SMALL[:, 48:50], axis=AX), "DVE")
                    W("vector", "DVE", dvl)
                    dvl = EMI("vector", lambda e, col=col: e.tensor_tensor(
                        SMALL[:, 8 + col:8 + col + 1],
                        SMALL[:, col:col + 1],
                        SMALL[:, 52:53], ALU.max), "DVE")
                    W("vector", "DVE", dvl)
                else:
                    dvl = EMI("vector", lambda e, col=col: e.reduce_max(
                        SMALL[:, col:col + 1],
                        SMALL[:, 48:50], axis=AX), "DVE")
                    W("vector", "DVE", dvl)

            q_evac = {}
            k_evac = {}
            stats_pending = []

            def qkv_n(n, wt, evac_fn_mk, tag_evacs):
                W("tensor", "DVE", xn_r[n])
                for m in range(NCH):
                    ev = psum_acc_group(
                        NCH,
                        lambda k, m=m, wt=wt: wt[:, k,
                                                 m * 128:(m + 1) * 128],
                        lambda k, n=n: XN[:, k, n * 512:(n + 1) * 512],
                        evac_fn_mk(m, n), "scalar")
                    if n == 1:
                        tag_evacs[m] = ev
                    if n == 1 and stats_pending:
                        stats_pending.pop(0)()

            def q_evac_mk(m, n):
                return lambda e, ps, m=m, n=n: e.activation(
                    QT[:, m, n * 512:(n + 1) * 512], ps, AF.Identity,
                    bias=CF[:, CF_BQ + m:CF_BQ + m + 1])

            def k_evac_mk(m, n):
                return lambda e, ps, m=m, n=n, T0=T0: e.activation(
                    KT[:, m, T0 + n * 512:T0 + (n + 1) * 512], ps,
                    AF.Identity, bias=CF[:, CF_BK + m:CF_BK + m + 1])

            def v_tile(t):
                g = h * NT + t
                dst = VH[:, g, :].rearrange("p (hh c) -> p hh c", c=65)
                W("tensor", "DVE", xn_r[t // 4])
                for n in range(2):
                    psum_acc_group(
                        NCH,
                        lambda k, t=t: XN[:, k, t * 128:(t + 1) * 128],
                        lambda k, n=n: W2[:, k, n * 512:(n + 1) * 512],
                        lambda e, ps, dst=dst, n=n: e.activation(
                            dst[:, n * 8:(n + 1) * 8, 0:64],
                            ps.rearrange("p (hh c) -> p hh c", c=64),
                            AF.Identity),
                        "scalar")
                if stats_pending:
                    stats_pending.pop(0)()

            cur_phase[0] = "A2_pass1_h%d" % h
            W("tensor", "WL", wl_ready["wq0" if h == 0 else "wq1"])
            qkv_n(0, W0, q_evac_mk, q_evac)
            W("tensor", "WL", wl_ready["wk0" if h == 0 else "wk1"])
            qkv_n(0, W1, k_evac_mk, k_evac)
            W("tensor", "WL", wl_ready["wv0" if h == 0 else "wv1"])
            W("tensor", "GP", gp_init)
            for t in range(4):
                v_tile(t)
            cur_phase[0] = "A_ln1b1_h%d" % h
            rstd_b[1] = ln_stats([X[:, t, :] for t in range(4, 8)],
                                 base=4, pre=(("LD", ld_x[h][1]),))
            ln1_batch(1)
            cur_phase[0] = "A2_pass2_h%d" % h
            qkv_n(1, W0, q_evac_mk, q_evac)
            q_done = cnt["PE"]
            load_weight("wr", "wr%d" % h, wait_pe=q_done)

            def two_stage(src_fn, col0, pi, comb):
                holder = {}

                def stage_sq(pi=pi):
                    holder["sqv"] = plan_chunk_sq(src_fn(pi),
                                                  holder["evac"])

                def stage_mm(pi=pi):
                    plan_chunk_stats(holder["sqv"], col0, pi, comb)
                return holder, stage_sq, stage_mm

            q_stage = []
            for pi in range(NCH):
                hld, s1f, s2f = two_stage(
                    lambda pi: QT[:, pi, :], 16, pi, False)
                hld["evac"] = q_evac[pi]
                q_stage.append((s1f, s2f))
            for s1f, s2f in q_stage:
                stats_pending.append(s1f)
                stats_pending.append(s2f)
            qkv_n(1, W1, k_evac_mk, k_evac)
            k_done = cnt["PE"]
            load_weight("wk2", "wk2%d" % h, wait_pe=k_done)
            k_stage = []
            for pi in range(NCH):
                hld, s1f, s2f = two_stage(
                    lambda pi, T0=T0: KT[:, pi, T0:T0 + HALF], 0, pi,
                    h == 1)
                hld["evac"] = k_evac[pi]
                k_stage.append((s1f, s2f))
            for s1f, s2f in k_stage:
                stats_pending.append(s1f)
                stats_pending.append(s2f)
            for t in range(4, NT):
                v_tile(t)
            while stats_pending:
                stats_pending.pop(0)()
            v_done = cnt["PE"]
            load_weight("wv2", "wv2%d" % h, wait_pe=v_done)

            cur_phase[0] = "NM_h%d" % h
            kb = 0 if h == 0 else 8
            dvq = EMI("vector", lambda e, kb=kb: e.tensor_tensor(
                SMALL[:, 32:40], SMALL[:, 16:24],
                SMALL[:, kb:kb + 8], ALU.mult), "DVE")
            W("scalar", "DVE", dvq)
            a1 = EMI("scalar", lambda e: e.activation(
                MH[:, 0:8], SMALL[:, 32:40], AF.Ln,
                bias=CF[:, CF_EPS:CF_EPS + 1]), "ACT")
            W("scalar", "ACT", a1)
            ac = EMI("scalar", lambda e: e.activation(
                MH[:, 8:16], MH[:, 0:8], AF.Exp, scale=0.5), "ACT")
            W("vector", "ACT", ac)
            dv = EMI("vector", lambda e: e.tensor_scalar(
                NMB[:, 0:8], MH[:, 8:16], -0.125, None, ALU.mult),
                "DVE")
            bank, done = grab_tb()
            W("tensor", "DVE", dv)
            EMI("tensor", lambda e, bank=bank: e.matmul(
                bank[:, 0:8], CB[0:1, CB_ONES:CB_ONES + 128],
                NMB[0:1, 0:8], start=True, stop=True), "PE")
            pe = EMI("tensor", lambda e, bank=bank: e.matmul(
                bank[:, 8:16], CB[64:65, CB_ONES:CB_ONES + 128],
                NMB[64:65, 0:8], start=True, stop=True,
                tile_position=(64, 0)), "PE")
            W("vector", "PE", pe)
            dcp = EMI("vector", lambda e, bank=bank: e.tensor_copy(
                MH[:, 0:8], bank[:, 0:8]), "DVE")
            W("vector", "DVE", dcp)
            mh_ready = EMI("vector", lambda e, bank=bank: e.tensor_tensor(
                MHP[:, 0:8], MH[:, 0:8], bank[:, 8:16], ALU.min), "DVE")
            done("DVE", mh_ready)

            if dbg_on("mh%d" % h, [128, 16]):
                W("gpsimd", "DVE", mh_ready)
                EMI("gpsimd", lambda e, h=h: e.dma_start(
                    out=dbg_d["mh%d" % h], in_=MH[:]), "ST", 16)
                dbg_bar()

            W("tensor", "ACT", cnt["ACT"])
            W("tensor", "DVE", cnt["DVE"])
            exp_of = {}
            pv_of = {}
            pt_guard = {}
            last_o_ac = [0]
            lns_flat = LNS[:]

            def jmax_of(sg):
                return (T0 + sg * 512 + 511) // 128

            def jparams(sg, j):
                seg_lo = T0 + sg * 512
                seg_hi = seg_lo + 512
                qstart = seg_lo if j * 128 < seg_lo else j * 128
                N = min(seg_hi - qstart, 512)
                diag = j * 128 >= seg_lo
                return qstart - T0, qstart - seg_lo, N, diag

            def emit_scores(sg, pi, j):
                cur_phase[0] = "attn_h%d_s%d_p%d" % (h, sg, pi)
                par = j % 2
                qoff, soff, N, diag = jparams(sg, j)
                if (par, 1) in exp_of:
                    W("tensor", "ACT", exp_of[(par, 1)])
                sc = None
                for head in range(2):
                    b = 2 * par + head
                    base = head * 64
                    sc = EMI("tensor", lambda e, b=b, base=base,
                             pi=pi, j=j, qoff=qoff, N=N: e.matmul(
                                 PSA[:, b, 0:N],
                                 KT[base:base + 64, pi,
                                    j * 128:(j + 1) * 128],
                                 QT[base:base + 64, pi,
                                    qoff:qoff + N],
                                 start=True, stop=True,
                                 tile_position=(base, 0)),
                             "PE" if head == 1 else None)
                return sc

            def emit_exp(sg, pi, j, sc_pe):
                par = j % 2
                slot = 2 if j == 0 else par
                qoff, soff, N, diag = jparams(sg, j)
                if slot in pt_guard:
                    W("scalar", "PE", pt_guard[slot])
                W("scalar", "PE", sc_pe)
                W("scalar", "DVE", mh_ready)
                EMI("scalar", lambda e, slot=slot, pi=pi, par=par, N=N:
                    e.activation(
                        PT[:, slot, :, 0:N],
                        PSA[:, 2 * par:2 * par + 2, 0:N],
                        AF.Exp, bias=MHP[:, pi:pi + 1],
                        scale=0.125), "ACT")
                exp_of[(par, 1)] = cnt["ACT"]
                ev = cnt["ACT"]
                pv_wait = [("ACT", ev)]
                if diag:
                    W("vector", "ACT", ev)
                    dvm = None
                    for head in range(2):
                        dvm = EMI("vector", lambda e, slot=slot,
                                  head=head: e.tensor_tensor(
                                      PT[:, slot, head, 0:128],
                                      PT[:, slot, head, 0:128],
                                      CB[:, CB_MASK:CB_MASK + 128],
                                      ALU.mult),
                                  "DVE" if head == 1 else None)
                    pv_wait.append(("DVE", dvm))
                return pv_wait

            def emit_pv(sg, pi, j, pv_wait):
                par = j % 2
                slot = 2 if j == 0 else par
                qoff, soff, N, diag = jparams(sg, j)
                jm = jmax_of(sg)
                if j == 0:
                    W("tensor", "DVE", last_o_ac[0])
                for sem, val in pv_wait:
                    W("tensor", sem, val)
                for head in range(2):
                    hh = pi * 2 + head
                    EMI("tensor", lambda e, head=head, hh=hh, j=j,
                        slot=slot, soff=soff, N=N, jm=jm:
                        e.matmul(
                            PSO[0:65, head, soff:soff + N],
                            VH[:, j, hh * 65:hh * 65 + 65],
                            PT[:, slot, head, 0:N],
                            start=(j == 0), stop=(j == jm)),
                        "PE" if head == 1 else None)
                pv_of[par] = cnt["PE"]
                pt_guard[slot] = cnt["PE"]

            def o_evac(sg, pi):
                cur_phase[0] = "ofin_h%d_s%d_p%d" % (h, sg, pi)
                W("vector", "PE", pv_of[0])
                W("vector", "PE", pv_of[1])
                for head in range(2):
                    EMI("vector", lambda e, head=head: e.tensor_copy(
                        OSB[0:65, head, :], PSO[0:65, head, :]),
                        "DVE" if head == 1 else None)
                o_dv = EMI("vector", lambda e: e.tensor_copy(
                    RC[:, 4:5], CF[:, CF_ONE:CF_ONE + 1]), "DVE")
                last_o_ac[0] = o_dv

            def o_transp(sg, pi):
                cur_phase[0] = "otr_h%d_s%d_p%d" % (h, sg, pi)
                o_ac = last_o_ac[0]
                for head in range(2):
                    hh = pi * 2 + head
                    bank, done = grab_tb()
                    W("tensor", "DVE", o_ac)
                    pe = None
                    for qt in range(4):
                        pe = EMI("tensor", lambda e, head=head, qt=qt,
                                 bank=bank: e.matmul(
                                     bank[:, qt * 128:qt * 128 + 65],
                                     OSB[0:65, head,
                                         qt * 128:(qt + 1) * 128],
                                     CB[0:65, CB_I65:CB_I65 + 65],
                                     start=True, stop=True),
                                 "PE" if qt == 3 else None)
                    W("vector", "PE", pe)
                    W("vector", "DVE", cnt["DVE"])
                    bv = bank[:].rearrange("p (i n) -> p i n", n=128)
                    dvr = EMI("vector", lambda e, bv=bv: e.reciprocal(
                        RC[:, 0:4], bv[:, 0:4, 64:65]), "DVE")
                    W("vector", "DVE", dvr)
                    dvo = None
                    for qt in range(4):
                        tt = sg * 4 + qt
                        dvo = EMI("vector", lambda e, bv=bv, qt=qt,
                                  tt=tt, hh=hh: e.scalar_tensor_tensor(
                                      AO[:, tt, hh * 64:hh * 64 + 64],
                                      bv[:, qt, 0:64],
                                      RC[:, qt:qt + 1],
                                      BVB[:, hh * 64:hh * 64 + 64],
                                      ALU.mult, ALU.add),
                                  "DVE" if qt == 3 else None)
                    done("DVE", dvo)

            def plan_C(qq, pre_dve):
                cur_phase[0] = "C_h%d_q%d" % (h, qq)
                tiles = [qq * 4 + i for i in range(4)]
                rstd_a = ln_stats([AO[:, t, :] for t in tiles],
                                  pre=(("DVE", pre_dve),))
                yield
                use_gp = (qq == 0)
                gp_adds = []
                dvx = None
                for i, t in enumerate(tiles):
                    W("vector", "ACT", rstd_a)
                    W("vector", "DVE", cnt["DVE"])
                    if gp_adds:
                        W("vector", "GP", gp_adds[-1])
                    dts = EMI("vector", lambda e, t=t, i=i: e.tensor_scalar(
                        lns_flat, AO[:, t, :], mcol(i), rcol(i),
                        ALU.subtract, ALU.mult), "DVE")
                    if use_gp:
                        W("gpsimd", "DVE", dts)
                        gp_adds.append(EMI("gpsimd", lambda e, t=t:
                                           e.tensor_tensor(
                                               X[:, t, :], X[:, t, :],
                                               lns_flat, ALU.add), "GP"))
                    else:
                        W("vector", "DVE", dts)
                        dvx = EMI("vector", lambda e, t=t: e.tensor_tensor(
                            X[:, t, :], X[:, t, :], lns_flat, ALU.add),
                            "DVE")
                    if i == 1:
                        yield
                yield
                if use_gp:
                    rstd_2 = ln_stats([X[:, t, :] for t in tiles],
                                      pre_tile=lambda i: W("vector", "GP",
                                                           gp_adds[i]))
                else:
                    rstd_2 = ln_stats([X[:, t, :] for t in tiles],
                                      pre=(("DVE", dvx),))
                if qq == 0:
                    W("vector", "GP", gp_init)
                    EMI("vector", lambda e: e.tensor_copy(
                        XN[:, :, 0:1], CAR[:]))
                else:
                    W("vector", "DVE", cnt["DVE"])
                    EMI("vector", lambda e: e.tensor_copy(
                        XN[:, :, 520:521], XN[:, :, 512:513]))
                yield
                osb_free = 0
                for i, t in enumerate(tiles):
                    W("vector", "ACT", rstd_2)
                    W("vector", "PE", osb_free)
                    dvn = EMI("vector", lambda e, t=t, i=i: e.tensor_scalar(
                        lns_flat, X[:, t, :], mcol(i), rcol(i),
                        ALU.subtract, ALU.mult), "DVE")
                    for g2 in range(2):
                        c0 = g2 * 4
                        transpose_group(
                            [lns_flat[:, (c0 + c) * 128:(c0 + c + 1) * 128]
                             for c in range(4)],
                            lambda e, pv, i=i, c0=c0, qq=qq: e.tensor_copy(
                                XN[:, c0:c0 + 4,
                                   qq * 520 + 1 + i * 128:
                                   qq * 520 + 1 + (i + 1) * 128], pv),
                            extra_pe_wait=(("DVE", dvn),))
                    osb_free = cnt["PE"]
                    yield
                if qq == 1:
                    W("vector", "DVE", cnt["DVE"])
                    EMI("vector", lambda e: e.tensor_copy(
                        CAR[:], XN[:, :, 1032:1033]))

            aof = AO[:].rearrange("p t f -> p (t f)")
            qtf = QT[:].rearrange("p c n -> p (c n)")
            Avs = [qtf[:, c * 512:(c + 1) * 512] for c in range(NCH)]
            Bvs = [qtf[:, 4096 + c * 512:4096 + (c + 1) * 512]
                   for c in range(NCH)]
            Cvs = [aof[:, c * 512:(c + 1) * 512] for c in range(NCH)]
            XKs = [aof[:, 4096 + c * 512:4096 + (c + 1) * 512]
                   for c in range(NCH)]
            OTs = [aof[:, 4096 + tl * 1024:4096 + (tl + 1) * 1024]
                   for tl in range(4)]

            def plan_mix_d(qq, pe_wait, dve_wait, on_gp):
                cur_phase[0] = "mixd_h%d_q%d" % (h, qq)
                x2 = [XN[:, c, qq * 520:qq * 520 + 513] for c in range(NCH)]
                eng, sem = ("gpsimd", "GP") if on_gp else ("vector", "DVE")
                if pe_wait:
                    W(eng, "PE", pe_wait)
                if on_gp and dve_wait:
                    W("gpsimd", "DVE", dve_wait)
                vals = []
                for c in range(NCH):
                    vals.append(EMI(eng, lambda e, c=c, xc=x2[c]:
                                    e.tensor_tensor(
                                        Cvs[c], xc[:, 1:513], xc[:, 0:512],
                                        ALU.subtract), sem))
                return vals

            def plan_mix_xr(qq, pe_wait, d_vals, d_on_gp):
                cur_phase[0] = "mixr_h%d_q%d" % (h, qq)
                x2 = [XN[:, c, qq * 520:qq * 520 + 513] for c in range(NCH)]
                if pe_wait:
                    W("vector", "PE", pe_wait)
                xr_vals = []
                for c in range(NCH):
                    if d_on_gp:
                        W("vector", "GP", d_vals[c])
                    xr_vals.append(EMI("vector", lambda e, c=c, xc=x2[c]:
                                       e.scalar_tensor_tensor(
                                           Avs[c], Cvs[c],
                                           CF[:, CF_MUR + c:CF_MUR + c + 1],
                                           xc[:, 0:512],
                                           ALU.mult, ALU.add), "DVE"))
                return xr_vals

            xr_gp = {}

            def mix0():
                cur_phase[0] = "mixd_h%d_q0" % h
                x2 = [XN[:, c, 0:513] for c in range(NCH)]
                W("vector", "PE", cnt["PE"])
                vals = []
                for c in range(NCH):
                    EMI("vector", lambda e, c=c, xc=x2[c]:
                        e.tensor_tensor(
                            Cvs[c], xc[:, 1:513], xc[:, 0:512],
                            ALU.subtract), None)
                    vals.append(EMI("vector", lambda e, c=c, xc=x2[c]:
                                    e.scalar_tensor_tensor(
                                        Avs[c], Cvs[c],
                                        CF[:, CF_MUR + c:CF_MUR + c + 1],
                                        xc[:, 0:512],
                                        ALU.mult, ALU.add), "DVE"))
                xr_gp[0] = vals

            mix_hook = [mix0]
            cgen = None
            units = [(sg, pi, j) for sg in range(2) for pi in range(NCH)
                     for j in range(jmax_of(sg) + 1)]
            nu = len(units)
            sc_of = {}
            pvw = {}

            def ensure_sc(ui):
                if ui < nu and ui not in sc_of:
                    sc_of[ui] = emit_scores(*units[ui])

            def ensure_exp(ui):
                if ui < nu and ui not in pvw:
                    ensure_sc(ui)
                    pvw[ui] = emit_exp(*units[ui], sc_of[ui])

            ensure_sc(0)
            ensure_sc(1)
            ensure_exp(0)
            pending_transp = [None]
            for ui in range(nu):
                sg, pi, j = units[ui]
                ensure_exp(ui + 1)
                ensure_sc(ui + 2)
                emit_pv(sg, pi, j, pvw[ui])
                if pending_transp[0] is not None:
                    pending_transp[0]()
                    pending_transp[0] = None
                if j == jmax_of(sg):
                    o_evac(sg, pi)
                    pending_transp[0] = (
                        lambda sg=sg, pi=pi: o_transp(sg, pi))
                    if sg == 1 and pi == 0:
                        cgen = plan_C(0, cnt["DVE"])
                    if sg == 1 and pi == NCH - 1:
                        if cgen is not None:
                            for _ in cgen:
                                pass
                            cgen = None
                        if mix_hook[0] is not None:
                            mix_hook[0]()
                            mix_hook[0] = None
                    ensure_exp(ui + 1)
                    ensure_exp(ui + 2)
                elif j == 1 and cgen is not None:
                    next(cgen, None)
            if pending_transp[0] is not None:
                pending_transp[0]()
                pending_transp[0] = None
            if cgen is not None:
                for _ in cgen:
                    pass
            attn_pe_done = cnt["PE"]
            seg1_done = cnt["DVE"]

            cgen1 = plan_C(1, seg1_done)

            if dbg_on("ao%d" % h, [128, NT, E]):
                W("gpsimd", "DVE", cnt["DVE"])
                EMI("gpsimd", lambda e, h=h: e.dma_start(
                    out=dbg_d["ao%d" % h], in_=AO[:]), "ST", 16)
                dbg_bar()

            ld_x1_tile = {}
            for qq in range(2):
                x2 = [XN[:, c, qq * 520:qq * 520 + 513] for c in range(NCH)]
                cur_phase[0] = "ffn_r_h%d_q%d" % (h, qq)

                def emit_xk(x2=x2, qq=qq):
                    cur_phase[0] = "mixk_h%d_q%d" % (h, qq)
                    xk = None
                    for c in range(NCH):
                        xk = EMI("vector", lambda e, c=c, xc=x2[c]:
                                 e.scalar_tensor_tensor(
                                     XKs[c], Cvs[c],
                                     CF[:, CF_MUK + c:CF_MUK + c + 1],
                                     xc[:, 0:512], ALU.mult, ALU.add),
                                 "DVE" if c == NCH - 1 else None)
                    return xk

                xk_dve = None
                W("tensor", "WL", wl_ready["wr%d" % h])
                for m in range(NCH):
                    def evac_r(ps, m=m):
                        s1 = PT[:, 0, m % 2, :]
                        s2 = PT[:, 1, m % 2, :]
                        a1 = EMI("scalar", lambda e, ps=ps, s1=s1:
                                 e.activation(s1, ps, AF.Exp, scale=-1.0),
                                 "ACT")
                        W("scalar", "ACT", a1)
                        a2 = EMI("scalar", lambda e, s1=s1, s2=s2:
                                 e.activation(s2, s1, AF.Ln,
                                              bias=CF[:, CF_ONE:CF_ONE + 1]),
                                 "ACT")
                        W("scalar", "ACT", a2)
                        return EMI("scalar", lambda e, s2=s2, m=m:
                                   e.activation(Bvs[m], s2, AF.Exp,
                                                scale=-1.0), "ACT")
                    r_ac = psum_acc_group(
                        NCH,
                        lambda k, m=m: W0[:, k, m * 128:(m + 1) * 128],
                        lambda k: Avs[k],
                        evac_r, "scalar", multi=True,
                        pre_k=lambda k, qq=qq: W("tensor", "DVE",
                                                 xr_gp[qq][k]))
                    if cgen1 is not None:
                        next(cgen1, None)
                        if m == 2:
                            xk_dve = emit_xk()
                r_pe = cnt["PE"]
                if cgen1 is not None:
                    for _ in cgen1:
                        pass
                    cgen1 = None
                    xn_q1_dve = cnt["DVE"]
                if xk_dve is None:
                    xk_dve = emit_xk()
                cur_phase[0] = "ffn_k_h%d_q%d" % (h, qq)
                W("tensor", "WL", wl_ready["wk2%d" % h])
                W("tensor", "DVE", xk_dve)
                preA_tile = [None]
                if h == 0 and qq == 1:
                    cur_phase[0] = "preA_h0"
                    preA[1] = ln_stats([X[:, t, :] for t in range(4)],
                                       base=0,
                                       pre_tile=lambda i: W(
                                           "vector", "LD", ld_x1_tile[i]))
                    ofp = [0]

                    def preA_tile_fn(t):
                        cur_phase[0] = "preA_h0"
                        W("vector", "ACT", preA[1])
                        W("vector", "PE", ofp[0])
                        dvn = EMI("vector", lambda e, t=t: e.tensor_scalar(
                            lns_flat, X[:, t, :], mcol(t), rcol(t),
                            ALU.subtract, ALU.mult), "DVE")
                        for g in range(2):
                            c0 = g * 4
                            transpose_group(
                                [lns_flat[:, (c0 + c) * 128:
                                          (c0 + c + 1) * 128]
                                 for c in range(4)],
                                lambda e, pv, t=t, c0=c0: e.tensor_copy(
                                    XN[:, c0:c0 + 4,
                                       t * 128:(t + 1) * 128], pv),
                                extra_pe_wait=(("DVE", dvn),))
                        ofp[0] = cnt["PE"]
                        if t == 3:
                            preA_xn[1] = cnt["DVE"]
                    preA_tile[0] = preA_tile_fn
                for m in range(NCH):
                    def evac_k(ps, m=m):
                        s1 = PT[:, 0, m % 2, :]
                        a1 = EMI("scalar", lambda e, ps=ps, s1=s1:
                                 e.activation(s1, ps, AF.Relu), "ACT")
                        W("scalar", "ACT", a1)
                        return EMI("scalar", lambda e, s1=s1, m=m:
                                   e.activation(Cvs[m], s1, AF.Square),
                                   "ACT")
                    psum_acc_group(
                        NCH,
                        lambda k, m=m: W1[:, k, m * 128:(m + 1) * 128],
                        lambda k: XKs[k],
                        evac_k, "scalar", multi=True)
                    if m % 2 == 1 and preA_tile[0] is not None:
                        preA_tile[0](m // 2)
                        cur_phase[0] = "ffn_k_h%d_q%d" % (h, qq)
                k2_pe = cnt["PE"]
                cur_phase[0] = "ffn_v_h%d_q%d" % (h, qq)
                W("tensor", "WL", wl_ready["wv2%d" % h])
                W("tensor", "ACT", cnt["ACT"])
                W("vector", "ACT", r_ac)
                for m in range(NCH):
                    psum_acc_group(
                        NCH,
                        lambda k, m=m: W2[:, k, m * 128:(m + 1) * 128],
                        lambda k: Cvs[k],
                        lambda e, ps, m=m: e.tensor_tensor(
                            Avs[m], ps, Bvs[m], ALU.mult),
                        "vector")
                v2_pe = cnt["PE"]
                outT_dve = cnt["DVE"]
                for tl in range(4):
                    for g2 in range(2):
                        c0 = g2 * 4
                        transpose_group(
                            [Avs[c0 + c][:, tl * 128:(tl + 1) * 128]
                             for c in range(4)],
                            lambda e, pv, tl=tl, c0=c0: e.tensor_copy(
                                OTs[tl].rearrange("p (c n) -> p c n",
                                                  n=128)[:, c0:c0 + 4, :],
                                pv),
                            extra_pe_wait=(("DVE", outT_dve),))
                outtok_dve = cnt["DVE"]
                if qq == 0:
                    d1_vals = plan_mix_d(1, cnt["PE"], xn_q1_dve, True)
                cur_phase[0] = "ln3_h%d_q%d" % (h, qq)
                tiles = [qq * 4 + i for i in range(4)]
                rstd_ac = ln_stats(OTs, pre=(("DVE", outtok_dve),), base=4)
                dvf = None
                for i, t in enumerate(tiles):
                    W("vector", "ACT", rstd_ac)
                    W("vector", "DVE", cnt["DVE"])
                    W("vector", "XS", cnt["XS"])
                    dts = EMI("vector", lambda e, i=i: e.tensor_scalar(
                        osb_flat, OTs[i], mcol(4 + i), rcol(4 + i),
                        ALU.subtract, ALU.mult), "DVE")
                    W("vector", "DVE", dts)
                    dvf = EMI("vector", lambda e, t=t: e.tensor_tensor(
                        X[:, t, :], X[:, t, :], osb_flat, ALU.add), "XS")
                    W("sync", "XS", dvf)
                    EMI("sync", lambda e, h=h, T0=T0, t=t: e.dma_start(
                        out=out_d[T0 + t * 128:T0 + (t + 1) * 128,
                                  :].rearrange("(a p) f -> p a f", p=128),
                        in_=X[:, t, :]), "ST", 16)
                    st_q = cnt["ST"]
                    if h == 0:
                        ldq = "gpsimd" if qq == 0 else "sync"
                        W(ldq, "ST", st_q)
                        W(ldq, "LD", cnt["LD"])
                        EMI(ldq, lambda e, t=t: e.dma_start(
                            out=X[:, t, :],
                            in_=x_d[HALF + t * 128:HALF + (t + 1) * 128,
                                    :].rearrange("(a p) f -> p a f", p=128)),
                            "LD", 16)
                        ld_x1_tile[t] = cnt["LD"]
                    if i == 0 and qq == 0:
                        xr_gp[1] = plan_mix_xr(1, cnt["PE"], d1_vals, True)
                        cur_phase[0] = "ln3_h%d_q%d" % (h, qq)
                if h == 0:
                    ld_x[1][qq] = cnt["LD"]

            half_done_dve = cnt["DVE"]
            if dbg_on("xf%d" % h, [128, NT, E]):
                W("gpsimd", "DVE", half_done_dve)
                EMI("gpsimd", lambda e, h=h: e.dma_start(
                    out=dbg_d["xf%d" % h], in_=X[:]), "ST", 16)
                dbg_bar()
            if h == 0:
                load_weight("wq", "wq1", wait_pe=r_pe)
                load_weight("wk", "wk1", wait_pe=k2_pe)
                load_weight("wv", "wv1", wait_pe=v2_pe)

        W("sync", "ST", cnt["ST"])

        def runner(name):
            def go(eng):
                for enm, f in items:
                    if enm == name:
                        f(eng)
            return go

        block.sync(runner("sync"))
        block.gpsimd(runner("gpsimd"))
        block.tensor(runner("tensor"))
        block.vector(runner("vector"))
        block.scalar(runner("scalar"))

    build_program.labels = plan_labels
    return nc, sorted(dbg_d.keys())


def _pack_inputs(ins):
    wqT = np.ascontiguousarray(
        np.asarray(ins["L0_attn_Wq"]).transpose(2, 0, 1).reshape(E, E))
    wkT = np.ascontiguousarray(
        np.asarray(ins["L0_attn_Wk"]).transpose(2, 0, 1).reshape(E, E))
    wvT = np.ascontiguousarray(
        np.asarray(ins["L0_attn_Wv"]).transpose(2, 0, 1).reshape(E, E))
    wrT = np.ascontiguousarray(np.asarray(ins["L0_ffn_W_r"]).T)
    wk2T = np.ascontiguousarray(np.asarray(ins["L0_ffn_W_k"]).T)
    wv2T = np.ascontiguousarray(np.asarray(ins["L0_ffn_W_v"]).T)

    cf = np.zeros((128, CF_W), dtype=np.float32)
    cf[:, CF_BQ:CF_BQ + 8] = np.asarray(
        ins["L0_attn_bq"], dtype=np.float32).reshape(8, 128).T
    cf[:, CF_BK:CF_BK + 8] = np.asarray(
        ins["L0_attn_bk"], dtype=np.float32).reshape(8, 128).T
    cf[:, CF_MUR:CF_MUR + 8] = np.asarray(
        ins["L0_ffn_mu_r"], dtype=np.float32).reshape(8, 128).T
    cf[:, CF_MUK:CF_MUK + 8] = np.asarray(
        ins["L0_ffn_mu_k"], dtype=np.float32).reshape(8, 128).T
    cf[:, CF_EPS] = EPS
    cf[:, CF_ONE] = 1.0

    cb = np.zeros((128, CB_W), dtype=np.float32)
    cb[:, CB_I128:CB_I128 + 128] = np.eye(128)
    p = np.arange(128)
    cb[:, CB_MASK:CB_MASK + 128] = (p[:, None] <= p[None, :])
    cb[0:65, CB_I65:CB_I65 + 65] = np.eye(65)
    cb[:, CB_ONES:CB_ONES + 128] = 1.0
    cb[0:64, CB_BD:CB_BD + 64] = 1.0
    cb[64:128, CB_BD + 64:CB_BD + 128] = 1.0

    bvf = np.ascontiguousarray(
        np.asarray(ins["L0_attn_bv"], dtype=np.float32).reshape(1, E))

    common = {"wq": wqT, "wk": wkT, "wv": wvT, "wr": wrT, "wk2": wk2T,
              "wv2": wv2T, "cf": cf, "cb": cb, "bvf": bvf}
    x = np.asarray(ins["L0_attn_x"])
    in_maps = []
    for b in range(8):
        m = dict(common)
        m["x"] = np.ascontiguousarray(x[b])
        in_maps.append(m)
    return in_maps


def kernel(**inputs):
    nc, _ = build_program()
    in_maps = _pack_inputs(inputs)
    res = run_bass_kernel_spmd(nc, in_maps, core_ids=list(range(8)))
    out = np.stack([np.asarray(res.results[b]["out"]) for b in range(8)],
                   axis=0)
    return out
